# version 11
# baseline (speedup 1.0000x reference)
import math
import os
from contextlib import ExitStack

def _env(name, dflt):
    return int(os.environ.get(name, dflt))

K_QC = _env("K_QC", 0)
K_PAIRS = _env("K_PAIRS", 0)
K_KT = _env("K_KT", 0)
K_TC = _env("K_TC", 0)
K_SKIP = os.environ.get("K_SKIP", "")
K_PRBASE = _env("K_PRBASE", 0)
K_FSLAB = _env("K_FSLAB", 0)


import numpy as np
import ml_dtypes

import concourse.bass as bass
import concourse.mybir as mybir
from concourse.bass_utils import run_bass_kernel_spmd

F32 = mybir.dt.float32
BF16 = mybir.dt.bfloat16
I32 = mybir.dt.int32
U16 = mybir.dt.uint16
ALU = mybir.AluOpType
AF = mybir.ActivationFunctionType

SET_TRIG = 9
SET_LNEXP = 6
SET_SILU = 18


def load_act_set(e, idx):
    return e.add_instruction(mybir.InstLoadActFuncSet(
        name=e.bass.get_next_instruction_name(),
        act_func_set_id=idx, ins=[], outs=[]))
AX = mybir.AxisListType

T = 4096
TO = 2048
D = 1024
NH = 16
HD = 64
NPAIR = 8
DFF = 2816
QC = 512
NQC = T // QC
NQC_OWN = TO // QC
NT = T // 128
NTO = TO // 128
NDT = D // 128
NIT = DFF // 128
EPS = 1e-5
ATT_SCALE = 1.0 / math.sqrt(HD)
QF = 1408

ENGS = ("pe", "act", "dve", "pool", "qs", "qg")


class Buf:
    __slots__ = ("ap", "w", "r", "name")

    def __init__(self, ap, name=""):
        self.ap = ap
        self.name = name
        self.w = None
        self.r = []


class Prog:
    def __init__(self, nc, stack: ExitStack):
        self.nc = nc
        self.stack = stack
        self.stream_of = {"pe": "pe", "act": "act", "dve": "dve",
                          "pool": "pg", "qg": "pg", "qs": "qs"}
        self.thunks = {st: [] for st in ("pe", "act", "dve", "pg", "qs")}
        self.nsem = {"pe": 1, "act": 1, "dve": 1, "pool": 1, "qs": 6, "qg": 6}
        self.cnt = {}
        self.rr = {e: 0 for e in ENGS}
        self.semkeys = {}
        for e in ENGS:
            self.semkeys[e] = [f"{e}{i}" for i in range(self.nsem[e])]
            for k in self.semkeys[e]:
                self.cnt[k] = 0
        self.wm = {st: {} for st in ("pe", "act", "dve", "pg", "qs")}
        self.n_ops = 0
        self.pe_geom = None
        self.pe_pending = 0

    def sbuf(self, name, shape, dtype, stack=None):
        t = (stack or self.stack).enter_context(self.nc.sbuf_tensor(name, shape, dtype))
        return Buf(t.ap(), name)

    def psum(self, name, shape, stack=None):
        t = (stack or self.stack).enter_context(self.nc.psum_tensor(name, shape, F32))
        return Buf(t.ap(), name)

    def dram(self, name, shape, dtype, kind="Internal"):
        t = self.nc.dram_tensor(name, shape, dtype, kind=kind)
        return Buf(t.ap(), name)

    def sub(self, buf, *slices):
        return Buf(buf.ap.__getitem__(slices if len(slices) > 1 else slices[0]),
                   buf.name + "/s")

    def op(self, eng, fn, reads=(), writes=(), inc=None, milestone=True, geom="f"):
        if inc is None:
            inc = 16 if eng in ("qs", "qg") else 1
        deps = []
        for b in reads:
            if b.w is not None:
                deps.append(b.w)
        for b in writes:
            if b.w is not None:
                deps.append(b.w)
            deps.extend(b.r)
        stream = self.stream_of[eng]
        need = {}
        wm = self.wm[stream]
        for (k, v) in deps:
            if eng == "pe" and k == "pe0":
                continue
            if k == "pe0" and v < self.cnt["pe0"]:
                v = self.cnt["pe0"]
            if wm.get(k, 0) >= v:
                continue
            if need.get(k, 0) < v:
                need[k] = v
        if eng == "pe" and geom != self.pe_geom:
            assert self.pe_pending == 0, "geometry switch after non-milestone PE op"
            v = self.cnt["pe0"]
            if v > 0 and wm.get("pe0", 0) < v and need.get("pe0", 0) < v:
                need["pe0"] = v
            self.pe_geom = geom
        for k, v in need.items():
            wm[k] = v
        semkey = self.semkeys[eng][self.rr[eng]]
        if eng != "pe" and self.cnt[semkey] > 0:
            v = self.cnt[semkey]
            if need.get(semkey, 0) < v:
                need[semkey] = v
                wm[semkey] = max(wm.get(semkey, 0), v)
        if eng == "pe" and not milestone:
            tok = (semkey, self.cnt[semkey] + 1)
            do_inc = 0
            self.pe_pending += 1
        else:
            self.rr[eng] = (self.rr[eng] + 1) % self.nsem[eng]
            self.cnt[semkey] += inc
            tok = (semkey, self.cnt[semkey])
            do_inc = inc
            if eng == "pe":
                self.pe_pending = 0
        needl = sorted(need.items())

        def thunk(e, sems, fn=fn, needl=needl, semkey=semkey, do_inc=do_inc):
            for k, v in needl:
                e.wait_ge(sems[k], v)
            inst = fn(e)
            if do_inc:
                inst.then_inc(sems[semkey], do_inc)

        self.thunks[stream].append(thunk)
        self.n_ops += 1
        for b in writes:
            b.w = tok
            b.r = []
        for b in reads:
            b.r.append(tok)
        return tok

    def barrier(self):
        self.pe_geom = None
        snapshot = dict(self.cnt)
        for e in ("pe", "act", "dve", "pg", "qs"):
            wm = self.wm[e]
            needl = [(k, v) for k, v in sorted(snapshot.items())
                     if v > 0 and wm.get(k, 0) < v]
            for k, v in needl:
                wm[k] = v
            if not needl:
                continue

            def thunk(e_obj, sems, needl=needl):
                for k, v in needl:
                    e_obj.wait_ge(sems[k], v)

            self.thunks[e].append(thunk)

    def finalize(self):
        nc = self.nc
        sems = {}
        for e in ENGS:
            for k in self.semkeys[e]:
                sems[k] = self.stack.enter_context(nc.semaphore(k))
        with nc.Block() as block:
            @block.tensor
            def _(t):
                for th in self.thunks["pe"]:
                    th(t, sems)

            @block.scalar
            def _(s):
                for th in self.thunks["act"]:
                    th(s, sems)

            @block.vector
            def _(v):
                for th in self.thunks["dve"]:
                    th(v, sems)

            @block.gpsimd
            def _(gg):
                for th in self.thunks["pg"]:
                    th(gg, sems)

            @block.sync
            def _(s):
                for th in self.thunks["qs"]:
                    th(s, sems)


def _quant_chunk(p, eng, x, q_out, b, off, W):
    NB = W // 16
    x3 = x.ap[:, off:off + W].rearrange("p (nb k) -> p nb k", k=16)
    p.op("dve", lambda e, o=b["bm"].ap, i=x3, NB=NB: e.tensor_reduce(
        o[:, 0:NB], i, AX.X, ALU.max, apply_absolute_value=True),
        reads=[x], writes=[b["bm"]])
    p.op(eng, lambda e, o=b["bmc"].ap, i=b["bm"].ap, NB=NB: e.tensor_scalar(
        o[:, 0:NB], i[:, 0:NB], 6e-12, None, ALU.max),
        reads=[b["bm"]], writes=[b["bmc"]])
    p.op("dve", lambda e, o=b["rb"].ap, i=b["bmc"].ap, NB=NB: e.reciprocal(o[:, 0:NB], i[:, 0:NB]),
         reads=[b["bmc"]], writes=[b["rb"]])
    p.op(eng, lambda e, o=b["rb6"].ap, i=b["rb"].ap, NB=NB: e.tensor_scalar(
        o[:, 0:NB], i[:, 0:NB], 6.0, None, ALU.mult),
        reads=[b["rb"]], writes=[b["rb6"]])
    p.op(eng, lambda e, o=b["sb"].ap, i=b["bmc"].ap, NB=NB: e.tensor_scalar(
        o[:, 0:NB], i[:, 0:NB], 1.0 / 6.0, None, ALU.mult),
        reads=[b["bmc"]], writes=[b["sb"]])
    rb6b = b["rb6"].ap[:, 0:NB].unsqueeze(2).broadcast_to([128, NB, 16])
    p.op(eng, lambda e, o=b["xn"].ap, i0=rb6b, i1=x3, W=W: e.tensor_tensor(
        o[:, 0:W].rearrange("p (nb k) -> p nb k", k=16), i0, i1, ALU.mult),
        reads=[b["rb6"], x], writes=[b["xn"]])
    p.op(eng, lambda e, o=b["c2"].ap, i=b["xn"].ap, W=W: e.tensor_scalar(
        o[:, 0:W], i[:, 0:W], 2.0, -2.0, ALU.min, ALU.max),
        reads=[b["xn"]], writes=[b["c2"]])
    p.op(eng, lambda e, o=b["c4"].ap, i=b["xn"].ap, W=W: e.tensor_scalar(
        o[:, 0:W], i[:, 0:W], 4.0, -4.0, ALU.min, ALU.max),
        reads=[b["xn"]], writes=[b["c4"]])
    p.op(eng, lambda e, o=b["bst"].ap, i=b["c2"].ap, W=W: e.tensor_scalar(
        o[:, 0:W], i[:, 0:W], 96.0, None, ALU.add),
        reads=[b["c2"]], writes=[b["bst"]])
    p.op(eng, lambda e, o=b["u1"].ap, i=b["bst"].ap, W=W: e.tensor_copy(
        o[:, 0:W], i[:, 0:W].bitcast(U16)),
        reads=[b["bst"]], writes=[b["u1"]])
    if eng == "dve":
        p.op(eng, lambda e, o=b["bst"].ap, i0=b["c2"].ap, i1=b["c4"].ap, W=W: e.scalar_tensor_tensor(
            o[:, 0:W], i0[:, 0:W], 192.0, i1[:, 0:W], ALU.add, ALU.subtract),
            reads=[b["c2"], b["c4"]], writes=[b["bst"]])
    else:
        p.op(eng, lambda e, o=b["t0"].ap, i=b["c2"].ap, W=W: e.tensor_scalar(
            o[:, 0:W], i[:, 0:W], 192.0, None, ALU.add),
            reads=[b["c2"]], writes=[b["t0"]])
        p.op(eng, lambda e, o=b["bst"].ap, i0=b["t0"].ap, i1=b["c4"].ap, W=W: e.tensor_tensor(
            o[:, 0:W], i0[:, 0:W], i1[:, 0:W], ALU.subtract),
            reads=[b["t0"], b["c4"]], writes=[b["bst"]])
    p.op(eng, lambda e, o=b["u2"].ap, i=b["bst"].ap, W=W: e.tensor_copy(
        o[:, 0:W], i[:, 0:W].bitcast(U16)),
        reads=[b["bst"]], writes=[b["u2"]])
    p.op(eng, lambda e, o=b["c2"].ap, i0=b["u1"].ap, i1=b["u2"].ap, W=W: e.tensor_tensor(
        o[:, 0:W], i0[:, 0:W].bitcast(BF16), i1[:, 0:W].bitcast(BF16), ALU.subtract),
        reads=[b["u1"], b["u2"]], writes=[b["c2"]])
    if eng == "dve":
        p.op(eng, lambda e, o=b["bst"].ap, i0=b["xn"].ap, i1=b["c4"].ap, W=W: e.scalar_tensor_tensor(
            o[:, 0:W], i0[:, 0:W], 384.0, i1[:, 0:W], ALU.add, ALU.subtract),
            reads=[b["xn"], b["c4"]], writes=[b["bst"]])
    else:
        p.op(eng, lambda e, o=b["t0"].ap, i=b["xn"].ap, W=W: e.tensor_scalar(
            o[:, 0:W], i[:, 0:W], 384.0, None, ALU.add),
            reads=[b["xn"]], writes=[b["t0"]])
        p.op(eng, lambda e, o=b["bst"].ap, i0=b["t0"].ap, i1=b["c4"].ap, W=W: e.tensor_tensor(
            o[:, 0:W], i0[:, 0:W], i1[:, 0:W], ALU.subtract),
            reads=[b["t0"], b["c4"]], writes=[b["bst"]])
    p.op(eng, lambda e, o=b["u2"].ap, i=b["bst"].ap, W=W: e.tensor_copy(
        o[:, 0:W], i[:, 0:W].bitcast(U16)),
        reads=[b["bst"]], writes=[b["u2"]])
    p.op(eng, lambda e, o=b["xn"].ap, i0=b["c2"].ap, i1=b["u2"].ap, W=W: e.tensor_tensor(
        o[:, 0:W], i0[:, 0:W], i1[:, 0:W].bitcast(BF16), ALU.add),
        reads=[b["c2"], b["u2"]], writes=[b["xn"]])
    sbb = b["sb"].ap[:, 0:NB].unsqueeze(2).broadcast_to([128, NB, 16])
    qs_ap = q_out.ap[:, off:off + W]
    if eng == "dve":
        p.op(eng, lambda e, o=qs_ap, i0=b["xn"].ap, i1=sbb, W=W, NB=NB: e.scalar_tensor_tensor(
            o.rearrange("p (nb k) -> p nb k", k=16),
            i0[:, 0:W].rearrange("p (nb k) -> p nb k", k=16), -288.0, i1, ALU.add, ALU.mult),
            reads=[b["xn"], b["sb"]], writes=[q_out])
    else:
        p.op(eng, lambda e, o=b["t0"].ap, i=b["xn"].ap, W=W: e.tensor_scalar(
            o[:, 0:W], i[:, 0:W], -288.0, None, ALU.add),
            reads=[b["xn"]], writes=[b["t0"]])
        p.op(eng, lambda e, o=qs_ap, i0=b["t0"].ap, i1=sbb, W=W, NB=NB: e.tensor_tensor(
            o.rearrange("p (nb k) -> p nb k", k=16),
            i0[:, 0:W].rearrange("p (nb k) -> p nb k", k=16), i1, ALU.mult),
            reads=[b["t0"], b["sb"]], writes=[q_out])


def emit_quant(p, x, q_out, b, F):
    sets = b["sets"] if "sets" in b else [b]
    qf = sets[0]["qf"]
    for off in range(0, F, qf):
        W = min(qf, F - off)
        _quant_chunk(p, "dve", x, q_out, sets[0], off, W)


def mk_quant_bufs(p, tag, stack, qf=QF, with_t0=True):
    NB = qf // 16
    return {
        "qf": qf,
        "bm": p.sbuf(f"{tag}_bm", [128, NB], F32, stack),
        "bmc": p.sbuf(f"{tag}_bmc", [128, NB], F32, stack),
        "rb": p.sbuf(f"{tag}_rb", [128, NB], F32, stack),
        "rb6": p.sbuf(f"{tag}_rb6", [128, NB], F32, stack),
        "sb": p.sbuf(f"{tag}_sb", [128, NB], F32, stack),
        "xn": p.sbuf(f"{tag}_xn", [128, qf], F32, stack),
        "c2": p.sbuf(f"{tag}_c2", [128, qf], F32, stack),
        "c4": p.sbuf(f"{tag}_c4", [128, qf], F32, stack),
        "bst": p.sbuf(f"{tag}_bst", [128, qf], BF16, stack),
        "u1": p.sbuf(f"{tag}_u1", [128, qf], U16, stack),
        "u2": p.sbuf(f"{tag}_u2", [128, qf], U16, stack),
        **({"t0": p.sbuf(f"{tag}_t0", [128, qf], F32, stack)} if with_t0 else {}),
    }


def emit_ln_stats(p, x, mu, rstd, scr, sumx, ssq, mu2, epsb=None):
    p.op("dve", lambda e, o=scr.ap, i=x.ap, a=ssq.ap: e.scalar_tensor_tensor(
        o[:, :], i[:, :], 1.0, i[:, :], ALU.mult, ALU.mult, accum_out=a[:, :]),
        reads=[x], writes=[scr, ssq])
    p.op("dve", lambda e, o=sumx.ap, i=x.ap: e.reduce_sum(o[:, :], i[:, :], AX.X),
         reads=[x], writes=[sumx])
    p.op("dve", lambda e, o=mu.ap, i=sumx.ap: e.tensor_scalar(o[:, :], i[:, :], 1.0 / D, None, ALU.mult),
         reads=[sumx], writes=[mu])
    p.op("dve", lambda e, o=mu2.ap, i=ssq.ap: e.tensor_scalar(o[:, :], i[:, :], 1.0 / D, None, ALU.mult),
         reads=[ssq], writes=[mu2])
    p.op("dve", lambda e, o=mu2.ap, i0=mu.ap, s=mu.ap, i1=mu2.ap: e.scalar_tensor_tensor(
        o[:, :], i0[:, :], s[:, 0:1], i1[:, :], ALU.mult, ALU.subtract),
        reads=[mu, mu2], writes=[mu2])
    p.op("act", lambda e, o=mu2.ap, i=mu2.ap, b=epsb.ap: e.activation(
        out=o[:, :], in_=i[:, :], func=AF.Ln, scale=-1.0, bias=b[:, 0:1]),
        reads=[mu2, epsb], writes=[mu2])
    p.op("act", lambda e, o=rstd.ap, i=mu2.ap: e.activation(
        out=o[:, :], in_=i[:, :], func=AF.Exp, scale=-0.5),
        reads=[mu2], writes=[rstd])


def build_program(stage="full"):
    nc = bass.Bass("TRN2", target_bir_lowering=False, debug=False)
    stack = ExitStack()
    p = Prog(nc, stack)
    g = {}

    for name, shape, dtype in (
            ("x0", [T, D], F32), ("qkvwT", [D, 3 * D], F32), ("qkvb", [128, 16], F32),
            ("vb", [128, 8], F32), ("projwT", [D, D], F32), ("projb_row", [1, D], F32),
            ("bW", [3, D, D], F32), ("bWp", [D, D], F32),
            ("wg", [DFF, D], F32), ("wu", [DFF, D], F32), ("wd", [D, DFF], F32),
            ("pos_rep", [64, T], F32), ("inv_in", [64, 1], F32),
            ("sgnpat", [64, 1], F32), ("ones_in", [128, 64], F32), ("perm_in", [128, 128], F32)):
        g[name] = p.dram(name, shape, dtype, kind="ExternalInput")
    g["out"] = p.dram("out", [TO, D], F32, kind="ExternalOutput")
    if stage != "full":
        g["dbg"] = p.dram("dbg", [T, D], F32, kind="ExternalOutput")
        p.g_dbg = g["dbg"]
    if stage == "p0":
        for nm, shp in (("dbg_freq", [64, T]), ("dbg_r", [64, T]), ("dbg_rc", [64, T]), ("dbg_inv", [64, 1])):
            g[nm] = p.dram(nm, shp, F32, kind="ExternalOutput")

    dbg_dram = set(os.environ.get("K_DBGDRAM", "").split(","))
    for name, shape, dtype in (
            ("x1d", [T, D], F32), ("x2d", [T, D], F32), ("x3d", [TO, D], F32),
            ("Cd", [128, T], F32), ("Sd", [128, T], F32),
            ("lnTd_a", [NT, 128, D], BF16), ("lnTd_b", [NT, 128, D], BF16),
            ("QTd_a", [NDT, 128, T], BF16), ("QTd_b", [NDT, 128, TO], BF16),
            ("Vd_a", [NT, 128, D], BF16), ("Vd_b", [NT, 128, D], BF16),
            ("wgTd", [128, NDT, NIT, 128], BF16), ("wuTd", [128, NDT, NIT, 128], BF16),
            ("wdTd", [128, NIT, NDT, 128], BF16),
            ("xqTd_1", [NT, 128, D], BF16), ("hqTd_1", [NT, 128, DFF], BF16),
            ("xqTd_2", [NTO, 128, D], BF16), ("hqTd_2", [NTO, 128, DFF], BF16),
            ("bWbf", [3, D, D], BF16), ("bWpbf", [D, D], BF16)):
        g[name] = p.dram(name, shape, dtype,
                         kind="ExternalOutput" if name in dbg_dram else "Internal")

    with stack:
        g["stats_sb"] = p.sbuf("stats", [128, 4 * NT * 2], F32)
        p.g_stats = g["stats_sb"]

        def stats_of(grp, i):
            base = (grp * NT + i) * 2
            return (p.sub(g["stats_sb"], slice(None), slice(base, base + 1)),
                    p.sub(g["stats_sb"], slice(None), slice(base + 1, base + 2)))
        g["stats"] = {grp: [stats_of(grp, i) for i in range(NT)] for grp in range(4)}

        g["ones64"] = p.sbuf("ones64", [128, 64], BF16)
        g["perm"] = p.sbuf("perm", [128, 128], BF16)
        g["biasrep"] = p.sbuf("biasrep", [128, D], F32)
        g["vb_sb"] = p.sbuf("vb_sb", [128, 8], F32)
        g["qkvb_sb"] = p.sbuf("qkvb_sb", [128, 16], F32)
        g["epsb"] = p.sbuf("epsb", [128, 1], F32)
        p.op("pool", lambda e, o=g["epsb"].ap: e.memset(o[:, :], EPS), writes=[g["epsb"]])
        g["c96"] = p.sbuf("c96", [128, 1], F32)
        p.op("pool", lambda e, o=g["c96"].ap: e.memset(o[:, :], 96.0), writes=[g["c96"]])

        p.g96 = g["c96"]
        build_phase0(p, g)
        p.barrier()

        if stage == "p0":
            dump_p0(p, g)
            p.finalize()
            return nc, p

        if stage == "ffnx":
            g["x1in"] = p.dram("x1in", [T, D], F32, kind="ExternalInput")
            p.op("act", lambda e: load_act_set(e, SET_LNEXP))
            with ExitStack() as ph:
                fx = p.sbuf("fx_x", [128, D], F32, ph)
                fscr = p.sbuf("fx_scr", [128, D], F32, ph)
                fsm = [p.sbuf(f"fx_sm{i}", [128, 1], F32, ph) for i in range(3)]
                for s in range(NT):
                    p.op("qg", lambda e, o=fx.ap, i=g["x1in"].ap, s=s: e.dma_start(
                        out=o[:, :], in_=i[s * 128:(s + 1) * 128, :]),
                        reads=[g["x1in"]], writes=[fx])
                    mu2, rstd2 = g["stats"][1][s]
                    emit_ln_stats(p, fx, mu2, rstd2, fscr, fsm[0], fsm[1], fsm[2], g["epsb"])
                p.barrier()
            p.op("act", lambda e: load_act_set(e, SET_SILU))
            build_ffn(p, g, g["x1in"], g["x2d"], g["stats"][1], g["xqTd_1"], g["hqTd_1"], NT)
            if not os.environ.get("K_QDBG"):
                finish_debug(p, g, g["x2d"], T)
            else:
                p.barrier()
            p.finalize()
            return nc, p

        p.op("act", lambda e: load_act_set(e, SET_LNEXP))
        build_attn(p, g, "a", g["x0"], g["x1d"], g["stats"][0], g["stats"][1],
                   g["lnTd_a"], g["QTd_a"], g["Vd_a"], NQC)
        if stage == "attn_a":
            finish_debug(p, g, g["x1d"], T)
            p.finalize()
            return nc, p
        p.barrier()

        p.op("act", lambda e: load_act_set(e, SET_SILU))
        build_ffn(p, g, g["x1d"], g["x2d"], g["stats"][1], g["xqTd_1"], g["hqTd_1"], NT)
        if stage == "ffn1":
            finish_debug(p, g, g["x2d"], T)
            p.finalize()
            return nc, p
        p.barrier()

        p.op("act", lambda e: load_act_set(e, SET_LNEXP))
        build_attn(p, g, "b", g["x2d"], g["x3d"], g["stats"][2], g["stats"][3],
                   g["lnTd_b"], g["QTd_b"], g["Vd_b"], NQC_OWN)
        if stage == "attn_b":
            finish_debug(p, g, g["x3d"], TO)
            p.finalize()
            return nc, p
        p.barrier()

        p.op("act", lambda e: load_act_set(e, SET_SILU))
        build_ffn(p, g, g["x3d"], None, g["stats"][3], g["xqTd_2"], g["hqTd_2"], NTO,
                  final_out=g["out"])
        p.barrier()
        p.finalize()
    return nc, p


def dump_p0(p, g):
    nc = p.nc
    with ExitStack() as ph:
        t32 = p.sbuf("d_t32", [128, T], F32, ph)
        tb = p.sbuf("d_tb", [128, D], BF16, ph)
        dbgv = g["dbg"].ap.rearrange("(a p r) d -> a p (r d)", p=128, r=4)
        for ix, srcb in enumerate((g["Cd"], g["Sd"])):
            p.op("qs", lambda e, o=t32.ap, i=srcb.ap: e.dma_start(out=o[:, :], in_=i[:, :]),
                 reads=[srcb], writes=[t32])
            p.op("qs", lambda e, o=dbgv, i=t32.ap, ix=ix: e.dma_start(out=o[ix], in_=i[:, :]),
                 reads=[t32], writes=[g["dbg"]])
        tb3 = tb.ap.rearrange("c (dk p) -> c dk p", p=128)
        for ix, (srcb, apsel, o_ap) in enumerate((
                (g["wgTd"], g["wgTd"].ap[:, :, 0, :], tb3),
                (g["bWbf"], g["bWbf"].ap[0, 0:128, :], tb.ap[:, :]),
                (g["wdTd"], g["wdTd"].ap[:, 0:NDT, 0, :], tb3))):
            p.op("qs", lambda e, o=o_ap, i=apsel: e.dma_start(out=o, in_=i),
                 reads=[srcb], writes=[tb])
            p.op("dve", lambda e, o=t32.ap, i=tb.ap: e.tensor_copy(o[:, 0:D], i[:, :]),
                 reads=[tb], writes=[t32])
            p.op("qs", lambda e, o=g["dbg"].ap, i=t32.ap, ix=ix: e.dma_start(
                out=o[2048 + ix * 128: 2048 + (ix + 1) * 128, :], in_=i[:, 0:D]),
                reads=[t32], writes=[g["dbg"]])
        p.barrier()


def finish_debug(p, g, src, rows):
    with ExitStack() as ph:
        tmp = p.sbuf("dbg_tmp", [128, D], F32, ph)
        for i in range(rows // 128):
            p.op("qs", lambda e, o=tmp.ap, i_=src.ap, i2=i: e.dma_start(
                out=o[:, :], in_=i_[i2 * 128:(i2 + 1) * 128, :]), reads=[src], writes=[tmp])
            p.op("qs", lambda e, o=g["dbg"].ap, i_=tmp.ap, i2=i: e.dma_start(
                out=o[i2 * 128:(i2 + 1) * 128, :], in_=i_[:, :]), reads=[tmp], writes=[g["dbg"]])
        p.barrier()


def build_phase0(p, g):
    nc = p.nc
    with ExitStack() as ph:
        tmpf = p.sbuf("c_tmpf", [128, 128], F32, ph)
        p.op("qs", lambda e, o=tmpf.ap, i=g["ones_in"].ap: e.dma_start(out=o[:, 0:64], in_=i[:, :]),
             reads=[g["ones_in"]], writes=[tmpf])
        p.op("dve", lambda e, o=g["ones64"].ap, i=tmpf.ap: e.tensor_copy(o[:, :], i[:, 0:64]),
             reads=[tmpf], writes=[g["ones64"]])
        p.op("qs", lambda e, o=tmpf.ap, i=g["perm_in"].ap: e.dma_start(out=o[:, :], in_=i[:, :]),
             reads=[g["perm_in"]], writes=[tmpf])
        p.op("dve", lambda e, o=g["perm"].ap, i=tmpf.ap: e.tensor_copy(o[:, :], i[:, :]),
             reads=[tmpf], writes=[g["perm"]])
        p.op("qs", lambda e, o=g["vb_sb"].ap, i=g["vb"].ap: e.dma_start(out=o[:, :], in_=i[:, :]),
             reads=[g["vb"]], writes=[g["vb_sb"]])
        p.op("qs", lambda e, o=g["qkvb_sb"].ap, i=g["qkvb"].ap: e.dma_start(out=o[:, :], in_=i[:, :]),
             reads=[g["qkvb"]], writes=[g["qkvb_sb"]])

        onesrow = p.sbuf("c_onesrow", [1, 128], BF16, ph)
        pbrow = p.sbuf("c_pbrow", [1, D], BF16, ph)
        pbrow32 = p.sbuf("c_pbrow32", [1, D], F32, ph)
        p.op("pool", lambda e, o=onesrow.ap: e.memset(o[:, :], 1.0), writes=[onesrow])
        p.op("qs", lambda e, o=pbrow32.ap, i=g["projb_row"].ap: e.dma_start(out=o[:, :], in_=i[:, :]),
             reads=[g["projb_row"]], writes=[pbrow32])
        p.op("dve", lambda e, o=pbrow.ap, i=pbrow32.ap: e.tensor_copy(o[:, :], i[:, :]),
             reads=[pbrow32], writes=[pbrow])
        psb = p.psum("c_ps", [128, 512], ph)
        for ch in range(2):
            p.op("pe", lambda e, o=psb.ap, l=onesrow.ap, r=pbrow.ap, c=ch: e.matmul(
                o[:, :], l[:, :], r[:, c * 512:(c + 1) * 512], start=True, stop=True),
                reads=[onesrow, pbrow], writes=[psb], geom="k1")
            p.op("dve", lambda e, o=g["biasrep"].ap, i=psb.ap, c=ch: e.tensor_copy(
                o[:, c * 512:(c + 1) * 512], i[:, :]),
                reads=[psb], writes=[g["biasrep"]])
        p.barrier()

    with ExitStack() as ph:
        posb = p.sbuf("c_pos", [64, T], F32, ph)
        sgnb = p.sbuf("c_sgn", [64, 1], F32, ph)
        inv = p.sbuf("c_inv", [64, 1], F32, ph)
        freq = p.sbuf("c_freq", [64, T], F32, ph)
        freq2 = p.sbuf("c_freq2", [64, T], F32, ph)
        kk = p.sbuf("c_kk", [64, T], I32, ph)
        s1 = p.sbuf("c_s1", [64, T], F32, ph)
        s2 = p.sbuf("c_s2", [64, T], F32, ph)
        tabC = p.sbuf("c_tabC", [128, T], F32, ph)
        tabS = p.sbuf("c_tabS", [128, T], F32, ph)
        for src, dst in ((g["pos_rep"], posb), (g["inv_in"], inv), (g["sgnpat"], sgnb)):
            p.op("qs", lambda e, o=dst.ap, i=src.ap: e.dma_start(out=o[:, :], in_=i[:, :]),
                 reads=[src], writes=[dst])
        p.op("act", lambda e: load_act_set(e, SET_TRIG))
        p.op("dve", lambda e, o=freq.ap, i=posb.ap, s=inv.ap: e.tensor_scalar(
            o[:, :], i[:, :], s[:, 0:1], None, ALU.mult),
            reads=[posb, inv], writes=[freq])
        p.op("dve", lambda e, o=kk.ap, i=freq.ap: e.tensor_scalar(
            o[:, :], i[:, :], 1.0 / (2 * math.pi), None, ALU.mult),
            reads=[freq], writes=[kk])
        TWO_PI = 2 * math.pi
        c1 = float(np.float32(TWO_PI))
        c2 = float(np.float32(TWO_PI - c1))
        c3 = float(np.float32(TWO_PI - c1 - c2))
        p.op("dve", lambda e, o=s1.ap, i0=kk.ap, i1=freq.ap: e.scalar_tensor_tensor(
            o[:, :], i0[:, :], -c1, i1[:, :], ALU.mult, ALU.add),
            reads=[kk, freq], writes=[s1])
        p.op("dve", lambda e, o=s2.ap, i0=kk.ap, i1=s1.ap: e.scalar_tensor_tensor(
            o[:, :], i0[:, :], -c2, i1[:, :], ALU.mult, ALU.add),
            reads=[kk, s1], writes=[s2])
        p.op("dve", lambda e, o=s1.ap, i0=kk.ap, i1=s2.ap: e.scalar_tensor_tensor(
            o[:, :], i0[:, :], -c3, i1[:, :], ALU.mult, ALU.add),
            reads=[kk, s2], writes=[s1])
        p.op("dve", lambda e, o=freq2.ap, i=s1.ap: e.tensor_scalar(
            o[:, :], i[:, :], math.pi, 2 * math.pi, ALU.is_gt, ALU.mult),
            reads=[s1], writes=[freq2])
        p.op("dve", lambda e, o=s1.ap, i0=s1.ap, i1=freq2.ap: e.tensor_tensor(
            o[:, :], i0[:, :], i1[:, :], ALU.subtract),
            reads=[s1, freq2], writes=[s1])
        p.op("dve", lambda e, o=freq2.ap, i=s1.ap: e.tensor_scalar(
            o[:, :], i[:, :], -math.pi, 2 * math.pi, ALU.is_lt, ALU.mult),
            reads=[s1], writes=[freq2])
        p.op("dve", lambda e, o=s1.ap, i0=s1.ap, i1=freq2.ap: e.tensor_tensor(
            o[:, :], i0[:, :], i1[:, :], ALU.add),
            reads=[s1, freq2], writes=[s1])
        p.op("act", lambda e, o=tabS.ap, i=s1.ap: e.activation(out=o[0:64, :], in_=i[:, :], func=AF.Sin),
             reads=[s1], writes=[tabS])
        p.op("dve", lambda e, o=s2.ap, i=s1.ap: e.tensor_scalar(o[:, :], i[:, :], math.pi / 2, None, ALU.add),
             reads=[s1], writes=[s2])
        p.op("dve", lambda e, o=freq2.ap, i=s2.ap: e.tensor_scalar(o[:, :], i[:, :], math.pi, 2 * math.pi, ALU.is_gt, ALU.mult),
             reads=[s2], writes=[freq2])
        p.op("dve", lambda e, o=s2.ap, i0=s2.ap, i1=freq2.ap: e.tensor_tensor(o[:, :], i0[:, :], i1[:, :], ALU.subtract),
             reads=[s2, freq2], writes=[s2])
        p.op("act", lambda e, o=tabC.ap, i=s2.ap: e.activation(out=o[0:64, :], in_=i[:, :], func=AF.Sin),
             reads=[s2], writes=[tabC])
        p.op("dve", lambda e, o=tabS.ap, i=tabS.ap, s=sgnb.ap: e.tensor_scalar(
            o[0:64, :], i[0:64, :], s[:, 0:1], None, ALU.mult),
            reads=[tabS, sgnb], writes=[tabS])
        p.op("dve", lambda e, o=tabC.ap, i=tabC.ap: e.tensor_copy(o[64:128, :], i[0:64, :]),
             reads=[tabC], writes=[tabC])
        p.op("dve", lambda e, o=tabS.ap, i=tabS.ap: e.tensor_copy(o[64:128, :], i[0:64, :]),
             reads=[tabS], writes=[tabS])
        if "dbg_freq" in g:
            p.op("qs", lambda e, o=g["dbg_freq"].ap, i=freq.ap: e.dma_start(out=o[:, :], in_=i[:, :]),
                 reads=[freq], writes=[g["dbg_freq"]])
            p.op("qs", lambda e, o=g["dbg_r"].ap, i=s1.ap: e.dma_start(out=o[:, :], in_=i[:, :]),
                 reads=[s1], writes=[g["dbg_r"]])
            p.op("qs", lambda e, o=g["dbg_rc"].ap, i=s2.ap: e.dma_start(out=o[:, :], in_=i[:, :]),
                 reads=[s2], writes=[g["dbg_rc"]])
            p.op("qs", lambda e, o=g["dbg_inv"].ap, i=inv.ap: e.dma_start(out=o[:, :], in_=i[:, :]),
                 reads=[inv], writes=[g["dbg_inv"]])
        p.op("qs", lambda e, o=g["Cd"].ap, i=tabC.ap: e.dma_start(out=o[:, :], in_=i[:, :]),
             reads=[tabC], writes=[g["Cd"]])
        p.op("qs", lambda e, o=g["Sd"].ap, i=tabS.ap: e.dma_start(out=o[:, :], in_=i[:, :]),
             reads=[tabS], writes=[g["Sd"]])
        p.barrier()

    with ExitStack() as ph:
        qb = {"sets": [mk_quant_bufs(p, "wqa", ph, with_t0=False)]}
        wslab = [p.sbuf(f"c_wslab{i}", [128, DFF], F32, ph) for i in range(2)]
        wqslab = [p.sbuf(f"c_wqslab{i}", [128, DFF], BF16, ph) for i in range(2)]
        for wsrc, wdst, nslab, FW in ((g["wg"], g["wgTd"], NIT, D),
                                      (g["wu"], g["wuTd"], NIT, D),
                                      (g["wd"], g["wdTd"], NDT, DFF)):
            for s in range(nslab):
                sl, ql = wslab[s % 2], wqslab[s % 2]
                p.op("qs", lambda e, o=sl.ap, i=wsrc.ap, s=s, F=FW: e.dma_start(
                    out=o[:, 0:F], in_=i[s * 128:(s + 1) * 128, :]),
                    reads=[wsrc], writes=[sl])
                emit_quant(p, sl, ql, qb, FW)
                p.op("qs", lambda e, o=wdst.ap, i=ql.ap, s=s, F=FW: e.dma_start(
                    out=o[:, :, s, :],
                    in_=i[:, 0:F].rearrange("c (dk p) -> c dk p", p=128)),
                    reads=[ql], writes=[wdst])
        for s in range(3 * NDT):
            sl, ql = wslab[s % 2], wqslab[s % 2]
            p.op("qs", lambda e, o=sl.ap, i=g["bW"].ap, s=s: e.dma_start(
                out=o[:, 0:D], in_=i[s // NDT, (s % NDT) * 128:(s % NDT + 1) * 128, :]),
                reads=[g["bW"]], writes=[sl])
            p.op("dve", lambda e, o=ql.ap, i=sl.ap: e.tensor_copy(o[:, 0:D], i[:, 0:D]),
                 reads=[sl], writes=[ql])
            p.op("qs", lambda e, o=g["bWbf"].ap, i=ql.ap, s=s: e.dma_start(
                out=o[s // NDT, (s % NDT) * 128:(s % NDT + 1) * 128, :], in_=i[:, 0:D]),
                reads=[ql], writes=[g["bWbf"]])
        for s in range(NDT):
            sl, ql = wslab[s % 2], wqslab[s % 2]
            p.op("qs", lambda e, o=sl.ap, i=g["bWp"].ap, s=s: e.dma_start(
                out=o[:, 0:D], in_=i[s * 128:(s + 1) * 128, :]),
                reads=[g["bWp"]], writes=[sl])
            p.op("dve", lambda e, o=ql.ap, i=sl.ap: e.tensor_copy(o[:, 0:D], i[:, 0:D]),
                 reads=[sl], writes=[ql])
            p.op("qs", lambda e, o=g["bWpbf"].ap, i=ql.ap, s=s: e.dma_start(
                out=o[s * 128:(s + 1) * 128, :], in_=i[:, 0:D]),
                reads=[ql], writes=[g["bWpbf"]])
        p.barrier()


def build_attn(p, g, layer, x_src, x_out, stats_ln, stats_next, lnTd, QTd, Vd, nqc):
    nc = p.nc
    rope = layer == "a"
    with ExitStack() as outer:
        KT_sb = p.sbuf(f"{layer}_KT", [128, NDT * T], BF16, outer)
        KT_t = [[p.sub(KT_sb, slice(None), slice(dk * T + tc * QC, dk * T + (tc + 1) * QC))
                 for tc in range(NQC)] for dk in range(NDT)]

        with ExitStack() as ph:
            def S(name, shape, dtype):
                return p.sbuf(f"{layer}A_{name}", shape, dtype, ph)

            xsb = [S(f"x{i}", [128, D], F32) for i in range(2)]
            scr = S("scr", [128, D], F32)
            small = [S(f"sm{i}", [128, 1], F32) for i in range(3)]
            lnsb = [S(f"ln{i}", [128, D], BF16) for i in range(2)]
            lnTc = [S(f"lnT{i}", [128, NDT * 512], BF16) for i in range(2)]
            wq_sb = [S(f"w{dk}", [128, 3 * D], BF16) for dk in range(NDT)]
            wtmp = [S(f"wtmp{i}", [128, 3 * D], F32) for i in range(2)]
            ropeC = [S(f"ropC{i}", [128, QC], F32) for i in range(2)]
            ropeS = [S(f"ropS{i}", [128, QC], F32) for i in range(2)]
            qb_e = [S(f"qb{i}", [128, QC], BF16) for i in range(2)]
            t1_e = [S(f"t1{i}", [128, QC], F32) for i in range(2)]
            t2_e = [S(f"t2{i}", [128, QC], F32) for i in range(2)]
            qro = [S(f"qro{i}", [128, QC], BF16) for i in range(2)]
            vev = [S(f"vev{i}", [128, 512], BF16) for i in range(2)]
            pp = [p.psum(f"{layer}A_pp{i}", [128, 512], ph) for i in range(2)]
            rot = [p.psum(f"{layer}A_rot{i}", [128, 512], ph) for i in range(2)]

            for dk in range(NDT):
                wt = wtmp[dk % 2]
                if layer == "a":
                    p.op("qs", lambda e, o=wt.ap, i=g["qkvwT"].ap, dk=dk: e.dma_start(
                        out=o[:, :], in_=i[dk * 128:(dk + 1) * 128, :]),
                        reads=[g["qkvwT"]], writes=[wt])
                    p.op("dve", lambda e, o=wq_sb[dk].ap, i=wt.ap: e.tensor_copy(o[:, :], i[:, :]),
                         reads=[wt], writes=[wq_sb[dk]])
                else:
                    p.op("qs", lambda e, o=wq_sb[dk].ap, i=g["bWbf"].ap, dk=dk: e.dma_start(
                        out=o.rearrange("p (w d) -> p w d", w=3),
                        in_=i[:, dk * 128:(dk + 1) * 128, :].transpose([1, 0, 2])),
                        reads=[g["bWbf"]], writes=[wq_sb[dk]])

            lp = 0
            for tc in range(K_TC or NQC):
                for tt4 in range(4):
                    ti = tc * 4 + tt4
                    xb, lb = xsb[lp], lnsb[lp]
                    lp ^= 1
                    p.op("qs", lambda e, o=xb.ap, i=x_src.ap, ti=ti: e.dma_start(
                        out=o[:, :], in_=i[ti * 128:(ti + 1) * 128, :]),
                        reads=[x_src], writes=[xb])
                    mu, rstd = stats_ln[ti]
                    emit_ln_stats(p, xb, mu, rstd, scr, small[0], small[1], small[2], g["epsb"])
                    p.op("dve", lambda e, o=lb.ap, i=xb.ap, m=mu.ap, r=rstd.ap: e.tensor_scalar(
                        o[:, :], i[:, :], m[:, 0:1], r[:, 0:1], ALU.subtract, ALU.mult),
                        reads=[xb, mu, rstd], writes=[lb])
                    p.op("qs", lambda e, o=lnTd.ap, i=lb.ap, ti=ti: e.dma_start(
                        out=o[ti, :, :], in_=i[:, :]),
                        reads=[lb], writes=[lnTd])
                lc = lnTc[tc % 2]
                src4 = lnTd.ap.rearrange("t c (dk p) -> t c dk p", p=128)
                ov = lc.ap.rearrange("p (dk r c) -> p dk r c", dk=NDT, r=4)
                for r4 in range(4):
                    p.op("qs", lambda e, o=ov, i=src4, tc=tc, r4=r4: e.dma_start(
                        out=o[:, :, r4, :], in_=i[tc * 4 + r4], transpose=True),
                        reads=[lnTd], writes=[lc])
                if rope:
                    rc, rs = ropeC[tc % 2], ropeS[tc % 2]
                    p.op("qg", lambda e, o=rc.ap, i=g["Cd"].ap, tc=tc: e.dma_start(
                        out=o[:, :], in_=i[:, tc * QC:(tc + 1) * QC]), reads=[g["Cd"]], writes=[rc])
                    p.op("qg", lambda e, o=rs.ap, i=g["Sd"].ap, tc=tc: e.dma_start(
                        out=o[:, :], in_=i[:, tc * QC:(tc + 1) * QC]), reads=[g["Sd"]], writes=[rs])

                for which in ("q", "k"):
                    if which == "q" and tc >= nqc:
                        continue
                    for dt in range(NDT):
                        ps = pp[dt % 2]
                        col = (0 if which == "q" else D) + dt * 128
                        for dk in range(NDT):
                            p.op("pe", lambda e, o=ps.ap, l=wq_sb[dk].ap, r=lc.ap, col=col, dk=dk: e.matmul(
                                o[:, :], l[:, col:col + 128], r[:, dk * 512:(dk + 1) * 512],
                                start=(dk == 0), stop=(dk == NDT - 1)),
                                reads=[wq_sb[dk], lc], writes=[ps], milestone=(dk == NDT - 1))
                        if rope:
                            qbv = qb_e[dt % 2]
                            bias_col = (0 if which == "q" else 8) + dt
                            p.op("dve", lambda e, o=qbv.ap, i=ps.ap, b=g["qkvb_sb"].ap, c=bias_col: e.tensor_scalar(
                                o[:, :], i[:, :], b[:, c:c + 1], None, ALU.add),
                                reads=[ps, g["qkvb_sb"]], writes=[qbv])
                            ro = rot[dt % 2]
                            p.op("pe", lambda e, o=ro.ap, l=g["perm"].ap, r=qbv.ap: e.matmul(
                                o[:, :], l[:, :], r[:, :], start=True, stop=True),
                                reads=[g["perm"], qbv], writes=[ro])
                            tv1, tv2 = t1_e[dt % 2], t2_e[dt % 2]
                            rc, rs = ropeC[tc % 2], ropeS[tc % 2]
                            p.op("dve", lambda e, o=tv1.ap, i0=qbv.ap, i1=rc.ap: e.tensor_tensor(
                                o[:, :], i0[:, :], i1[:, :], ALU.mult),
                                reads=[qbv, rc], writes=[tv1])
                            p.op("dve", lambda e, o=tv2.ap, i0=ro.ap, i1=rs.ap: e.tensor_tensor(
                                o[:, :], i0[:, :], i1[:, :], ALU.mult),
                                reads=[ro, rs], writes=[tv2])
                            dst = qro[dt % 2] if which == "q" else KT_t[dt][tc]
                            p.op("dve", lambda e, o=dst.ap, i0=tv1.ap, i1=tv2.ap: e.tensor_tensor(
                                o[:, :], i0[:, :], i1[:, :], ALU.add),
                                reads=[tv1, tv2], writes=[dst])
                        else:
                            dst = qro[dt % 2] if which == "q" else KT_t[dt][tc]
                            p.op("act", lambda e, o=dst.ap, i=ps.ap: e.activation(
                                out=o[:, :], in_=i[:, :], func=AF.Copy),
                                reads=[ps], writes=[dst])
                        if which == "q":
                            dst = qro[dt % 2]
                            p.op("qs", lambda e, o=QTd.ap, i=dst.ap, dt=dt, tc=tc: e.dma_start(
                                out=o[dt, :, tc * QC:(tc + 1) * QC], in_=i[:, :]),
                                reads=[dst], writes=[QTd])

                for tt4 in range(4):
                    ti = tc * 4 + tt4
                    for vc in range(2):
                        ps = pp[vc]
                        for dk in range(NDT):
                            p.op("pe", lambda e, o=ps.ap, l=lc.ap, r=wq_sb[dk].ap, tt4=tt4, vc=vc, dk=dk: e.matmul(
                                o[:, :], l[:, dk * 512 + tt4 * 128: dk * 512 + (tt4 + 1) * 128],
                                r[:, 2 * D + vc * 512: 2 * D + (vc + 1) * 512],
                                start=(dk == 0), stop=(dk == NDT - 1)),
                                reads=[lc, wq_sb[dk]], writes=[ps], milestone=(dk == NDT - 1))
                        ve = vev[vc]
                        p.op("act", lambda e, o=ve.ap, i=ps.ap: e.activation(
                            out=o[:, :], in_=i[:, :], func=AF.Copy),
                            reads=[ps], writes=[ve])
                        p.op("qs", lambda e, o=Vd.ap, i=ve.ap, ti=ti, vc=vc: e.dma_start(
                            out=o[ti, :, vc * 512:(vc + 1) * 512], in_=i[:, :]),
                            reads=[ve], writes=[Vd])
            p.barrier()

        with ExitStack() as ph:
            def S2(name, shape, dtype):
                return p.sbuf(f"{layer}B_{name}", shape, dtype, ph)

            V_sb = S2("V", [128, NT * D], BF16)
            V_t = [p.sub(V_sb, slice(None), slice(kt * D, (kt + 1) * D)) for kt in range(NT)]
            QT_c = [S2(f"QTc{i}", [128, NDT * QC], BF16) for i in range(2)]
            PT2 = S2("PT", [128, 2048], BF16)
            PT = [p.sub(PT2, slice(None), slice(i * 1024, (i + 1) * 1024)) for i in range(2)]
            lnD = S2("lnD", [128, 512], F32)
            rdrep = S2("rdrep", [128, 512], BF16)
            attn_sb = S2("attn", [128, NPAIR * QC], BF16)
            attn_t = [p.sub(attn_sb, slice(None), slice(pr * QC, (pr + 1) * QC)) for pr in range(NPAIR)]
            pw_sb = [S2(f"pw{i}", [128, D], BF16) for i in range(NDT)]
            pwt = [S2("pwt0", [128, D], F32)] if layer == "a" else None
            xprev = [S2(f"xp{i}", [128, D], F32) for i in range(2)]
            xnew = [S2(f"xo{i}", [128, D], F32) for i in range(2)]
            scrB = S2("scrB", [128, D], F32)
            ln3t = scrB if layer == "a" else S2("ln3t", [128, D], F32)
            smallB = [S2(f"smB{i}", [128, 1], F32) for i in range(3)]
            SS2 = p.psum(f"{layer}B_SS", [128, 2048], ph)
            SSh = [[p.sub(SS2, slice(None), slice(j * 1024 + hh * 512, j * 1024 + (hh + 1) * 512))
                    for hh in range(2)] for j in range(2)]
            PVh = [p.psum(f"{layer}B_PV{hh}", [128, 512], ph) for hh in range(2)]
            DDh = [p.psum(f"{layer}B_DD{hh}", [128, 512], ph) for hh in range(2)]
            PR = SSh[0]

            for kt in range(NT):
                p.op("qs" if kt % 2 == 0 else "qg",
                     lambda e, o=V_t[kt].ap, i=Vd.ap, kt=kt: e.dma_start(out=o[:, :], in_=i[kt, :, :]),
                     reads=[Vd], writes=[V_t[kt]])
            for dt in range(NDT):
                if layer == "a":
                    wt = pwt[0]
                    p.op("qg", lambda e, o=wt.ap, i=g["projwT"].ap, dt=dt: e.dma_start(
                        out=o[:, :], in_=i[dt * 128:(dt + 1) * 128, :]),
                        reads=[g["projwT"]], writes=[wt])
                    p.op("dve", lambda e, o=pw_sb[dt].ap, i=wt.ap: e.tensor_copy(o[:, :], i[:, :]),
                         reads=[wt], writes=[pw_sb[dt]])
                else:
                    p.op("qg", lambda e, o=pw_sb[dt].ap, i=g["bWpbf"].ap, dt=dt: e.dma_start(
                        out=o[:, :], in_=i[dt * 128:(dt + 1) * 128, :]),
                        reads=[g["bWpbf"]], writes=[pw_sb[dt]])

            for qc in range(K_QC or nqc):
                qtc = QT_c[qc % 2]
                p.op("qs", lambda e, o=qtc.ap, i=QTd.ap, qc=qc: e.dma_start(
                    out=o.rearrange("p (dt c) -> p dt c", dt=NDT),
                    in_=i[:, :, qc * QC:(qc + 1) * QC].transpose([1, 0, 2])),
                    reads=[QTd], writes=[qtc])
                for pr0 in range(K_PAIRS or NPAIR):
                    pr = pr0 + K_PRBASE
                    nkt = K_KT or NT
                    skips0 = set(K_SKIP.split(",")) if (K_SKIP and (pr >= 1 or K_PRBASE)) else set()
                    if "all" in skips0:
                        continue
                    def emit_S2(kt):
                        if "s" in skips0:
                            return
                        n2 = 2 if kt + 1 < nkt else 1
                        for j in range(n2):
                            for hh in range(2):
                                o0 = 64 * hh
                                sshb = SSh[(kt + j) % 2][hh]
                                p.op("pe", lambda e, o=sshb.ap, l=KT_sb.ap, r=qtc.ap, pr=pr, kt=kt + j, o0=o0: e.matmul(
                                    o[:, :],
                                    l[o0:o0 + 64, pr * T + kt * 128: pr * T + (kt + 1) * 128],
                                    r[o0:o0 + 64, pr * QC:(pr + 1) * QC],
                                    start=True, stop=True),
                                    reads=[KT_sb, qtc], writes=[sshb], milestone=(hh == 1), geom="r")
                        if "exp" not in skips0:
                            p.op("act", lambda e, o=PT2.ap, i=SS2.ap, n2=n2: e.activation(
                                out=o[:, 0:n2 * 1024], in_=i[:, 0:n2 * 1024],
                                func=AF.Exp, scale=ATT_SCALE),
                                reads=[SSh[0][0], SSh[0][1]] + ([SSh[1][0], SSh[1][1]] if n2 == 2 else []),
                                writes=[PT[0]] + ([PT[1]] if n2 == 2 else []))

                    def emit_PVDD(kt):
                        if "pvdd" in skips0:
                            return
                        pt = PT[kt % 2]
                        for hh in range(2):
                            tp = (0, 64 * hh)
                            st_ = (kt == 0)
                            sp_ = (kt == nkt - 1)
                            vpr = 0 if "vclamp" in skips0 else pr
                            if "pv" not in skips0:
                                p.op("pe", lambda e, o=PVh[hh].ap, l=V_t[kt].ap, r=pt.ap, pr=vpr, kt=kt, hh=hh, tp=tp, st_=st_, sp_=sp_: e.matmul(
                                    o[64 * hh:64 * (hh + 1), :],
                                    l[:, (2 * pr + hh) * 64:(2 * pr + hh + 1) * 64],
                                    r[:, hh * 512:(hh + 1) * 512],
                                    start=st_, stop=sp_, tile_position=tp),
                                    reads=[V_t[kt], pt], writes=[PVh[hh]], milestone=sp_, geom="c")
                            if "dd" not in skips0:
                                p.op("pe", lambda e, o=DDh[hh].ap, l=g["ones64"].ap, r=pt.ap, kt=kt, hh=hh, tp=tp, st_=st_, sp_=sp_: e.matmul(
                                    o[64 * hh:64 * (hh + 1), :], l[:, :], r[:, hh * 512:(hh + 1) * 512],
                                    start=st_, stop=sp_, tile_position=tp),
                                    reads=[g["ones64"], pt], writes=[DDh[hh]], milestone=True, geom="c")

                    emit_S2(0)
                    for kt in range(0, nkt, 2):
                        if kt + 2 < nkt:
                            emit_S2(kt + 2)
                        emit_PVDD(kt)
                        if kt + 1 < nkt:
                            emit_PVDD(kt + 1)
                    at = attn_t[pr]
                    skips = set(K_SKIP.split(",")) if (K_SKIP and (pr >= 1 or K_PRBASE)) else set()
                    for hh in range(2):
                        if "lnd" in skips:
                            break
                        rows = slice(64 * hh, 64 * (hh + 1))
                        p.op("act", lambda e, o=lnD.ap, i=DDh[hh].ap, r=rows: e.activation(
                            out=o[r, :], in_=i[r, :], func=AF.Ln),
                            reads=[DDh[hh]], writes=[lnD])
                    if "rd" not in skips:
                        p.op("act", lambda e, o=rdrep.ap, i=lnD.ap: e.activation(
                            out=o[:, :], in_=i[:, :], func=AF.Exp, scale=-1.0),
                            reads=[lnD], writes=[rdrep])
                    for hh in range(2):
                        if "at" in skips:
                            break
                        rows = slice(64 * hh, 64 * (hh + 1))
                        p.op("dve", lambda e, o=at.ap, i0=PVh[hh].ap, i1=rdrep.ap, r=rows: e.tensor_tensor(
                            o[r, :], i0[r, :], i1[r, :], ALU.mult),
                            reads=[PVh[hh], rdrep], writes=[at])
                    if layer == "a":
                        p.op("dve", lambda e, o=at.ap, i=at.ap, b=g["vb_sb"].ap, pr=pr: e.tensor_scalar(
                            o[:, :], i[:, :], b[:, pr:pr + 1], None, ALU.add),
                            reads=[at, g["vb_sb"]], writes=[at])
                for tt4 in range(4):
                    ti = qc * 4 + tt4
                    xp, xo = xprev[tt4 % 2], xnew[tt4 % 2]
                    p.op("qg", lambda e, o=xp.ap, i=x_src.ap, ti=ti: e.dma_start(
                        out=o[:, :], in_=i[ti * 128:(ti + 1) * 128, :]),
                        reads=[x_src], writes=[xp])
                    for ch in range(2):
                        ps = PR[ch]
                        for pr in range(NPAIR):
                            p.op("pe", lambda e, o=ps.ap, l=attn_t[pr].ap, r=pw_sb[pr].ap, tt4=tt4, ch=ch, pr=pr: e.matmul(
                                o[:, :], l[:, tt4 * 128:(tt4 + 1) * 128], r[:, ch * 512:(ch + 1) * 512],
                                start=(pr == 0), stop=(pr == NPAIR - 1)),
                                reads=[attn_t[pr], pw_sb[pr]], writes=[ps], milestone=(pr == NPAIR - 1))
                    for ch in range(2):
                        p.op("dve", lambda e, o=xo.ap, i0=PR[ch].ap, i1=xp.ap, ch=ch: e.tensor_tensor(
                            o[:, ch * 512:(ch + 1) * 512], i0[:, :], i1[:, ch * 512:(ch + 1) * 512], ALU.add),
                            reads=[PR[ch], xp], writes=[xo])
                    if layer == "a":
                        p.op("dve", lambda e, o=xo.ap, i0=xo.ap, i1=g["biasrep"].ap: e.tensor_tensor(
                            o[:, :], i0[:, :], i1[:, :], ALU.add),
                            reads=[xo, g["biasrep"]], writes=[xo])
                    else:
                        mu, rstd = stats_ln[ti]
                        p.op("dve", lambda e, o=ln3t.ap, i=xp.ap, m=mu.ap, r=rstd.ap: e.tensor_scalar(
                            o[:, :], i[:, :], m[:, 0:1], r[:, 0:1], ALU.subtract, ALU.mult),
                            reads=[xp, mu, rstd], writes=[ln3t])
                        p.op("dve", lambda e, o=xo.ap, i0=xo.ap, i1=ln3t.ap: e.tensor_tensor(
                            o[:, :], i0[:, :], i1[:, :], ALU.add),
                            reads=[xo, ln3t], writes=[xo])
                    mu2, rstd2 = stats_next[ti]
                    emit_ln_stats(p, xo, mu2, rstd2, scrB, smallB[0], smallB[1], smallB[2], g["epsb"])
                    p.op("qg", lambda e, o=x_out.ap, i=xo.ap, ti=ti: e.dma_start(
                        out=o[ti * 128:(ti + 1) * 128, :], in_=i[:, :]),
                        reads=[xo], writes=[x_out])
            p.barrier()


def build_ffn(p, g, x_src, x_out, stats_ln, xqTd, hqTd, nslab, final_out=None):
    nc = p.nc
    with ExitStack() as ph:
        def S(name, shape, dtype):
            return p.sbuf(f"f{nslab}_{name}", shape, dtype, ph)

        wgT = [S(f"wgT{dk}", [128, DFF], BF16) for dk in range(NDT)]
        wuT = [S(f"wuT{dk}", [128, DFF], BF16) for dk in range(NDT)]
        wdT = [S(f"wdT{k}", [128, D], BF16) for k in range(NIT)]
        for dk in range(NDT):
            p.op("qs", lambda e, o=wgT[dk].ap, i=g["wgTd"].ap, dk=dk: e.dma_start(
                out=o.rearrange("p (r c) -> p r c", r=NIT), in_=i[:, dk, :, :], transpose=True),
                reads=[g["wgTd"]], writes=[wgT[dk]])
            p.op("qs", lambda e, o=wuT[dk].ap, i=g["wuTd"].ap, dk=dk: e.dma_start(
                out=o.rearrange("p (r c) -> p r c", r=NIT), in_=i[:, dk, :, :], transpose=True),
                reads=[g["wuTd"]], writes=[wuT[dk]])
        for k in range(NIT):
            p.op("qs", lambda e, o=wdT[k].ap, i=g["wdTd"].ap, k=k: e.dma_start(
                out=o.rearrange("p (r c) -> p r c", r=NDT), in_=i[:, k, :, :], transpose=True),
                reads=[g["wdTd"]], writes=[wdT[k]])

        xsl = [S(f"x{i}", [128, D], F32) for i in range(2)]
        xq0 = S("xq0", [128, D], BF16)
        xq = [xq0, xq0]
        xqT = [S(f"xqT{i}", [128, NDT * 128], BF16) for i in range(2)]
        lnf0 = S("lnf0", [128, D], F32)
        hsl = [S(f"h{i}", [128, DFF], F32) for i in range(2)]
        hq0 = S("hq0", [128, DFF], BF16)
        hq = [hq0, hq0]
        hqT0 = S("hqT0", [128, NIT * 128], BF16)
        hqT = [hqT0, hqT0]
        sg0 = S("sg0", [128, 512], F32)
        sg = [sg0, sg0]
        xo0 = S("xo0", [128, D], F32)
        xo = [xo0, xo0]
        lnf = [lnf0, lnf0]
        qbx = {"sets": [mk_quant_bufs(p, f"fqa{nslab}", ph, qf=512, with_t0=False)]}
        gu = [p.psum(f"f{nslab}_gu{i}", [128, 512], ph) for i in range(4)]
        dn = [p.psum(f"f{nslab}_dn{i}", [128, 512], ph) for i in range(2)]

        CH = [(i * 512, min(512, DFF - i * 512)) for i in range((DFF + 511) // 512)]
        nsl = K_FSLAB or nslab

        def part1a(s):
            par = s % 2
            xb, lb, xqb, xqTb = xsl[par], lnf[par], xq[par], xqT[par]
            p.op("qg", lambda e, o=xb.ap, i=x_src.ap, s=s: e.dma_start(
                out=o[:, :], in_=i[s * 128:(s + 1) * 128, :]),
                reads=[x_src], writes=[xb])
            mu, rstd = stats_ln[s]
            p.op("dve", lambda e, o=lb.ap, i=xb.ap, m=mu.ap, r=rstd.ap: e.tensor_scalar(
                o[:, :], i[:, :], m[:, 0:1], r[:, 0:1], ALU.subtract, ALU.mult),
                reads=[xb, mu, rstd], writes=[lb])
            emit_quant(p, lb, xqb, qbx, D)
            p.op("qs", lambda e, o=xqTd.ap, i=xqb.ap, s=s: e.dma_start(
                out=o[s, :, :], in_=i[:, :]),
                reads=[xqb], writes=[xqTd])
            p.op("qs", lambda e, o=xqTb.ap, i=xqTd.ap, s=s: e.dma_start(
                out=o.rearrange("p (dk c) -> p dk c", dk=NDT),
                in_=i[s].rearrange("c (dk p) -> c dk p", p=128), transpose=True),
                reads=[xqTd], writes=[xqTb])

        def part1b(s):
            par = s % 2
            xqTb, hb = xqT[par], hsl[par]
            for ci, (c0, cw) in enumerate(CH):
                gp, up = gu[2 * (ci % 2)], gu[2 * (ci % 2) + 1]
                for dk in range(NDT):
                    p.op("pe", lambda e, o=gp.ap, l=xqTb.ap, r=wgT[dk].ap, dk=dk, c0=c0, cw=cw: e.matmul(
                        o[:, 0:cw], l[:, dk * 128:(dk + 1) * 128], r[:, c0:c0 + cw],
                        start=(dk == 0), stop=(dk == NDT - 1)),
                        reads=[xqTb, wgT[dk]], writes=[gp], milestone=(dk == NDT - 1))
                for dk in range(NDT):
                    p.op("pe", lambda e, o=up.ap, l=xqTb.ap, r=wuT[dk].ap, dk=dk, c0=c0, cw=cw: e.matmul(
                        o[:, 0:cw], l[:, dk * 128:(dk + 1) * 128], r[:, c0:c0 + cw],
                        start=(dk == 0), stop=(dk == NDT - 1)),
                        reads=[xqTb, wuT[dk]], writes=[up], milestone=(dk == NDT - 1))
                sgb = sg[ci % 2]
                p.op("act", lambda e, o=sgb.ap, i=gp.ap, cw=cw: e.activation(
                    out=o[:, 0:cw], in_=i[:, 0:cw], func=AF.Silu),
                    reads=[gp], writes=[sgb])
                p.op("dve", lambda e, o=hb.ap, i0=sgb.ap, i1=up.ap, c0=c0, cw=cw: e.tensor_tensor(
                    o[:, c0:c0 + cw], i0[:, 0:cw], i1[:, 0:cw], ALU.mult),
                    reads=[sgb, up], writes=[hb])

        def part2a(s):
            par = s % 2
            hb, hqb, hqTb = hsl[par], hq[par], hqT[par]
            emit_quant(p, hb, hqb, qbx, DFF)
            p.op("qs", lambda e, o=hqTd.ap, i=hqb.ap, s=s: e.dma_start(
                out=o[s, :, :], in_=i[:, :]),
                reads=[hqb], writes=[hqTd])
            p.op("qs", lambda e, o=hqTb.ap, i=hqTd.ap, s=s: e.dma_start(
                out=o.rearrange("p (k c) -> p k c", k=NIT),
                in_=i[s].rearrange("c (k p) -> c k p", p=128), transpose=True),
                reads=[hqTd], writes=[hqTb])

        def part2b(s):
            par = s % 2
            xb, hqTb, xob = xsl[par], hqT[par], xo[par]
            for ch in range(2):
                dp = dn[ch]
                for k in range(NIT):
                    p.op("pe", lambda e, o=dp.ap, l=hqTb.ap, r=wdT[k].ap, k=k, ch=ch: e.matmul(
                        o[:, :], l[:, k * 128:(k + 1) * 128], r[:, ch * 512:(ch + 1) * 512],
                        start=(k == 0), stop=(k == NIT - 1)),
                        reads=[hqTb, wdT[k]], writes=[dp], milestone=(k == NIT - 1))
                p.op("dve", lambda e, o=xob.ap, i0=dn[ch].ap, i1=xb.ap, ch=ch: e.tensor_tensor(
                    o[:, ch * 512:(ch + 1) * 512], i0[:, :], i1[:, ch * 512:(ch + 1) * 512], ALU.add),
                    reads=[dn[ch], xb], writes=[xob])
            tgt = final_out if final_out is not None else x_out
            p.op("qg", lambda e, o=tgt.ap, i=xob.ap, s=s: e.dma_start(
                out=o[s * 128:(s + 1) * 128, :], in_=i[:, :]),
                reads=[xob], writes=[tgt])

        part1a(0)
        part1b(0)
        for s in range(nsl):
            if s + 1 < nsl:
                part1a(s + 1)
            part2a(s)
            if s + 1 < nsl:
                part1b(s + 1)
            part2b(s)
        p.barrier()


_PROG_CACHE = {}


def _get_prog(stage="full"):
    if stage not in _PROG_CACHE:
        _PROG_CACHE[stage] = build_program(stage)
    return _PROG_CACHE[stage]


def host_inputs(inputs):
    x = np.asarray(inputs["L0_attn_a_x"])
    qkv_w = np.asarray(inputs["L0_attn_a_qkv_w"])
    qkv_b = np.asarray(inputs["L0_attn_a_qkv_b"])
    proj_w = np.asarray(inputs["L0_attn_a_proj_w"])
    proj_b = np.asarray(inputs["L0_attn_a_proj_b"])
    Wq = np.asarray(inputs["L0_attn_b_Wq"])
    Wk = np.asarray(inputs["L0_attn_b_Wk"])
    Wv = np.asarray(inputs["L0_attn_b_Wv"])
    Wp = np.asarray(inputs["L0_attn_b_Wp"])
    wg = np.asarray(inputs["L0_ffn_w_gate"])
    wu = np.asarray(inputs["L0_ffn_w_up"])
    wd = np.asarray(inputs["L0_ffn_w_down"])

    common = {
        "qkvwT": np.ascontiguousarray(qkv_w.T),
        "qkvb": np.ascontiguousarray(qkv_b[:2048].reshape(16, 128).T),
        "vb": np.ascontiguousarray(qkv_b[2048:].reshape(8, 128).T),
        "projwT": np.ascontiguousarray(proj_w.T),
        "projb_row": np.ascontiguousarray(proj_b.reshape(1, D)),
        "bW": np.ascontiguousarray(np.stack([Wq, Wk, Wv], 0)),
        "bWp": np.ascontiguousarray(Wp),
        "wg": np.ascontiguousarray(wg), "wu": np.ascontiguousarray(wu),
        "wd": np.ascontiguousarray(wd),
        "ones_in": np.ones((128, 64), np.float32),
    }
    permv = np.zeros((128, 128), np.float32)
    for j in range(64):
        permv[2 * j, 2 * j + 1] = 1.0
        permv[2 * j + 1, 2 * j] = 1.0
    common["perm_in"] = permv
    invv = (1.0 / 10000.0 ** (np.arange(0, 64, 2, dtype=np.float32) / 64)).astype(np.float32)
    common["inv_in"] = np.repeat(invv, 2).reshape(64, 1)
    common["sgnpat"] = np.where(np.arange(64) % 2 == 0, -1.0, 1.0).astype(np.float32).reshape(64, 1)

    in_maps = []
    for c in range(8):
        b, h = c // 2, c % 2
        xb = x[b]
        posv = np.arange(T, dtype=np.float32)
        if h == 1:
            xb = np.roll(xb, -TO, axis=0)
            posv = np.roll(posv, -TO)
        m = dict(common)
        m["x0"] = np.ascontiguousarray(xb)
        m["pos_rep"] = np.ascontiguousarray(np.tile(posv.reshape(1, T), (64, 1)))
        in_maps.append(m)
    return in_maps


def kernel(**inputs):
    nc, _ = _get_prog("full")
    in_maps = host_inputs(inputs)
    res = run_bass_kernel_spmd(nc, in_maps, core_ids=list(range(8)))
    outp = np.zeros((4, T, D), np.float32)
    for c in range(8):
        b, h = c // 2, c % 2
        outp[b, h * TO:(h + 1) * TO, :] = np.asarray(res.results[c]["out"])
    return outp


# version 14 (final)
# speedup vs baseline: 1.4171x; 1.4171x over previous
import math
import os
from contextlib import ExitStack

def _env(name, dflt):
    return int(os.environ.get(name, dflt))

K_QC = _env("K_QC", 0)
K_PAIRS = _env("K_PAIRS", 0)
K_KT = _env("K_KT", 0)
K_TC = _env("K_TC", 0)
K_SKIP = os.environ.get("K_SKIP", "")
K_PRBASE = _env("K_PRBASE", 0)
K_FSLAB = _env("K_FSLAB", 0)


import numpy as np
import ml_dtypes

import concourse.bass as bass
import concourse.mybir as mybir
from concourse.bass_utils import run_bass_kernel_spmd

F32 = mybir.dt.float32
BF16 = mybir.dt.bfloat16
I32 = mybir.dt.int32
U16 = mybir.dt.uint16
ALU = mybir.AluOpType
AF = mybir.ActivationFunctionType

SET_TRIG = 9
SET_LNEXP = 6
SET_SILU = 18


def load_act_set(e, idx):
    return e.add_instruction(mybir.InstLoadActFuncSet(
        name=e.bass.get_next_instruction_name(),
        act_func_set_id=idx, ins=[], outs=[]))
AX = mybir.AxisListType

T = 4096
TO = 2048
D = 1024
NH = 16
HD = 64
NPAIR = 8
DFF = 2816
QC = 512
NQC = T // QC
NQC_OWN = TO // QC
NT = T // 128
NTO = TO // 128
NDT = D // 128
NIT = DFF // 128
EPS = 1e-5
ATT_SCALE = 1.0 / math.sqrt(HD)
QF = 1408

ENGS = ("pe", "act", "dve", "pool", "qs", "qg")


class Buf:
    __slots__ = ("ap", "w", "r", "name")

    def __init__(self, ap, name=""):
        self.ap = ap
        self.name = name
        self.w = None
        self.r = []


class Prog:
    def __init__(self, nc, stack: ExitStack):
        self.nc = nc
        self.stack = stack
        self.stream_of = {"pe": "pe", "act": "act", "dve": "dve",
                          "pool": "pg", "qg": "pg", "qs": "qs"}
        self.thunks = {st: [] for st in ("pe", "act", "dve", "pg", "qs")}
        self.nsem = {"pe": 1, "act": 1, "dve": 1, "pool": 1, "qs": 6, "qg": 6}
        self.cnt = {}
        self.rr = {e: 0 for e in ENGS}
        self.semkeys = {}
        for e in ENGS:
            self.semkeys[e] = [f"{e}{i}" for i in range(self.nsem[e])]
            for k in self.semkeys[e]:
                self.cnt[k] = 0
        self.wm = {st: {} for st in ("pe", "act", "dve", "pg", "qs")}
        self.n_ops = 0
        self.pe_geom = None
        self.pe_pending = 0

    def sbuf(self, name, shape, dtype, stack=None):
        t = (stack or self.stack).enter_context(self.nc.sbuf_tensor(name, shape, dtype))
        return Buf(t.ap(), name)

    def psum(self, name, shape, stack=None):
        t = (stack or self.stack).enter_context(self.nc.psum_tensor(name, shape, F32))
        return Buf(t.ap(), name)

    def dram(self, name, shape, dtype, kind="Internal"):
        t = self.nc.dram_tensor(name, shape, dtype, kind=kind)
        return Buf(t.ap(), name)

    def sub(self, buf, *slices):
        return Buf(buf.ap.__getitem__(slices if len(slices) > 1 else slices[0]),
                   buf.name + "/s")

    def op(self, eng, fn, reads=(), writes=(), inc=None, milestone=True, geom="f"):
        if inc is None:
            inc = 16 if eng in ("qs", "qg") else 1
        deps = []
        for b in reads:
            if b.w is not None:
                deps.append(b.w)
        for b in writes:
            if b.w is not None:
                deps.append(b.w)
            deps.extend(b.r)
        stream = self.stream_of[eng]
        need = {}
        wm = self.wm[stream]
        for (k, v) in deps:
            if eng == "pe" and k == "pe0":
                continue
            if k == "pe0" and v < self.cnt["pe0"]:
                v = self.cnt["pe0"]
            if wm.get(k, 0) >= v:
                continue
            if need.get(k, 0) < v:
                need[k] = v
        if eng == "pe" and geom != self.pe_geom:
            assert self.pe_pending == 0, "geometry switch after non-milestone PE op"
            v = self.cnt["pe0"]
            if v > 0 and wm.get("pe0", 0) < v and need.get("pe0", 0) < v:
                need["pe0"] = v
            self.pe_geom = geom
        for k, v in need.items():
            wm[k] = v
        semkey = self.semkeys[eng][self.rr[eng]]
        if eng != "pe" and self.cnt[semkey] > 0:
            v = self.cnt[semkey]
            if need.get(semkey, 0) < v:
                need[semkey] = v
                wm[semkey] = max(wm.get(semkey, 0), v)
        if eng == "pe" and not milestone:
            tok = (semkey, self.cnt[semkey] + 1)
            do_inc = 0
            self.pe_pending += 1
        else:
            self.rr[eng] = (self.rr[eng] + 1) % self.nsem[eng]
            self.cnt[semkey] += inc
            tok = (semkey, self.cnt[semkey])
            do_inc = inc
            if eng == "pe":
                self.pe_pending = 0
        needl = sorted(need.items())

        def thunk(e, sems, fn=fn, needl=needl, semkey=semkey, do_inc=do_inc):
            for k, v in needl:
                e.wait_ge(sems[k], v)
            inst = fn(e)
            if do_inc:
                inst.then_inc(sems[semkey], do_inc)

        self.thunks[stream].append(thunk)
        self.n_ops += 1
        for b in writes:
            b.w = tok
            b.r = []
        for b in reads:
            b.r.append(tok)
        return tok

    def barrier(self):
        self.pe_geom = None
        snapshot = dict(self.cnt)
        for e in ("pe", "act", "dve", "pg", "qs"):
            wm = self.wm[e]
            needl = [(k, v) for k, v in sorted(snapshot.items())
                     if v > 0 and wm.get(k, 0) < v]
            for k, v in needl:
                wm[k] = v
            if not needl:
                continue

            def thunk(e_obj, sems, needl=needl):
                for k, v in needl:
                    e_obj.wait_ge(sems[k], v)

            self.thunks[e].append(thunk)

    def finalize(self):
        nc = self.nc
        sems = {}
        for e in ENGS:
            for k in self.semkeys[e]:
                sems[k] = self.stack.enter_context(nc.semaphore(k))
        with nc.Block() as block:
            @block.tensor
            def _(t):
                for th in self.thunks["pe"]:
                    th(t, sems)

            @block.scalar
            def _(s):
                for th in self.thunks["act"]:
                    th(s, sems)

            @block.vector
            def _(v):
                for th in self.thunks["dve"]:
                    th(v, sems)

            @block.gpsimd
            def _(gg):
                for th in self.thunks["pg"]:
                    th(gg, sems)

            @block.sync
            def _(s):
                for th in self.thunks["qs"]:
                    th(s, sems)


def _quant_chunk(p, eng, x, q_out, b, off, W):
    NB = W // 16
    x3 = x.ap[:, off:off + W].rearrange("p (nb k) -> p nb k", k=16)
    p.op("dve", lambda e, o=b["bm"].ap, i=x3, NB=NB: e.tensor_reduce(
        o[:, 0:NB], i, AX.X, ALU.max, apply_absolute_value=True),
        reads=[x], writes=[b["bm"]])
    p.op(eng, lambda e, o=b["bmc"].ap, i=b["bm"].ap, NB=NB: e.tensor_scalar(
        o[:, 0:NB], i[:, 0:NB], 6e-12, None, ALU.max),
        reads=[b["bm"]], writes=[b["bmc"]])
    p.op("dve", lambda e, o=b["rb"].ap, i=b["bmc"].ap, NB=NB: e.reciprocal(o[:, 0:NB], i[:, 0:NB]),
         reads=[b["bmc"]], writes=[b["rb"]])
    p.op(eng, lambda e, o=b["rb6"].ap, i=b["rb"].ap, NB=NB: e.tensor_scalar(
        o[:, 0:NB], i[:, 0:NB], 6.0, None, ALU.mult),
        reads=[b["rb"]], writes=[b["rb6"]])
    p.op(eng, lambda e, o=b["sb"].ap, i=b["bmc"].ap, NB=NB: e.tensor_scalar(
        o[:, 0:NB], i[:, 0:NB], 1.0 / 6.0, None, ALU.mult),
        reads=[b["bmc"]], writes=[b["sb"]])
    rb6b = b["rb6"].ap[:, 0:NB].unsqueeze(2).broadcast_to([128, NB, 16])
    p.op(eng, lambda e, o=b["xn"].ap, i0=rb6b, i1=x3, W=W: e.tensor_tensor(
        o[:, 0:W].rearrange("p (nb k) -> p nb k", k=16), i0, i1, ALU.mult),
        reads=[b["rb6"], x], writes=[b["xn"]])
    p.op(eng, lambda e, o=b["c2"].ap, i=b["xn"].ap, W=W: e.tensor_scalar(
        o[:, 0:W], i[:, 0:W], 2.0, -2.0, ALU.min, ALU.max),
        reads=[b["xn"]], writes=[b["c2"]])
    p.op(eng, lambda e, o=b["c4"].ap, i=b["xn"].ap, W=W: e.tensor_scalar(
        o[:, 0:W], i[:, 0:W], 4.0, -4.0, ALU.min, ALU.max),
        reads=[b["xn"]], writes=[b["c4"]])
    p.op(eng, lambda e, o=b["bst"].ap, i=b["c2"].ap, W=W: e.tensor_scalar(
        o[:, 0:W], i[:, 0:W], 96.0, None, ALU.add),
        reads=[b["c2"]], writes=[b["bst"]])
    p.op(eng, lambda e, o=b["u1"].ap, i=b["bst"].ap, W=W: e.tensor_copy(
        o[:, 0:W], i[:, 0:W].bitcast(U16)),
        reads=[b["bst"]], writes=[b["u1"]])
    if eng == "dve":
        p.op(eng, lambda e, o=b["bst"].ap, i0=b["c2"].ap, i1=b["c4"].ap, W=W: e.scalar_tensor_tensor(
            o[:, 0:W], i0[:, 0:W], 192.0, i1[:, 0:W], ALU.add, ALU.subtract),
            reads=[b["c2"], b["c4"]], writes=[b["bst"]])
    else:
        p.op(eng, lambda e, o=b["t0"].ap, i=b["c2"].ap, W=W: e.tensor_scalar(
            o[:, 0:W], i[:, 0:W], 192.0, None, ALU.add),
            reads=[b["c2"]], writes=[b["t0"]])
        p.op(eng, lambda e, o=b["bst"].ap, i0=b["t0"].ap, i1=b["c4"].ap, W=W: e.tensor_tensor(
            o[:, 0:W], i0[:, 0:W], i1[:, 0:W], ALU.subtract),
            reads=[b["t0"], b["c4"]], writes=[b["bst"]])
    p.op(eng, lambda e, o=b["u2"].ap, i=b["bst"].ap, W=W: e.tensor_copy(
        o[:, 0:W], i[:, 0:W].bitcast(U16)),
        reads=[b["bst"]], writes=[b["u2"]])
    p.op(eng, lambda e, o=b["c2"].ap, i0=b["u1"].ap, i1=b["u2"].ap, W=W: e.tensor_tensor(
        o[:, 0:W], i0[:, 0:W].bitcast(BF16), i1[:, 0:W].bitcast(BF16), ALU.subtract),
        reads=[b["u1"], b["u2"]], writes=[b["c2"]])
    if eng == "dve":
        p.op(eng, lambda e, o=b["bst"].ap, i0=b["xn"].ap, i1=b["c4"].ap, W=W: e.scalar_tensor_tensor(
            o[:, 0:W], i0[:, 0:W], 384.0, i1[:, 0:W], ALU.add, ALU.subtract),
            reads=[b["xn"], b["c4"]], writes=[b["bst"]])
    else:
        p.op(eng, lambda e, o=b["t0"].ap, i=b["xn"].ap, W=W: e.tensor_scalar(
            o[:, 0:W], i[:, 0:W], 384.0, None, ALU.add),
            reads=[b["xn"]], writes=[b["t0"]])
        p.op(eng, lambda e, o=b["bst"].ap, i0=b["t0"].ap, i1=b["c4"].ap, W=W: e.tensor_tensor(
            o[:, 0:W], i0[:, 0:W], i1[:, 0:W], ALU.subtract),
            reads=[b["t0"], b["c4"]], writes=[b["bst"]])
    p.op(eng, lambda e, o=b["u2"].ap, i=b["bst"].ap, W=W: e.tensor_copy(
        o[:, 0:W], i[:, 0:W].bitcast(U16)),
        reads=[b["bst"]], writes=[b["u2"]])
    p.op(eng, lambda e, o=b["xn"].ap, i0=b["c2"].ap, i1=b["u2"].ap, W=W: e.tensor_tensor(
        o[:, 0:W], i0[:, 0:W], i1[:, 0:W].bitcast(BF16), ALU.add),
        reads=[b["c2"], b["u2"]], writes=[b["xn"]])
    sbb = b["sb"].ap[:, 0:NB].unsqueeze(2).broadcast_to([128, NB, 16])
    qs_ap = q_out.ap[:, off:off + W]
    if eng == "dve":
        p.op(eng, lambda e, o=qs_ap, i0=b["xn"].ap, i1=sbb, W=W, NB=NB: e.scalar_tensor_tensor(
            o.rearrange("p (nb k) -> p nb k", k=16),
            i0[:, 0:W].rearrange("p (nb k) -> p nb k", k=16), -288.0, i1, ALU.add, ALU.mult),
            reads=[b["xn"], b["sb"]], writes=[q_out])
    else:
        p.op(eng, lambda e, o=b["t0"].ap, i=b["xn"].ap, W=W: e.tensor_scalar(
            o[:, 0:W], i[:, 0:W], -288.0, None, ALU.add),
            reads=[b["xn"]], writes=[b["t0"]])
        p.op(eng, lambda e, o=qs_ap, i0=b["t0"].ap, i1=sbb, W=W, NB=NB: e.tensor_tensor(
            o.rearrange("p (nb k) -> p nb k", k=16),
            i0[:, 0:W].rearrange("p (nb k) -> p nb k", k=16), i1, ALU.mult),
            reads=[b["t0"], b["sb"]], writes=[q_out])


def emit_quant(p, x, q_out, b, F):
    sets = b["sets"] if "sets" in b else [b]
    qf = sets[0]["qf"]
    for off in range(0, F, qf):
        W = min(qf, F - off)
        _quant_chunk(p, "dve", x, q_out, sets[0], off, W)


def mk_quant_bufs(p, tag, stack, qf=QF, with_t0=True):
    NB = qf // 16
    return {
        "qf": qf,
        "bm": p.sbuf(f"{tag}_bm", [128, NB], F32, stack),
        "bmc": p.sbuf(f"{tag}_bmc", [128, NB], F32, stack),
        "rb": p.sbuf(f"{tag}_rb", [128, NB], F32, stack),
        "rb6": p.sbuf(f"{tag}_rb6", [128, NB], F32, stack),
        "sb": p.sbuf(f"{tag}_sb", [128, NB], F32, stack),
        "xn": p.sbuf(f"{tag}_xn", [128, qf], F32, stack),
        "c2": p.sbuf(f"{tag}_c2", [128, qf], F32, stack),
        "c4": p.sbuf(f"{tag}_c4", [128, qf], F32, stack),
        "bst": p.sbuf(f"{tag}_bst", [128, qf], BF16, stack),
        "u1": p.sbuf(f"{tag}_u1", [128, qf], U16, stack),
        "u2": p.sbuf(f"{tag}_u2", [128, qf], U16, stack),
        **({"t0": p.sbuf(f"{tag}_t0", [128, qf], F32, stack)} if with_t0 else {}),
    }


def emit_ln_stats(p, x, mu, rstd, scr, sumx, ssq, mu2, epsb=None):
    p.op("dve", lambda e, o=scr.ap, i=x.ap, a=ssq.ap: e.scalar_tensor_tensor(
        o[:, :], i[:, :], 1.0, i[:, :], ALU.mult, ALU.mult, accum_out=a[:, :]),
        reads=[x], writes=[scr, ssq])
    p.op("dve", lambda e, o=sumx.ap, i=x.ap: e.reduce_sum(o[:, :], i[:, :], AX.X),
         reads=[x], writes=[sumx])
    p.op("dve", lambda e, o=mu.ap, i=sumx.ap: e.tensor_scalar(o[:, :], i[:, :], 1.0 / D, None, ALU.mult),
         reads=[sumx], writes=[mu])
    p.op("dve", lambda e, o=mu2.ap, i=ssq.ap: e.tensor_scalar(o[:, :], i[:, :], 1.0 / D, None, ALU.mult),
         reads=[ssq], writes=[mu2])
    p.op("dve", lambda e, o=mu2.ap, i0=mu.ap, s=mu.ap, i1=mu2.ap: e.scalar_tensor_tensor(
        o[:, :], i0[:, :], s[:, 0:1], i1[:, :], ALU.mult, ALU.subtract),
        reads=[mu, mu2], writes=[mu2])
    p.op("act", lambda e, o=mu2.ap, i=mu2.ap, b=epsb.ap: e.activation(
        out=o[:, :], in_=i[:, :], func=AF.Ln, scale=-1.0, bias=b[:, 0:1]),
        reads=[mu2, epsb], writes=[mu2])
    p.op("act", lambda e, o=rstd.ap, i=mu2.ap: e.activation(
        out=o[:, :], in_=i[:, :], func=AF.Exp, scale=-0.5),
        reads=[mu2], writes=[rstd])


def build_program(stage="full"):
    nc = bass.Bass("TRN2", target_bir_lowering=False, debug=False)
    stack = ExitStack()
    p = Prog(nc, stack)
    g = {}

    for name, shape, dtype in (
            ("x0", [T, D], F32), ("qkvwT", [D, 3 * D], F32), ("qkvb", [128, 16], F32),
            ("vb", [128, 8], F32), ("projwT", [D, D], F32), ("projb_row", [1, D], F32),
            ("bW", [3, D, D], F32), ("bWp", [D, D], F32),
            ("wg", [DFF, D], F32), ("wu", [DFF, D], F32), ("wd", [D, DFF], F32),
            ("pos_rep", [64, T], F32), ("inv_in", [64, 1], F32),
            ("sgnpat", [64, 1], F32), ("ones_in", [128, 64], F32), ("perm_in", [128, 128], F32)):
        g[name] = p.dram(name, shape, dtype, kind="ExternalInput")
    g["out"] = p.dram("out", [TO, D], F32, kind="ExternalOutput")
    if stage != "full":
        g["dbg"] = p.dram("dbg", [T, D], F32, kind="ExternalOutput")
        p.g_dbg = g["dbg"]
    if stage == "p0":
        for nm, shp in (("dbg_freq", [64, T]), ("dbg_r", [64, T]), ("dbg_rc", [64, T]), ("dbg_inv", [64, 1])):
            g[nm] = p.dram(nm, shp, F32, kind="ExternalOutput")

    dbg_dram = set(os.environ.get("K_DBGDRAM", "").split(","))
    for name, shape, dtype in (
            ("x1d", [T, D], F32), ("x2d", [T, D], F32), ("x3d", [TO, D], F32),
            ("Cd", [128, T], F32), ("Sd", [128, T], F32),
            ("lnTd_a", [NT, 128, D], BF16), ("lnTd_b", [NT, 128, D], BF16),
            ("QTd_a", [NDT, 128, T], BF16), ("QTd_b", [NDT, 128, TO], BF16),
            ("Vd_a", [NT, 128, D], BF16), ("Vd_b", [NT, 128, D], BF16),
            ("wgTd", [128, NDT, NIT, 128], BF16), ("wuTd", [128, NDT, NIT, 128], BF16),
            ("wdTd", [128, NIT, NDT, 128], BF16),
            ("xqTd_1", [NT, 128, D], BF16), ("hqTd_1", [NT, 128, DFF], BF16),
            ("xqTd_2", [NTO, 128, D], BF16), ("hqTd_2", [NTO, 128, DFF], BF16),
            ("bWbf", [3, D, D], BF16), ("bWpbf", [D, D], BF16)):
        g[name] = p.dram(name, shape, dtype,
                         kind="ExternalOutput" if name in dbg_dram else "Internal")

    with stack:
        g["stats_sb"] = p.sbuf("stats", [128, 4 * NT * 2], F32)
        p.g_stats = g["stats_sb"]

        def stats_of(grp, i):
            base = (grp * NT + i) * 2
            return (p.sub(g["stats_sb"], slice(None), slice(base, base + 1)),
                    p.sub(g["stats_sb"], slice(None), slice(base + 1, base + 2)))
        g["stats"] = {grp: [stats_of(grp, i) for i in range(NT)] for grp in range(4)}

        g["ones64"] = p.sbuf("ones64", [128, 64], BF16)
        g["perm"] = p.sbuf("perm", [128, 128], BF16)
        g["biasrep"] = p.sbuf("biasrep", [128, D], F32)
        g["vb_sb"] = p.sbuf("vb_sb", [128, 8], F32)
        g["qkvb_sb"] = p.sbuf("qkvb_sb", [128, 16], F32)
        g["epsb"] = p.sbuf("epsb", [128, 1], F32)
        p.op("pool", lambda e, o=g["epsb"].ap: e.memset(o[:, :], EPS), writes=[g["epsb"]])
        g["c96"] = p.sbuf("c96", [128, 1], F32)
        p.op("pool", lambda e, o=g["c96"].ap: e.memset(o[:, :], 96.0), writes=[g["c96"]])

        p.g96 = g["c96"]
        build_phase0(p, g)
        p.barrier()

        if stage == "p0":
            dump_p0(p, g)
            p.finalize()
            return nc, p

        if stage == "ffnx":
            g["x1in"] = p.dram("x1in", [T, D], F32, kind="ExternalInput")
            p.op("act", lambda e: load_act_set(e, SET_LNEXP))
            with ExitStack() as ph:
                fx = p.sbuf("fx_x", [128, D], F32, ph)
                fscr = p.sbuf("fx_scr", [128, D], F32, ph)
                fsm = [p.sbuf(f"fx_sm{i}", [128, 1], F32, ph) for i in range(3)]
                for s in range(NT):
                    p.op("qg", lambda e, o=fx.ap, i=g["x1in"].ap, s=s: e.dma_start(
                        out=o[:, :], in_=i[s * 128:(s + 1) * 128, :]),
                        reads=[g["x1in"]], writes=[fx])
                    mu2, rstd2 = g["stats"][1][s]
                    emit_ln_stats(p, fx, mu2, rstd2, fscr, fsm[0], fsm[1], fsm[2], g["epsb"])
                p.barrier()
            p.op("act", lambda e: load_act_set(e, SET_SILU))
            build_ffn(p, g, g["x1in"], g["x2d"], g["stats"][1], g["xqTd_1"], g["hqTd_1"], NT)
            if not os.environ.get("K_QDBG"):
                finish_debug(p, g, g["x2d"], T)
            else:
                p.barrier()
            p.finalize()
            return nc, p

        p.op("act", lambda e: load_act_set(e, SET_LNEXP))
        build_attn(p, g, "a", g["x0"], g["x1d"], g["stats"][0], g["stats"][1],
                   g["lnTd_a"], g["QTd_a"], g["Vd_a"], NQC)
        if stage == "attn_a":
            finish_debug(p, g, g["x1d"], T)
            p.finalize()
            return nc, p
        p.barrier()

        p.op("act", lambda e: load_act_set(e, SET_SILU))
        build_ffn(p, g, g["x1d"], g["x2d"], g["stats"][1], g["xqTd_1"], g["hqTd_1"], NT)
        if stage == "ffn1":
            finish_debug(p, g, g["x2d"], T)
            p.finalize()
            return nc, p
        p.barrier()

        p.op("act", lambda e: load_act_set(e, SET_LNEXP))
        build_attn(p, g, "b", g["x2d"], g["x3d"], g["stats"][2], g["stats"][3],
                   g["lnTd_b"], g["QTd_b"], g["Vd_b"], NQC_OWN)
        if stage == "attn_b":
            finish_debug(p, g, g["x3d"], TO)
            p.finalize()
            return nc, p
        p.barrier()

        p.op("act", lambda e: load_act_set(e, SET_SILU))
        build_ffn(p, g, g["x3d"], None, g["stats"][3], g["xqTd_2"], g["hqTd_2"], NTO,
                  final_out=g["out"])
        p.barrier()
        p.finalize()
    return nc, p


def dump_p0(p, g):
    nc = p.nc
    with ExitStack() as ph:
        t32 = p.sbuf("d_t32", [128, T], F32, ph)
        tb = p.sbuf("d_tb", [128, D], BF16, ph)
        dbgv = g["dbg"].ap.rearrange("(a p r) d -> a p (r d)", p=128, r=4)
        for ix, srcb in enumerate((g["Cd"], g["Sd"])):
            p.op("qs", lambda e, o=t32.ap, i=srcb.ap: e.dma_start(out=o[:, :], in_=i[:, :]),
                 reads=[srcb], writes=[t32])
            p.op("qs", lambda e, o=dbgv, i=t32.ap, ix=ix: e.dma_start(out=o[ix], in_=i[:, :]),
                 reads=[t32], writes=[g["dbg"]])
        tb3 = tb.ap.rearrange("c (dk p) -> c dk p", p=128)
        for ix, (srcb, apsel, o_ap) in enumerate((
                (g["wgTd"], g["wgTd"].ap[:, :, 0, :], tb3),
                (g["bWbf"], g["bWbf"].ap[0, 0:128, :], tb.ap[:, :]),
                (g["wdTd"], g["wdTd"].ap[:, 0:NDT, 0, :], tb3))):
            p.op("qs", lambda e, o=o_ap, i=apsel: e.dma_start(out=o, in_=i),
                 reads=[srcb], writes=[tb])
            p.op("dve", lambda e, o=t32.ap, i=tb.ap: e.tensor_copy(o[:, 0:D], i[:, :]),
                 reads=[tb], writes=[t32])
            p.op("qs", lambda e, o=g["dbg"].ap, i=t32.ap, ix=ix: e.dma_start(
                out=o[2048 + ix * 128: 2048 + (ix + 1) * 128, :], in_=i[:, 0:D]),
                reads=[t32], writes=[g["dbg"]])
        p.barrier()


def finish_debug(p, g, src, rows):
    with ExitStack() as ph:
        tmp = p.sbuf("dbg_tmp", [128, D], F32, ph)
        for i in range(rows // 128):
            p.op("qs", lambda e, o=tmp.ap, i_=src.ap, i2=i: e.dma_start(
                out=o[:, :], in_=i_[i2 * 128:(i2 + 1) * 128, :]), reads=[src], writes=[tmp])
            p.op("qs", lambda e, o=g["dbg"].ap, i_=tmp.ap, i2=i: e.dma_start(
                out=o[i2 * 128:(i2 + 1) * 128, :], in_=i_[:, :]), reads=[tmp], writes=[g["dbg"]])
        p.barrier()


def build_phase0(p, g):
    nc = p.nc
    with ExitStack() as ph:
        tmpf = p.sbuf("c_tmpf", [128, 128], F32, ph)
        p.op("qs", lambda e, o=tmpf.ap, i=g["ones_in"].ap: e.dma_start(out=o[:, 0:64], in_=i[:, :]),
             reads=[g["ones_in"]], writes=[tmpf])
        p.op("dve", lambda e, o=g["ones64"].ap, i=tmpf.ap: e.tensor_copy(o[:, :], i[:, 0:64]),
             reads=[tmpf], writes=[g["ones64"]])
        p.op("qs", lambda e, o=tmpf.ap, i=g["perm_in"].ap: e.dma_start(out=o[:, :], in_=i[:, :]),
             reads=[g["perm_in"]], writes=[tmpf])
        p.op("dve", lambda e, o=g["perm"].ap, i=tmpf.ap: e.tensor_copy(o[:, :], i[:, :]),
             reads=[tmpf], writes=[g["perm"]])
        p.op("qs", lambda e, o=g["vb_sb"].ap, i=g["vb"].ap: e.dma_start(out=o[:, :], in_=i[:, :]),
             reads=[g["vb"]], writes=[g["vb_sb"]])
        p.op("qs", lambda e, o=g["qkvb_sb"].ap, i=g["qkvb"].ap: e.dma_start(out=o[:, :], in_=i[:, :]),
             reads=[g["qkvb"]], writes=[g["qkvb_sb"]])

        onesrow = p.sbuf("c_onesrow", [1, 128], BF16, ph)
        pbrow = p.sbuf("c_pbrow", [1, D], BF16, ph)
        pbrow32 = p.sbuf("c_pbrow32", [1, D], F32, ph)
        p.op("pool", lambda e, o=onesrow.ap: e.memset(o[:, :], 1.0), writes=[onesrow])
        p.op("qs", lambda e, o=pbrow32.ap, i=g["projb_row"].ap: e.dma_start(out=o[:, :], in_=i[:, :]),
             reads=[g["projb_row"]], writes=[pbrow32])
        p.op("dve", lambda e, o=pbrow.ap, i=pbrow32.ap: e.tensor_copy(o[:, :], i[:, :]),
             reads=[pbrow32], writes=[pbrow])
        psb = p.psum("c_ps", [128, 512], ph)
        for ch in range(2):
            p.op("pe", lambda e, o=psb.ap, l=onesrow.ap, r=pbrow.ap, c=ch: e.matmul(
                o[:, :], l[:, :], r[:, c * 512:(c + 1) * 512], start=True, stop=True),
                reads=[onesrow, pbrow], writes=[psb], geom="k1")
            p.op("dve", lambda e, o=g["biasrep"].ap, i=psb.ap, c=ch: e.tensor_copy(
                o[:, c * 512:(c + 1) * 512], i[:, :]),
                reads=[psb], writes=[g["biasrep"]])
        p.barrier()

    with ExitStack() as ph:
        posb = p.sbuf("c_pos", [64, T], F32, ph)
        sgnb = p.sbuf("c_sgn", [64, 1], F32, ph)
        inv = p.sbuf("c_inv", [64, 1], F32, ph)
        freq = p.sbuf("c_freq", [64, T], F32, ph)
        freq2 = p.sbuf("c_freq2", [64, T], F32, ph)
        kk = p.sbuf("c_kk", [64, T], I32, ph)
        s1 = p.sbuf("c_s1", [64, T], F32, ph)
        s2 = p.sbuf("c_s2", [64, T], F32, ph)
        tabC = p.sbuf("c_tabC", [128, T], F32, ph)
        tabS = p.sbuf("c_tabS", [128, T], F32, ph)
        for src, dst in ((g["pos_rep"], posb), (g["inv_in"], inv), (g["sgnpat"], sgnb)):
            p.op("qs", lambda e, o=dst.ap, i=src.ap: e.dma_start(out=o[:, :], in_=i[:, :]),
                 reads=[src], writes=[dst])
        p.op("act", lambda e: load_act_set(e, SET_TRIG))
        p.op("dve", lambda e, o=freq.ap, i=posb.ap, s=inv.ap: e.tensor_scalar(
            o[:, :], i[:, :], s[:, 0:1], None, ALU.mult),
            reads=[posb, inv], writes=[freq])
        p.op("dve", lambda e, o=kk.ap, i=freq.ap: e.tensor_scalar(
            o[:, :], i[:, :], 1.0 / (2 * math.pi), None, ALU.mult),
            reads=[freq], writes=[kk])
        TWO_PI = 2 * math.pi
        c1 = float(np.float32(TWO_PI))
        c2 = float(np.float32(TWO_PI - c1))
        c3 = float(np.float32(TWO_PI - c1 - c2))
        p.op("dve", lambda e, o=s1.ap, i0=kk.ap, i1=freq.ap: e.scalar_tensor_tensor(
            o[:, :], i0[:, :], -c1, i1[:, :], ALU.mult, ALU.add),
            reads=[kk, freq], writes=[s1])
        p.op("dve", lambda e, o=s2.ap, i0=kk.ap, i1=s1.ap: e.scalar_tensor_tensor(
            o[:, :], i0[:, :], -c2, i1[:, :], ALU.mult, ALU.add),
            reads=[kk, s1], writes=[s2])
        p.op("dve", lambda e, o=s1.ap, i0=kk.ap, i1=s2.ap: e.scalar_tensor_tensor(
            o[:, :], i0[:, :], -c3, i1[:, :], ALU.mult, ALU.add),
            reads=[kk, s2], writes=[s1])
        p.op("dve", lambda e, o=freq2.ap, i=s1.ap: e.tensor_scalar(
            o[:, :], i[:, :], math.pi, 2 * math.pi, ALU.is_gt, ALU.mult),
            reads=[s1], writes=[freq2])
        p.op("dve", lambda e, o=s1.ap, i0=s1.ap, i1=freq2.ap: e.tensor_tensor(
            o[:, :], i0[:, :], i1[:, :], ALU.subtract),
            reads=[s1, freq2], writes=[s1])
        p.op("dve", lambda e, o=freq2.ap, i=s1.ap: e.tensor_scalar(
            o[:, :], i[:, :], -math.pi, 2 * math.pi, ALU.is_lt, ALU.mult),
            reads=[s1], writes=[freq2])
        p.op("dve", lambda e, o=s1.ap, i0=s1.ap, i1=freq2.ap: e.tensor_tensor(
            o[:, :], i0[:, :], i1[:, :], ALU.add),
            reads=[s1, freq2], writes=[s1])
        p.op("act", lambda e, o=tabS.ap, i=s1.ap: e.activation(out=o[0:64, :], in_=i[:, :], func=AF.Sin),
             reads=[s1], writes=[tabS])
        p.op("dve", lambda e, o=s2.ap, i=s1.ap: e.tensor_scalar(o[:, :], i[:, :], math.pi / 2, None, ALU.add),
             reads=[s1], writes=[s2])
        p.op("dve", lambda e, o=freq2.ap, i=s2.ap: e.tensor_scalar(o[:, :], i[:, :], math.pi, 2 * math.pi, ALU.is_gt, ALU.mult),
             reads=[s2], writes=[freq2])
        p.op("dve", lambda e, o=s2.ap, i0=s2.ap, i1=freq2.ap: e.tensor_tensor(o[:, :], i0[:, :], i1[:, :], ALU.subtract),
             reads=[s2, freq2], writes=[s2])
        p.op("act", lambda e, o=tabC.ap, i=s2.ap: e.activation(out=o[0:64, :], in_=i[:, :], func=AF.Sin),
             reads=[s2], writes=[tabC])
        p.op("dve", lambda e, o=tabS.ap, i=tabS.ap, s=sgnb.ap: e.tensor_scalar(
            o[0:64, :], i[0:64, :], s[:, 0:1], None, ALU.mult),
            reads=[tabS, sgnb], writes=[tabS])
        p.op("dve", lambda e, o=tabC.ap, i=tabC.ap: e.tensor_copy(o[64:128, :], i[0:64, :]),
             reads=[tabC], writes=[tabC])
        p.op("dve", lambda e, o=tabS.ap, i=tabS.ap: e.tensor_copy(o[64:128, :], i[0:64, :]),
             reads=[tabS], writes=[tabS])
        if "dbg_freq" in g:
            p.op("qs", lambda e, o=g["dbg_freq"].ap, i=freq.ap: e.dma_start(out=o[:, :], in_=i[:, :]),
                 reads=[freq], writes=[g["dbg_freq"]])
            p.op("qs", lambda e, o=g["dbg_r"].ap, i=s1.ap: e.dma_start(out=o[:, :], in_=i[:, :]),
                 reads=[s1], writes=[g["dbg_r"]])
            p.op("qs", lambda e, o=g["dbg_rc"].ap, i=s2.ap: e.dma_start(out=o[:, :], in_=i[:, :]),
                 reads=[s2], writes=[g["dbg_rc"]])
            p.op("qs", lambda e, o=g["dbg_inv"].ap, i=inv.ap: e.dma_start(out=o[:, :], in_=i[:, :]),
                 reads=[inv], writes=[g["dbg_inv"]])
        p.op("qs", lambda e, o=g["Cd"].ap, i=tabC.ap: e.dma_start(out=o[:, :], in_=i[:, :]),
             reads=[tabC], writes=[g["Cd"]])
        p.op("qs", lambda e, o=g["Sd"].ap, i=tabS.ap: e.dma_start(out=o[:, :], in_=i[:, :]),
             reads=[tabS], writes=[g["Sd"]])
        p.barrier()

    with ExitStack() as ph:
        qb = {"sets": [mk_quant_bufs(p, "wqa", ph, with_t0=False)]}
        wslab = [p.sbuf(f"c_wslab{i}", [128, DFF], F32, ph) for i in range(2)]
        wqslab = [p.sbuf(f"c_wqslab{i}", [128, DFF], BF16, ph) for i in range(2)]
        for wsrc, wdst, nslab, FW in ((g["wg"], g["wgTd"], NIT, D),
                                      (g["wu"], g["wuTd"], NIT, D),
                                      (g["wd"], g["wdTd"], NDT, DFF)):
            for s in range(nslab):
                sl, ql = wslab[s % 2], wqslab[s % 2]
                p.op("qs", lambda e, o=sl.ap, i=wsrc.ap, s=s, F=FW: e.dma_start(
                    out=o[:, 0:F], in_=i[s * 128:(s + 1) * 128, :]),
                    reads=[wsrc], writes=[sl])
                emit_quant(p, sl, ql, qb, FW)
                p.op("qs", lambda e, o=wdst.ap, i=ql.ap, s=s, F=FW: e.dma_start(
                    out=o[:, :, s, :],
                    in_=i[:, 0:F].rearrange("c (dk p) -> c dk p", p=128)),
                    reads=[ql], writes=[wdst])
        for s in range(3 * NDT):
            sl, ql = wslab[s % 2], wqslab[s % 2]
            p.op("qs", lambda e, o=sl.ap, i=g["bW"].ap, s=s: e.dma_start(
                out=o[:, 0:D], in_=i[s // NDT, (s % NDT) * 128:(s % NDT + 1) * 128, :]),
                reads=[g["bW"]], writes=[sl])
            p.op("dve", lambda e, o=ql.ap, i=sl.ap: e.tensor_copy(o[:, 0:D], i[:, 0:D]),
                 reads=[sl], writes=[ql])
            p.op("qs", lambda e, o=g["bWbf"].ap, i=ql.ap, s=s: e.dma_start(
                out=o[s // NDT, (s % NDT) * 128:(s % NDT + 1) * 128, :], in_=i[:, 0:D]),
                reads=[ql], writes=[g["bWbf"]])
        for s in range(NDT):
            sl, ql = wslab[s % 2], wqslab[s % 2]
            p.op("qs", lambda e, o=sl.ap, i=g["bWp"].ap, s=s: e.dma_start(
                out=o[:, 0:D], in_=i[s * 128:(s + 1) * 128, :]),
                reads=[g["bWp"]], writes=[sl])
            p.op("dve", lambda e, o=ql.ap, i=sl.ap: e.tensor_copy(o[:, 0:D], i[:, 0:D]),
                 reads=[sl], writes=[ql])
            p.op("qs", lambda e, o=g["bWpbf"].ap, i=ql.ap, s=s: e.dma_start(
                out=o[s * 128:(s + 1) * 128, :], in_=i[:, 0:D]),
                reads=[ql], writes=[g["bWpbf"]])
        p.barrier()


def build_attn(p, g, layer, x_src, x_out, stats_ln, stats_next, lnTd, QTd, Vd, nqc):
    nc = p.nc
    rope = layer == "a"
    with ExitStack() as outer:
        KT_sb = p.sbuf(f"{layer}_KT", [128, NDT * T], BF16, outer)
        KT_t = [[p.sub(KT_sb, slice(None), slice(dk * T + tc * QC, dk * T + (tc + 1) * QC))
                 for tc in range(NQC)] for dk in range(NDT)]

        with ExitStack() as ph:
            def S(name, shape, dtype):
                return p.sbuf(f"{layer}A_{name}", shape, dtype, ph)

            xsb = [S(f"x{i}", [128, D], F32) for i in range(2)]
            scr = S("scr", [128, D], F32)
            small = [S(f"sm{i}", [128, 1], F32) for i in range(3)]
            lnsb = [S(f"ln{i}", [128, D], BF16) for i in range(2)]
            lnTc = [S(f"lnT{i}", [128, NDT * 512], BF16) for i in range(2)]
            wq_sb = [S(f"w{dk}", [128, 3 * D], BF16) for dk in range(NDT)]
            wtmp = [S(f"wtmp{i}", [128, 3 * D], F32) for i in range(2)]
            ropeC = [S(f"ropC{i}", [128, QC], F32) for i in range(2)]
            ropeS = [S(f"ropS{i}", [128, QC], F32) for i in range(2)]
            qb_e = [S(f"qb{i}", [128, QC], BF16) for i in range(2)]
            t1_e = [S(f"t1{i}", [128, QC], F32) for i in range(2)]
            t2_e = [S(f"t2{i}", [128, QC], F32) for i in range(2)]
            qro = [S(f"qro{i}", [128, QC], BF16) for i in range(2)]
            vev = [S(f"vev{i}", [128, 512], BF16) for i in range(2)]
            pp = [p.psum(f"{layer}A_pp{i}", [128, 512], ph) for i in range(2)]
            rot = [p.psum(f"{layer}A_rot{i}", [128, 512], ph) for i in range(2)]

            for dk in range(NDT):
                wt = wtmp[dk % 2]
                if layer == "a":
                    p.op("qs", lambda e, o=wt.ap, i=g["qkvwT"].ap, dk=dk: e.dma_start(
                        out=o[:, :], in_=i[dk * 128:(dk + 1) * 128, :]),
                        reads=[g["qkvwT"]], writes=[wt])
                    p.op("dve", lambda e, o=wq_sb[dk].ap, i=wt.ap: e.tensor_copy(o[:, :], i[:, :]),
                         reads=[wt], writes=[wq_sb[dk]])
                else:
                    p.op("qs", lambda e, o=wq_sb[dk].ap, i=g["bWbf"].ap, dk=dk: e.dma_start(
                        out=o.rearrange("p (w d) -> p w d", w=3),
                        in_=i[:, dk * 128:(dk + 1) * 128, :].transpose([1, 0, 2])),
                        reads=[g["bWbf"]], writes=[wq_sb[dk]])

            lp = 0
            for tc in range(K_TC or NQC):
                for tt4 in range(4):
                    ti = tc * 4 + tt4
                    xb, lb = xsb[lp], lnsb[lp]
                    lp ^= 1
                    p.op("qs", lambda e, o=xb.ap, i=x_src.ap, ti=ti: e.dma_start(
                        out=o[:, :], in_=i[ti * 128:(ti + 1) * 128, :]),
                        reads=[x_src], writes=[xb])
                    mu, rstd = stats_ln[ti]
                    emit_ln_stats(p, xb, mu, rstd, scr, small[0], small[1], small[2], g["epsb"])
                    p.op("dve", lambda e, o=lb.ap, i=xb.ap, m=mu.ap, r=rstd.ap: e.tensor_scalar(
                        o[:, :], i[:, :], m[:, 0:1], r[:, 0:1], ALU.subtract, ALU.mult),
                        reads=[xb, mu, rstd], writes=[lb])
                    p.op("qs", lambda e, o=lnTd.ap, i=lb.ap, ti=ti: e.dma_start(
                        out=o[ti, :, :], in_=i[:, :]),
                        reads=[lb], writes=[lnTd])
                lc = lnTc[tc % 2]
                src4 = lnTd.ap.rearrange("t c (dk p) -> t c dk p", p=128)
                ov = lc.ap.rearrange("p (dk r c) -> p dk r c", dk=NDT, r=4)
                for r4 in range(4):
                    p.op("qs", lambda e, o=ov, i=src4, tc=tc, r4=r4: e.dma_start(
                        out=o[:, :, r4, :], in_=i[tc * 4 + r4], transpose=True),
                        reads=[lnTd], writes=[lc])
                if rope:
                    rc, rs = ropeC[tc % 2], ropeS[tc % 2]
                    p.op("qg", lambda e, o=rc.ap, i=g["Cd"].ap, tc=tc: e.dma_start(
                        out=o[:, :], in_=i[:, tc * QC:(tc + 1) * QC]), reads=[g["Cd"]], writes=[rc])
                    p.op("qg", lambda e, o=rs.ap, i=g["Sd"].ap, tc=tc: e.dma_start(
                        out=o[:, :], in_=i[:, tc * QC:(tc + 1) * QC]), reads=[g["Sd"]], writes=[rs])

                for which in ("q", "k"):
                    if which == "q" and tc >= nqc:
                        continue
                    for dt in range(NDT):
                        ps = pp[dt % 2]
                        col = (0 if which == "q" else D) + dt * 128
                        for dk in range(NDT):
                            p.op("pe", lambda e, o=ps.ap, l=wq_sb[dk].ap, r=lc.ap, col=col, dk=dk: e.matmul(
                                o[:, :], l[:, col:col + 128], r[:, dk * 512:(dk + 1) * 512],
                                start=(dk == 0), stop=(dk == NDT - 1)),
                                reads=[wq_sb[dk], lc], writes=[ps], milestone=(dk == NDT - 1))
                        if rope:
                            qbv = qb_e[dt % 2]
                            bias_col = (0 if which == "q" else 8) + dt
                            p.op("dve", lambda e, o=qbv.ap, i=ps.ap, b=g["qkvb_sb"].ap, c=bias_col: e.tensor_scalar(
                                o[:, :], i[:, :], b[:, c:c + 1], None, ALU.add),
                                reads=[ps, g["qkvb_sb"]], writes=[qbv])
                            ro = rot[dt % 2]
                            p.op("pe", lambda e, o=ro.ap, l=g["perm"].ap, r=qbv.ap: e.matmul(
                                o[:, :], l[:, :], r[:, :], start=True, stop=True),
                                reads=[g["perm"], qbv], writes=[ro])
                            tv1, tv2 = t1_e[dt % 2], t2_e[dt % 2]
                            rc, rs = ropeC[tc % 2], ropeS[tc % 2]
                            p.op("dve", lambda e, o=tv1.ap, i0=qbv.ap, i1=rc.ap: e.tensor_tensor(
                                o[:, :], i0[:, :], i1[:, :], ALU.mult),
                                reads=[qbv, rc], writes=[tv1])
                            p.op("dve", lambda e, o=tv2.ap, i0=ro.ap, i1=rs.ap: e.tensor_tensor(
                                o[:, :], i0[:, :], i1[:, :], ALU.mult),
                                reads=[ro, rs], writes=[tv2])
                            dst = qro[dt % 2] if which == "q" else KT_t[dt][tc]
                            p.op("dve", lambda e, o=dst.ap, i0=tv1.ap, i1=tv2.ap: e.tensor_tensor(
                                o[:, :], i0[:, :], i1[:, :], ALU.add),
                                reads=[tv1, tv2], writes=[dst])
                        else:
                            dst = qro[dt % 2] if which == "q" else KT_t[dt][tc]
                            p.op("act", lambda e, o=dst.ap, i=ps.ap: e.activation(
                                out=o[:, :], in_=i[:, :], func=AF.Copy),
                                reads=[ps], writes=[dst])
                        if which == "q":
                            dst = qro[dt % 2]
                            p.op("qs", lambda e, o=QTd.ap, i=dst.ap, dt=dt, tc=tc: e.dma_start(
                                out=o[dt, :, tc * QC:(tc + 1) * QC], in_=i[:, :]),
                                reads=[dst], writes=[QTd])

                for tt4 in range(4):
                    ti = tc * 4 + tt4
                    for vc in range(2):
                        ps = pp[vc]
                        for dk in range(NDT):
                            p.op("pe", lambda e, o=ps.ap, l=lc.ap, r=wq_sb[dk].ap, tt4=tt4, vc=vc, dk=dk: e.matmul(
                                o[:, :], l[:, dk * 512 + tt4 * 128: dk * 512 + (tt4 + 1) * 128],
                                r[:, 2 * D + vc * 512: 2 * D + (vc + 1) * 512],
                                start=(dk == 0), stop=(dk == NDT - 1)),
                                reads=[lc, wq_sb[dk]], writes=[ps], milestone=(dk == NDT - 1))
                        ve = vev[vc]
                        p.op("act", lambda e, o=ve.ap, i=ps.ap: e.activation(
                            out=o[:, :], in_=i[:, :], func=AF.Copy),
                            reads=[ps], writes=[ve])
                        p.op("qs", lambda e, o=Vd.ap, i=ve.ap, ti=ti, vc=vc: e.dma_start(
                            out=o[ti, :, vc * 512:(vc + 1) * 512], in_=i[:, :]),
                            reads=[ve], writes=[Vd])
            p.barrier()

        with ExitStack() as ph:
            def S2(name, shape, dtype):
                return p.sbuf(f"{layer}B_{name}", shape, dtype, ph)

            V_sb = S2("V", [128, NT * D], BF16)
            V_t = [p.sub(V_sb, slice(None), slice(kt * D, (kt + 1) * D)) for kt in range(NT)]
            QT_c = [S2(f"QTc{i}", [128, NDT * QC], BF16) for i in range(2)]
            PT4 = S2("PT", [128, 4096], BF16)
            PTt = [p.sub(PT4, slice(None), slice(i * 1024, (i + 1) * 1024)) for i in range(4)]
            rdrep = S2("rdrep", [128, 512], F32)
            attn_sb = S2("attn", [128, NPAIR * QC], BF16)
            attn_t = [p.sub(attn_sb, slice(None), slice(pr * QC, (pr + 1) * QC)) for pr in range(NPAIR)]
            pw_sb = [S2(f"pw{i}", [128, D], BF16) for i in range(NDT)]
            pwt = [S2("pwt0", [128, D], F32)] if layer == "a" else None
            xprev = [S2(f"xp{i}", [128, D], F32) for i in range(2)]
            xnew = [S2(f"xo{i}", [128, D], F32) for i in range(2)]
            scrB = S2("scrB", [128, D], F32)
            ln3t = scrB if layer == "a" else S2("ln3t", [128, D], F32)
            smallB = [S2(f"smB{i}", [128, 1], F32) for i in range(3)]
            SS2 = p.psum(f"{layer}B_SS", [128, 2048], ph)
            SSh = [[p.sub(SS2, slice(None), slice(j * 1024 + hh * 512, j * 1024 + (hh + 1) * 512))
                    for hh in range(2)] for j in range(2)]
            PVh = [p.psum(f"{layer}B_PV{hh}", [128, 512], ph) for hh in range(2)]
            DDh = [p.psum(f"{layer}B_DD{hh}", [128, 512], ph) for hh in range(2)]
            PR = SSh[0]

            for kt in range(NT):
                p.op("qs" if kt % 2 == 0 else "qg",
                     lambda e, o=V_t[kt].ap, i=Vd.ap, kt=kt: e.dma_start(out=o[:, :], in_=i[kt, :, :]),
                     reads=[Vd], writes=[V_t[kt]])
            for dt in range(NDT):
                if layer == "a":
                    wt = pwt[0]
                    p.op("qg", lambda e, o=wt.ap, i=g["projwT"].ap, dt=dt: e.dma_start(
                        out=o[:, :], in_=i[dt * 128:(dt + 1) * 128, :]),
                        reads=[g["projwT"]], writes=[wt])
                    p.op("dve", lambda e, o=pw_sb[dt].ap, i=wt.ap: e.tensor_copy(o[:, :], i[:, :]),
                         reads=[wt], writes=[pw_sb[dt]])
                else:
                    p.op("qg", lambda e, o=pw_sb[dt].ap, i=g["bWpbf"].ap, dt=dt: e.dma_start(
                        out=o[:, :], in_=i[dt * 128:(dt + 1) * 128, :]),
                        reads=[g["bWpbf"]], writes=[pw_sb[dt]])

            for qc in range(K_QC or nqc):
                qtc = QT_c[qc % 2]
                p.op("qs", lambda e, o=qtc.ap, i=QTd.ap, qc=qc: e.dma_start(
                    out=o.rearrange("p (dt c) -> p dt c", dt=NDT),
                    in_=i[:, :, qc * QC:(qc + 1) * QC].transpose([1, 0, 2])),
                    reads=[QTd], writes=[qtc])
                for pr0 in range(K_PAIRS or NPAIR):
                    pr = pr0 + K_PRBASE
                    nkt = K_KT or NT
                    skips0 = set(K_SKIP.split(",")) if (K_SKIP and (pr >= 1 or K_PRBASE)) else set()
                    if "all" in skips0:
                        continue
                    def emit_S2(kt):
                        if "s" in skips0:
                            return
                        n2 = 2 if kt + 1 < nkt else 1
                        for j in range(n2):
                            for hh in range(2):
                                o0 = 64 * hh
                                sshb = SSh[(kt + j) % 2][hh]
                                p.op("pe", lambda e, o=sshb.ap, l=KT_sb.ap, r=qtc.ap, pr=pr, kt=kt + j, o0=o0: e.matmul(
                                    o[:, :],
                                    l[o0:o0 + 64, pr * T + kt * 128: pr * T + (kt + 1) * 128],
                                    r[o0:o0 + 64, pr * QC:(pr + 1) * QC],
                                    start=True, stop=True),
                                    reads=[KT_sb, qtc], writes=[sshb], milestone=(hh == 1), geom="r")

                    def emit_exp2(kt):
                        if "s" in skips0 or "exp" in skips0:
                            return
                        n2 = 2 if kt + 1 < nkt else 1
                        base = (kt % 4) * 1024
                        p.op("act", lambda e, o=PT4.ap, i=SS2.ap, n2=n2, base=base: e.activation(
                            out=o[:, base:base + n2 * 1024], in_=i[:, 0:n2 * 1024],
                            func=AF.Exp, scale=ATT_SCALE),
                            reads=[SSh[0][0], SSh[0][1]] + ([SSh[1][0], SSh[1][1]] if n2 == 2 else []),
                            writes=[PTt[kt % 4]] + ([PTt[(kt + 1) % 4]] if n2 == 2 else []))

                    def emit_PVDD(kt):
                        if "pvdd" in skips0:
                            return
                        pt = PTt[kt % 4]
                        for hh in range(2):
                            tp = (0, 64 * hh)
                            st_ = (kt == 0)
                            sp_ = (kt == nkt - 1)
                            vpr = 0 if "vclamp" in skips0 else pr
                            if "pv" not in skips0:
                                p.op("pe", lambda e, o=PVh[hh].ap, l=V_t[kt].ap, r=pt.ap, pr=vpr, kt=kt, hh=hh, tp=tp, st_=st_, sp_=sp_: e.matmul(
                                    o[64 * hh:64 * (hh + 1), :],
                                    l[:, (2 * pr + hh) * 64:(2 * pr + hh + 1) * 64],
                                    r[:, hh * 512:(hh + 1) * 512],
                                    start=st_, stop=sp_, tile_position=tp),
                                    reads=[V_t[kt], pt], writes=[PVh[hh]], milestone=sp_, geom="c")
                            if "dd" not in skips0:
                                p.op("pe", lambda e, o=DDh[hh].ap, l=g["ones64"].ap, r=pt.ap, kt=kt, hh=hh, tp=tp, st_=st_, sp_=sp_: e.matmul(
                                    o[64 * hh:64 * (hh + 1), :], l[:, :], r[:, hh * 512:(hh + 1) * 512],
                                    start=st_, stop=sp_, tile_position=tp),
                                    reads=[g["ones64"], pt], writes=[DDh[hh]], milestone=True, geom="c")

                    emit_S2(0)
                    emit_exp2(0)
                    for kt in range(0, nkt, 2):
                        if kt + 2 < nkt:
                            emit_S2(kt + 2)
                            emit_exp2(kt + 2)
                        emit_PVDD(kt)
                        if kt + 1 < nkt:
                            emit_PVDD(kt + 1)
                    at = attn_t[pr]
                    skips = set(K_SKIP.split(",")) if (K_SKIP and (pr >= 1 or K_PRBASE)) else set()
                    for hh in range(2):
                        if "lnd" in skips or "rd" in skips:
                            break
                        rows = slice(64 * hh, 64 * (hh + 1))
                        p.op("dve", lambda e, o=rdrep.ap, i=DDh[hh].ap, r=rows: e.reciprocal(
                            o[r, :], i[r, :]),
                            reads=[DDh[hh]], writes=[rdrep])
                    for hh in range(2):
                        if "at" in skips:
                            break
                        rows = slice(64 * hh, 64 * (hh + 1))
                        p.op("dve", lambda e, o=at.ap, i0=PVh[hh].ap, i1=rdrep.ap, r=rows: e.tensor_tensor(
                            o[r, :], i0[r, :], i1[r, :], ALU.mult),
                            reads=[PVh[hh], rdrep], writes=[at])
                    if layer == "a":
                        p.op("dve", lambda e, o=at.ap, i=at.ap, b=g["vb_sb"].ap, pr=pr: e.tensor_scalar(
                            o[:, :], i[:, :], b[:, pr:pr + 1], None, ALU.add),
                            reads=[at, g["vb_sb"]], writes=[at])
                for tt4 in range(4):
                    ti = qc * 4 + tt4
                    xp, xo = xprev[tt4 % 2], xnew[tt4 % 2]
                    p.op("qg", lambda e, o=xp.ap, i=x_src.ap, ti=ti: e.dma_start(
                        out=o[:, :], in_=i[ti * 128:(ti + 1) * 128, :]),
                        reads=[x_src], writes=[xp])
                    for ch in range(2):
                        ps = PR[ch]
                        for pr in range(NPAIR):
                            p.op("pe", lambda e, o=ps.ap, l=attn_t[pr].ap, r=pw_sb[pr].ap, tt4=tt4, ch=ch, pr=pr: e.matmul(
                                o[:, :], l[:, tt4 * 128:(tt4 + 1) * 128], r[:, ch * 512:(ch + 1) * 512],
                                start=(pr == 0), stop=(pr == NPAIR - 1)),
                                reads=[attn_t[pr], pw_sb[pr]], writes=[ps], milestone=(pr == NPAIR - 1))
                    for ch in range(2):
                        p.op("dve", lambda e, o=xo.ap, i0=PR[ch].ap, i1=xp.ap, ch=ch: e.tensor_tensor(
                            o[:, ch * 512:(ch + 1) * 512], i0[:, :], i1[:, ch * 512:(ch + 1) * 512], ALU.add),
                            reads=[PR[ch], xp], writes=[xo])
                    if layer == "a":
                        p.op("dve", lambda e, o=xo.ap, i0=xo.ap, i1=g["biasrep"].ap: e.tensor_tensor(
                            o[:, :], i0[:, :], i1[:, :], ALU.add),
                            reads=[xo, g["biasrep"]], writes=[xo])
                    else:
                        mu, rstd = stats_ln[ti]
                        p.op("dve", lambda e, o=ln3t.ap, i=xp.ap, m=mu.ap, r=rstd.ap: e.tensor_scalar(
                            o[:, :], i[:, :], m[:, 0:1], r[:, 0:1], ALU.subtract, ALU.mult),
                            reads=[xp, mu, rstd], writes=[ln3t])
                        p.op("dve", lambda e, o=xo.ap, i0=xo.ap, i1=ln3t.ap: e.tensor_tensor(
                            o[:, :], i0[:, :], i1[:, :], ALU.add),
                            reads=[xo, ln3t], writes=[xo])
                    mu2, rstd2 = stats_next[ti]
                    emit_ln_stats(p, xo, mu2, rstd2, scrB, smallB[0], smallB[1], smallB[2], g["epsb"])
                    p.op("qg", lambda e, o=x_out.ap, i=xo.ap, ti=ti: e.dma_start(
                        out=o[ti * 128:(ti + 1) * 128, :], in_=i[:, :]),
                        reads=[xo], writes=[x_out])
            p.barrier()


def build_ffn(p, g, x_src, x_out, stats_ln, xqTd, hqTd, nslab, final_out=None):
    nc = p.nc
    with ExitStack() as ph:
        def S(name, shape, dtype):
            return p.sbuf(f"f{nslab}_{name}", shape, dtype, ph)

        wgT = [S(f"wgT{dk}", [128, DFF], BF16) for dk in range(NDT)]
        wuT = [S(f"wuT{dk}", [128, DFF], BF16) for dk in range(NDT)]
        wdT = [S(f"wdT{k}", [128, D], BF16) for k in range(NIT)]
        for dk in range(NDT):
            p.op("qs", lambda e, o=wgT[dk].ap, i=g["wgTd"].ap, dk=dk: e.dma_start(
                out=o.rearrange("p (r c) -> p r c", r=NIT), in_=i[:, dk, :, :], transpose=True),
                reads=[g["wgTd"]], writes=[wgT[dk]])
            p.op("qs", lambda e, o=wuT[dk].ap, i=g["wuTd"].ap, dk=dk: e.dma_start(
                out=o.rearrange("p (r c) -> p r c", r=NIT), in_=i[:, dk, :, :], transpose=True),
                reads=[g["wuTd"]], writes=[wuT[dk]])
        for k in range(NIT):
            p.op("qs", lambda e, o=wdT[k].ap, i=g["wdTd"].ap, k=k: e.dma_start(
                out=o.rearrange("p (r c) -> p r c", r=NDT), in_=i[:, k, :, :], transpose=True),
                reads=[g["wdTd"]], writes=[wdT[k]])

        xsl = [S(f"x{i}", [128, D], F32) for i in range(2)]
        xq0 = S("xq0", [128, D], BF16)
        xq = [xq0, xq0]
        xqT = [S(f"xqT{i}", [128, NDT * 128], BF16) for i in range(2)]
        lnf0 = S("lnf0", [128, D], F32)
        hsl = [S(f"h{i}", [128, DFF], F32) for i in range(2)]
        hq0 = S("hq0", [128, DFF], BF16)
        hq = [hq0, hq0]
        hqT0 = S("hqT0", [128, NIT * 128], BF16)
        hqT = [hqT0, hqT0]
        sg0 = S("sg0", [128, 512], F32)
        sg = [sg0, sg0]
        xo0 = S("xo0", [128, D], F32)
        xo = [xo0, xo0]
        lnf = [lnf0, lnf0]
        qbx = {"sets": [mk_quant_bufs(p, f"fqa{nslab}", ph, qf=512, with_t0=False)]}
        gu = [p.psum(f"f{nslab}_gu{i}", [128, 512], ph) for i in range(4)]
        dn = [p.psum(f"f{nslab}_dn{i}", [128, 512], ph) for i in range(2)]

        CH = [(i * 512, min(512, DFF - i * 512)) for i in range((DFF + 511) // 512)]
        nsl = K_FSLAB or nslab

        def part1a(s):
            par = s % 2
            xb, lb, xqb, xqTb = xsl[par], lnf[par], xq[par], xqT[par]
            p.op("qg", lambda e, o=xb.ap, i=x_src.ap, s=s: e.dma_start(
                out=o[:, :], in_=i[s * 128:(s + 1) * 128, :]),
                reads=[x_src], writes=[xb])
            mu, rstd = stats_ln[s]
            p.op("dve", lambda e, o=lb.ap, i=xb.ap, m=mu.ap, r=rstd.ap: e.tensor_scalar(
                o[:, :], i[:, :], m[:, 0:1], r[:, 0:1], ALU.subtract, ALU.mult),
                reads=[xb, mu, rstd], writes=[lb])
            emit_quant(p, lb, xqb, qbx, D)
            p.op("qs", lambda e, o=xqTd.ap, i=xqb.ap, s=s: e.dma_start(
                out=o[s, :, :], in_=i[:, :]),
                reads=[xqb], writes=[xqTd])
            p.op("qs", lambda e, o=xqTb.ap, i=xqTd.ap, s=s: e.dma_start(
                out=o.rearrange("p (dk c) -> p dk c", dk=NDT),
                in_=i[s].rearrange("c (dk p) -> c dk p", p=128), transpose=True),
                reads=[xqTd], writes=[xqTb])

        def part1b(s):
            par = s % 2
            xqTb, hb = xqT[par], hsl[par]
            for ci, (c0, cw) in enumerate(CH):
                gp, up = gu[2 * (ci % 2)], gu[2 * (ci % 2) + 1]
                for dk in range(NDT):
                    p.op("pe", lambda e, o=gp.ap, l=xqTb.ap, r=wgT[dk].ap, dk=dk, c0=c0, cw=cw: e.matmul(
                        o[:, 0:cw], l[:, dk * 128:(dk + 1) * 128], r[:, c0:c0 + cw],
                        start=(dk == 0), stop=(dk == NDT - 1)),
                        reads=[xqTb, wgT[dk]], writes=[gp], milestone=(dk == NDT - 1))
                for dk in range(NDT):
                    p.op("pe", lambda e, o=up.ap, l=xqTb.ap, r=wuT[dk].ap, dk=dk, c0=c0, cw=cw: e.matmul(
                        o[:, 0:cw], l[:, dk * 128:(dk + 1) * 128], r[:, c0:c0 + cw],
                        start=(dk == 0), stop=(dk == NDT - 1)),
                        reads=[xqTb, wuT[dk]], writes=[up], milestone=(dk == NDT - 1))
                sgb = sg[ci % 2]
                p.op("act", lambda e, o=sgb.ap, i=gp.ap, cw=cw: e.activation(
                    out=o[:, 0:cw], in_=i[:, 0:cw], func=AF.Silu),
                    reads=[gp], writes=[sgb])
                p.op("dve", lambda e, o=hb.ap, i0=sgb.ap, i1=up.ap, c0=c0, cw=cw: e.tensor_tensor(
                    o[:, c0:c0 + cw], i0[:, 0:cw], i1[:, 0:cw], ALU.mult),
                    reads=[sgb, up], writes=[hb])

        def part2a(s):
            par = s % 2
            hb, hqb, hqTb = hsl[par], hq[par], hqT[par]
            emit_quant(p, hb, hqb, qbx, DFF)
            p.op("qs", lambda e, o=hqTd.ap, i=hqb.ap, s=s: e.dma_start(
                out=o[s, :, :], in_=i[:, :]),
                reads=[hqb], writes=[hqTd])
            p.op("qs", lambda e, o=hqTb.ap, i=hqTd.ap, s=s: e.dma_start(
                out=o.rearrange("p (k c) -> p k c", k=NIT),
                in_=i[s].rearrange("c (k p) -> c k p", p=128), transpose=True),
                reads=[hqTd], writes=[hqTb])

        def part2b(s):
            par = s % 2
            xb, hqTb, xob = xsl[par], hqT[par], xo[par]
            for ch in range(2):
                dp = dn[ch]
                for k in range(NIT):
                    p.op("pe", lambda e, o=dp.ap, l=hqTb.ap, r=wdT[k].ap, k=k, ch=ch: e.matmul(
                        o[:, :], l[:, k * 128:(k + 1) * 128], r[:, ch * 512:(ch + 1) * 512],
                        start=(k == 0), stop=(k == NIT - 1)),
                        reads=[hqTb, wdT[k]], writes=[dp], milestone=(k == NIT - 1))
                p.op("dve", lambda e, o=xob.ap, i0=dn[ch].ap, i1=xb.ap, ch=ch: e.tensor_tensor(
                    o[:, ch * 512:(ch + 1) * 512], i0[:, :], i1[:, ch * 512:(ch + 1) * 512], ALU.add),
                    reads=[dn[ch], xb], writes=[xob])
            tgt = final_out if final_out is not None else x_out
            p.op("qg", lambda e, o=tgt.ap, i=xob.ap, s=s: e.dma_start(
                out=o[s * 128:(s + 1) * 128, :], in_=i[:, :]),
                reads=[xob], writes=[tgt])

        part1a(0)
        part1b(0)
        for s in range(nsl):
            if s + 1 < nsl:
                part1a(s + 1)
            part2a(s)
            if s + 1 < nsl:
                part1b(s + 1)
            part2b(s)
        p.barrier()


_PROG_CACHE = {}


def _get_prog(stage="full"):
    if stage not in _PROG_CACHE:
        _PROG_CACHE[stage] = build_program(stage)
    return _PROG_CACHE[stage]


def host_inputs(inputs):
    x = np.asarray(inputs["L0_attn_a_x"])
    qkv_w = np.asarray(inputs["L0_attn_a_qkv_w"])
    qkv_b = np.asarray(inputs["L0_attn_a_qkv_b"])
    proj_w = np.asarray(inputs["L0_attn_a_proj_w"])
    proj_b = np.asarray(inputs["L0_attn_a_proj_b"])
    Wq = np.asarray(inputs["L0_attn_b_Wq"])
    Wk = np.asarray(inputs["L0_attn_b_Wk"])
    Wv = np.asarray(inputs["L0_attn_b_Wv"])
    Wp = np.asarray(inputs["L0_attn_b_Wp"])
    wg = np.asarray(inputs["L0_ffn_w_gate"])
    wu = np.asarray(inputs["L0_ffn_w_up"])
    wd = np.asarray(inputs["L0_ffn_w_down"])

    common = {
        "qkvwT": np.ascontiguousarray(qkv_w.T),
        "qkvb": np.ascontiguousarray(qkv_b[:2048].reshape(16, 128).T),
        "vb": np.ascontiguousarray(qkv_b[2048:].reshape(8, 128).T),
        "projwT": np.ascontiguousarray(proj_w.T),
        "projb_row": np.ascontiguousarray(proj_b.reshape(1, D)),
        "bW": np.ascontiguousarray(np.stack([Wq, Wk, Wv], 0)),
        "bWp": np.ascontiguousarray(Wp),
        "wg": np.ascontiguousarray(wg), "wu": np.ascontiguousarray(wu),
        "wd": np.ascontiguousarray(wd),
        "ones_in": np.ones((128, 64), np.float32),
    }
    permv = np.zeros((128, 128), np.float32)
    for j in range(64):
        permv[2 * j, 2 * j + 1] = 1.0
        permv[2 * j + 1, 2 * j] = 1.0
    common["perm_in"] = permv
    invv = (1.0 / 10000.0 ** (np.arange(0, 64, 2, dtype=np.float32) / 64)).astype(np.float32)
    common["inv_in"] = np.repeat(invv, 2).reshape(64, 1)
    common["sgnpat"] = np.where(np.arange(64) % 2 == 0, -1.0, 1.0).astype(np.float32).reshape(64, 1)

    in_maps = []
    for c in range(8):
        b, h = c // 2, c % 2
        xb = x[b]
        posv = np.arange(T, dtype=np.float32)
        if h == 1:
            xb = np.roll(xb, -TO, axis=0)
            posv = np.roll(posv, -TO)
        m = dict(common)
        m["x0"] = np.ascontiguousarray(xb)
        m["pos_rep"] = np.ascontiguousarray(np.tile(posv.reshape(1, T), (64, 1)))
        in_maps.append(m)
    return in_maps


def kernel(**inputs):
    nc, _ = _get_prog("full")
    in_maps = host_inputs(inputs)
    res = run_bass_kernel_spmd(nc, in_maps, core_ids=list(range(8)))
    outp = np.zeros((4, T, D), np.float32)
    for c in range(8):
        b, h = c // 2, c % 2
        outp[b, h * TO:(h + 1) * TO, :] = np.asarray(res.results[c]["out"])
    return outp


# version 15
# speedup vs baseline: 1.4532x; 1.0255x over previous
import math
import os
from contextlib import ExitStack

def _env(name, dflt):
    return int(os.environ.get(name, dflt))

K_QC = _env("K_QC", 0)
K_PAIRS = _env("K_PAIRS", 0)
K_KT = _env("K_KT", 0)
K_TC = _env("K_TC", 0)
K_SKIP = os.environ.get("K_SKIP", "")
K_PRBASE = _env("K_PRBASE", 0)
K_FSLAB = _env("K_FSLAB", 0)


import numpy as np
import ml_dtypes

import concourse.bass as bass
import concourse.mybir as mybir
from concourse.bass_utils import run_bass_kernel_spmd

F32 = mybir.dt.float32
BF16 = mybir.dt.bfloat16
I32 = mybir.dt.int32
U16 = mybir.dt.uint16
ALU = mybir.AluOpType
AF = mybir.ActivationFunctionType

SET_TRIG = 9
SET_LNEXP = 6
SET_SILU = 18


def load_act_set(e, idx):
    return e.add_instruction(mybir.InstLoadActFuncSet(
        name=e.bass.get_next_instruction_name(),
        act_func_set_id=idx, ins=[], outs=[]))
AX = mybir.AxisListType

T = 4096
TO = 2048
D = 1024
NH = 16
HD = 64
NPAIR = 8
DFF = 2816
QC = 512
NQC = T // QC
NQC_OWN = TO // QC
NT = T // 128
NTO = TO // 128
NDT = D // 128
NIT = DFF // 128
EPS = 1e-5
ATT_SCALE = 1.0 / math.sqrt(HD)
QF = 1408

ENGS = ("pe", "act", "dve", "pool", "qs", "qg")


class Buf:
    __slots__ = ("ap", "w", "r", "name")

    def __init__(self, ap, name=""):
        self.ap = ap
        self.name = name
        self.w = None
        self.r = []


class Prog:
    def __init__(self, nc, stack: ExitStack):
        self.nc = nc
        self.stack = stack
        self.stream_of = {"pe": "pe", "act": "act", "dve": "dve",
                          "pool": "pg", "qg": "pg", "qs": "qs"}
        self.thunks = {st: [] for st in ("pe", "act", "dve", "pg", "qs")}
        self.nsem = {"pe": 1, "act": 1, "dve": 1, "pool": 1, "qs": 6, "qg": 6}
        self.cnt = {}
        self.rr = {e: 0 for e in ENGS}
        self.semkeys = {}
        for e in ENGS:
            self.semkeys[e] = [f"{e}{i}" for i in range(self.nsem[e])]
            for k in self.semkeys[e]:
                self.cnt[k] = 0
        self.wm = {st: {} for st in ("pe", "act", "dve", "pg", "qs")}
        self.n_ops = 0
        self.pe_geom = None
        self.pe_pending = 0

    def sbuf(self, name, shape, dtype, stack=None):
        t = (stack or self.stack).enter_context(self.nc.sbuf_tensor(name, shape, dtype))
        return Buf(t.ap(), name)

    def psum(self, name, shape, stack=None):
        t = (stack or self.stack).enter_context(self.nc.psum_tensor(name, shape, F32))
        return Buf(t.ap(), name)

    def dram(self, name, shape, dtype, kind="Internal"):
        t = self.nc.dram_tensor(name, shape, dtype, kind=kind)
        return Buf(t.ap(), name)

    def sub(self, buf, *slices):
        return Buf(buf.ap.__getitem__(slices if len(slices) > 1 else slices[0]),
                   buf.name + "/s")

    def op(self, eng, fn, reads=(), writes=(), inc=None, milestone=True, geom="f"):
        if inc is None:
            inc = 16 if eng in ("qs", "qg") else 1
        deps = []
        for b in reads:
            if b.w is not None:
                deps.append(b.w)
        for b in writes:
            if b.w is not None:
                deps.append(b.w)
            deps.extend(b.r)
        stream = self.stream_of[eng]
        need = {}
        wm = self.wm[stream]
        for (k, v) in deps:
            if eng == "pe" and k == "pe0":
                continue
            if k == "pe0" and v < self.cnt["pe0"]:
                v = self.cnt["pe0"]
            if wm.get(k, 0) >= v:
                continue
            if need.get(k, 0) < v:
                need[k] = v
        if eng == "pe" and geom != self.pe_geom:
            assert self.pe_pending == 0, "geometry switch after non-milestone PE op"
            v = self.cnt["pe0"]
            if v > 0 and wm.get("pe0", 0) < v and need.get("pe0", 0) < v:
                need["pe0"] = v
            self.pe_geom = geom
        for k, v in need.items():
            wm[k] = v
        semkey = self.semkeys[eng][self.rr[eng]]
        if eng != "pe" and self.cnt[semkey] > 0:
            v = self.cnt[semkey]
            if need.get(semkey, 0) < v:
                need[semkey] = v
                wm[semkey] = max(wm.get(semkey, 0), v)
        if eng == "pe" and not milestone:
            tok = (semkey, self.cnt[semkey] + 1)
            do_inc = 0
            self.pe_pending += 1
        else:
            self.rr[eng] = (self.rr[eng] + 1) % self.nsem[eng]
            self.cnt[semkey] += inc
            tok = (semkey, self.cnt[semkey])
            do_inc = inc
            if eng == "pe":
                self.pe_pending = 0
        needl = sorted(need.items())

        def thunk(e, sems, fn=fn, needl=needl, semkey=semkey, do_inc=do_inc):
            for k, v in needl:
                e.wait_ge(sems[k], v)
            inst = fn(e)
            if do_inc:
                inst.then_inc(sems[semkey], do_inc)

        self.thunks[stream].append(thunk)
        self.n_ops += 1
        for b in writes:
            b.w = tok
            b.r = []
        for b in reads:
            b.r.append(tok)
        return tok

    def barrier(self):
        self.pe_geom = None
        snapshot = dict(self.cnt)
        for e in ("pe", "act", "dve", "pg", "qs"):
            wm = self.wm[e]
            needl = [(k, v) for k, v in sorted(snapshot.items())
                     if v > 0 and wm.get(k, 0) < v]
            for k, v in needl:
                wm[k] = v
            if not needl:
                continue

            def thunk(e_obj, sems, needl=needl):
                for k, v in needl:
                    e_obj.wait_ge(sems[k], v)

            self.thunks[e].append(thunk)

    def finalize(self):
        nc = self.nc
        sems = {}
        for e in ENGS:
            for k in self.semkeys[e]:
                sems[k] = self.stack.enter_context(nc.semaphore(k))
        with nc.Block() as block:
            @block.tensor
            def _(t):
                for th in self.thunks["pe"]:
                    th(t, sems)

            @block.scalar
            def _(s):
                for th in self.thunks["act"]:
                    th(s, sems)

            @block.vector
            def _(v):
                for th in self.thunks["dve"]:
                    th(v, sems)

            @block.gpsimd
            def _(gg):
                for th in self.thunks["pg"]:
                    th(gg, sems)

            @block.sync
            def _(s):
                for th in self.thunks["qs"]:
                    th(s, sems)


def _quant_chunk(p, eng, x, q_out, b, off, W):
    NB = W // 16
    x3 = x.ap[:, off:off + W].rearrange("p (nb k) -> p nb k", k=16)
    p.op("dve", lambda e, o=b["bm"].ap, i=x3, NB=NB: e.tensor_reduce(
        o[:, 0:NB], i, AX.X, ALU.max, apply_absolute_value=True),
        reads=[x], writes=[b["bm"]])
    p.op(eng, lambda e, o=b["bmc"].ap, i=b["bm"].ap, NB=NB: e.tensor_scalar(
        o[:, 0:NB], i[:, 0:NB], 6e-12, None, ALU.max),
        reads=[b["bm"]], writes=[b["bmc"]])
    p.op("dve", lambda e, o=b["rb"].ap, i=b["bmc"].ap, NB=NB: e.reciprocal(o[:, 0:NB], i[:, 0:NB]),
         reads=[b["bmc"]], writes=[b["rb"]])
    p.op(eng, lambda e, o=b["rb6"].ap, i=b["rb"].ap, NB=NB: e.tensor_scalar(
        o[:, 0:NB], i[:, 0:NB], 6.0, None, ALU.mult),
        reads=[b["rb"]], writes=[b["rb6"]])
    p.op(eng, lambda e, o=b["sb"].ap, i=b["bmc"].ap, NB=NB: e.tensor_scalar(
        o[:, 0:NB], i[:, 0:NB], 1.0 / 6.0, None, ALU.mult),
        reads=[b["bmc"]], writes=[b["sb"]])
    rb6b = b["rb6"].ap[:, 0:NB].unsqueeze(2).broadcast_to([128, NB, 16])
    p.op(eng, lambda e, o=b["xn"].ap, i0=rb6b, i1=x3, W=W: e.tensor_tensor(
        o[:, 0:W].rearrange("p (nb k) -> p nb k", k=16), i0, i1, ALU.mult),
        reads=[b["rb6"], x], writes=[b["xn"]])
    p.op(eng, lambda e, o=b["c2"].ap, i=b["xn"].ap, W=W: e.tensor_scalar(
        o[:, 0:W], i[:, 0:W], 2.0, -2.0, ALU.min, ALU.max),
        reads=[b["xn"]], writes=[b["c2"]])
    p.op(eng, lambda e, o=b["c4"].ap, i=b["xn"].ap, W=W: e.tensor_scalar(
        o[:, 0:W], i[:, 0:W], 4.0, -4.0, ALU.min, ALU.max),
        reads=[b["xn"]], writes=[b["c4"]])
    p.op(eng, lambda e, o=b["bst"].ap, i=b["c2"].ap, W=W: e.tensor_scalar(
        o[:, 0:W], i[:, 0:W], 96.0, None, ALU.add),
        reads=[b["c2"]], writes=[b["bst"]])
    p.op(eng, lambda e, o=b["u1"].ap, i=b["bst"].ap, W=W: e.tensor_copy(
        o[:, 0:W], i[:, 0:W].bitcast(U16)),
        reads=[b["bst"]], writes=[b["u1"]])
    if eng == "dve":
        p.op(eng, lambda e, o=b["bst"].ap, i0=b["c2"].ap, i1=b["c4"].ap, W=W: e.scalar_tensor_tensor(
            o[:, 0:W], i0[:, 0:W], 192.0, i1[:, 0:W], ALU.add, ALU.subtract),
            reads=[b["c2"], b["c4"]], writes=[b["bst"]])
    else:
        p.op(eng, lambda e, o=b["t0"].ap, i=b["c2"].ap, W=W: e.tensor_scalar(
            o[:, 0:W], i[:, 0:W], 192.0, None, ALU.add),
            reads=[b["c2"]], writes=[b["t0"]])
        p.op(eng, lambda e, o=b["bst"].ap, i0=b["t0"].ap, i1=b["c4"].ap, W=W: e.tensor_tensor(
            o[:, 0:W], i0[:, 0:W], i1[:, 0:W], ALU.subtract),
            reads=[b["t0"], b["c4"]], writes=[b["bst"]])
    p.op(eng, lambda e, o=b["u2"].ap, i=b["bst"].ap, W=W: e.tensor_copy(
        o[:, 0:W], i[:, 0:W].bitcast(U16)),
        reads=[b["bst"]], writes=[b["u2"]])
    p.op(eng, lambda e, o=b["c2"].ap, i0=b["u1"].ap, i1=b["u2"].ap, W=W: e.tensor_tensor(
        o[:, 0:W], i0[:, 0:W].bitcast(BF16), i1[:, 0:W].bitcast(BF16), ALU.subtract),
        reads=[b["u1"], b["u2"]], writes=[b["c2"]])
    if eng == "dve":
        p.op(eng, lambda e, o=b["bst"].ap, i0=b["xn"].ap, i1=b["c4"].ap, W=W: e.scalar_tensor_tensor(
            o[:, 0:W], i0[:, 0:W], 384.0, i1[:, 0:W], ALU.add, ALU.subtract),
            reads=[b["xn"], b["c4"]], writes=[b["bst"]])
    else:
        p.op(eng, lambda e, o=b["t0"].ap, i=b["xn"].ap, W=W: e.tensor_scalar(
            o[:, 0:W], i[:, 0:W], 384.0, None, ALU.add),
            reads=[b["xn"]], writes=[b["t0"]])
        p.op(eng, lambda e, o=b["bst"].ap, i0=b["t0"].ap, i1=b["c4"].ap, W=W: e.tensor_tensor(
            o[:, 0:W], i0[:, 0:W], i1[:, 0:W], ALU.subtract),
            reads=[b["t0"], b["c4"]], writes=[b["bst"]])
    p.op(eng, lambda e, o=b["u2"].ap, i=b["bst"].ap, W=W: e.tensor_copy(
        o[:, 0:W], i[:, 0:W].bitcast(U16)),
        reads=[b["bst"]], writes=[b["u2"]])
    p.op(eng, lambda e, o=b["xn"].ap, i0=b["c2"].ap, i1=b["u2"].ap, W=W: e.tensor_tensor(
        o[:, 0:W], i0[:, 0:W], i1[:, 0:W].bitcast(BF16), ALU.add),
        reads=[b["c2"], b["u2"]], writes=[b["xn"]])
    sbb = b["sb"].ap[:, 0:NB].unsqueeze(2).broadcast_to([128, NB, 16])
    qs_ap = q_out.ap[:, off:off + W]
    if eng == "dve":
        p.op(eng, lambda e, o=qs_ap, i0=b["xn"].ap, i1=sbb, W=W, NB=NB: e.scalar_tensor_tensor(
            o.rearrange("p (nb k) -> p nb k", k=16),
            i0[:, 0:W].rearrange("p (nb k) -> p nb k", k=16), -288.0, i1, ALU.add, ALU.mult),
            reads=[b["xn"], b["sb"]], writes=[q_out])
    else:
        p.op(eng, lambda e, o=b["t0"].ap, i=b["xn"].ap, W=W: e.tensor_scalar(
            o[:, 0:W], i[:, 0:W], -288.0, None, ALU.add),
            reads=[b["xn"]], writes=[b["t0"]])
        p.op(eng, lambda e, o=qs_ap, i0=b["t0"].ap, i1=sbb, W=W, NB=NB: e.tensor_tensor(
            o.rearrange("p (nb k) -> p nb k", k=16),
            i0[:, 0:W].rearrange("p (nb k) -> p nb k", k=16), i1, ALU.mult),
            reads=[b["t0"], b["sb"]], writes=[q_out])


def emit_quant(p, x, q_out, b, F):
    sets = b["sets"] if "sets" in b else [b]
    qf = sets[0]["qf"]
    for off in range(0, F, qf):
        W = min(qf, F - off)
        _quant_chunk(p, "dve", x, q_out, sets[0], off, W)


def mk_quant_bufs(p, tag, stack, qf=QF, with_t0=True):
    NB = qf // 16
    return {
        "qf": qf,
        "bm": p.sbuf(f"{tag}_bm", [128, NB], F32, stack),
        "bmc": p.sbuf(f"{tag}_bmc", [128, NB], F32, stack),
        "rb": p.sbuf(f"{tag}_rb", [128, NB], F32, stack),
        "rb6": p.sbuf(f"{tag}_rb6", [128, NB], F32, stack),
        "sb": p.sbuf(f"{tag}_sb", [128, NB], F32, stack),
        "xn": p.sbuf(f"{tag}_xn", [128, qf], F32, stack),
        "c2": p.sbuf(f"{tag}_c2", [128, qf], F32, stack),
        "c4": p.sbuf(f"{tag}_c4", [128, qf], F32, stack),
        "bst": p.sbuf(f"{tag}_bst", [128, qf], BF16, stack),
        "u1": p.sbuf(f"{tag}_u1", [128, qf], U16, stack),
        "u2": p.sbuf(f"{tag}_u2", [128, qf], U16, stack),
        **({"t0": p.sbuf(f"{tag}_t0", [128, qf], F32, stack)} if with_t0 else {}),
    }


def emit_ln_stats(p, x, mu, rstd, scr, sumx, ssq, mu2, epsb=None):
    p.op("dve", lambda e, o=scr.ap, i=x.ap, a=ssq.ap: e.scalar_tensor_tensor(
        o[:, :], i[:, :], 1.0, i[:, :], ALU.mult, ALU.mult, accum_out=a[:, :]),
        reads=[x], writes=[scr, ssq])
    p.op("dve", lambda e, o=sumx.ap, i=x.ap: e.reduce_sum(o[:, :], i[:, :], AX.X),
         reads=[x], writes=[sumx])
    p.op("dve", lambda e, o=mu.ap, i=sumx.ap: e.tensor_scalar(o[:, :], i[:, :], 1.0 / D, None, ALU.mult),
         reads=[sumx], writes=[mu])
    p.op("dve", lambda e, o=mu2.ap, i=ssq.ap: e.tensor_scalar(o[:, :], i[:, :], 1.0 / D, None, ALU.mult),
         reads=[ssq], writes=[mu2])
    p.op("dve", lambda e, o=mu2.ap, i0=mu.ap, s=mu.ap, i1=mu2.ap: e.scalar_tensor_tensor(
        o[:, :], i0[:, :], s[:, 0:1], i1[:, :], ALU.mult, ALU.subtract),
        reads=[mu, mu2], writes=[mu2])
    p.op("act", lambda e, o=mu2.ap, i=mu2.ap, b=epsb.ap: e.activation(
        out=o[:, :], in_=i[:, :], func=AF.Ln, scale=-1.0, bias=b[:, 0:1]),
        reads=[mu2, epsb], writes=[mu2])
    p.op("act", lambda e, o=rstd.ap, i=mu2.ap: e.activation(
        out=o[:, :], in_=i[:, :], func=AF.Exp, scale=-0.5),
        reads=[mu2], writes=[rstd])


def build_program(stage="full"):
    nc = bass.Bass("TRN2", target_bir_lowering=False, debug=False)
    stack = ExitStack()
    p = Prog(nc, stack)
    g = {}

    for name, shape, dtype in (
            ("x0", [T, D], F32), ("qkvwT", [D, 3 * D], F32), ("qkvb", [128, 16], F32),
            ("vb", [128, 8], F32), ("projwT", [D, D], F32), ("projb_row", [1, D], F32),
            ("bW", [3, D, D], F32), ("bWp", [D, D], F32),
            ("wg", [DFF, D], F32), ("wu", [DFF, D], F32), ("wd", [D, DFF], F32),
            ("pos_rep", [64, T], F32), ("inv_in", [64, 1], F32),
            ("sgnpat", [64, 1], F32), ("ones_in", [128, 64], F32), ("perm_in", [128, 128], F32)):
        g[name] = p.dram(name, shape, dtype, kind="ExternalInput")
    g["out"] = p.dram("out", [TO, D], F32, kind="ExternalOutput")
    if stage != "full":
        g["dbg"] = p.dram("dbg", [T, D], F32, kind="ExternalOutput")
        p.g_dbg = g["dbg"]
    if stage == "p0":
        for nm, shp in (("dbg_freq", [64, T]), ("dbg_r", [64, T]), ("dbg_rc", [64, T]), ("dbg_inv", [64, 1])):
            g[nm] = p.dram(nm, shp, F32, kind="ExternalOutput")

    dbg_dram = set(os.environ.get("K_DBGDRAM", "").split(","))
    for name, shape, dtype in (
            ("x1d", [T, D], F32), ("x2d", [T, D], F32), ("x3d", [TO, D], F32),
            ("Cd", [128, T], F32), ("Sd", [128, T], F32),
            ("lnTd_a", [NT, 128, D], BF16), ("lnTd_b", [NT, 128, D], BF16),
            ("QTd_a", [NDT, 128, T], BF16), ("QTd_b", [NDT, 128, TO], BF16),
            ("Vd_a", [NT, 128, D], BF16), ("Vd_b", [NT, 128, D], BF16),
            ("wgTd", [128, NDT, NIT, 128], BF16), ("wuTd", [128, NDT, NIT, 128], BF16),
            ("wdTd", [128, NIT, NDT, 128], BF16),
            ("xqTd_1", [NT, 128, D], BF16), ("hqTd_1", [NT, 128, DFF], BF16),
            ("xqTd_2", [NTO, 128, D], BF16), ("hqTd_2", [NTO, 128, DFF], BF16),
            ("bWbf", [3, D, D], BF16), ("bWpbf", [D, D], BF16)):
        g[name] = p.dram(name, shape, dtype,
                         kind="ExternalOutput" if name in dbg_dram else "Internal")

    with stack:
        g["stats_sb"] = p.sbuf("stats", [128, 4 * NT * 2], F32)
        p.g_stats = g["stats_sb"]

        def stats_of(grp, i):
            base = (grp * NT + i) * 2
            return (p.sub(g["stats_sb"], slice(None), slice(base, base + 1)),
                    p.sub(g["stats_sb"], slice(None), slice(base + 1, base + 2)))
        g["stats"] = {grp: [stats_of(grp, i) for i in range(NT)] for grp in range(4)}

        g["ones64"] = p.sbuf("ones64", [128, 64], BF16)
        g["perm"] = p.sbuf("perm", [128, 128], BF16)
        g["biasrep"] = p.sbuf("biasrep", [128, D], F32)
        g["vb_sb"] = p.sbuf("vb_sb", [128, 8], F32)
        g["qkvb_sb"] = p.sbuf("qkvb_sb", [128, 16], F32)
        g["epsb"] = p.sbuf("epsb", [128, 1], F32)
        p.op("pool", lambda e, o=g["epsb"].ap: e.memset(o[:, :], EPS), writes=[g["epsb"]])
        g["c96"] = p.sbuf("c96", [128, 1], F32)
        p.op("pool", lambda e, o=g["c96"].ap: e.memset(o[:, :], 96.0), writes=[g["c96"]])

        p.g96 = g["c96"]
        build_phase0(p, g)
        p.barrier()

        if stage == "p0":
            dump_p0(p, g)
            p.finalize()
            return nc, p

        if stage == "ffnx":
            g["x1in"] = p.dram("x1in", [T, D], F32, kind="ExternalInput")
            p.op("act", lambda e: load_act_set(e, SET_LNEXP))
            with ExitStack() as ph:
                fx = p.sbuf("fx_x", [128, D], F32, ph)
                fscr = p.sbuf("fx_scr", [128, D], F32, ph)
                fsm = [p.sbuf(f"fx_sm{i}", [128, 1], F32, ph) for i in range(3)]
                for s in range(NT):
                    p.op("qg", lambda e, o=fx.ap, i=g["x1in"].ap, s=s: e.dma_start(
                        out=o[:, :], in_=i[s * 128:(s + 1) * 128, :]),
                        reads=[g["x1in"]], writes=[fx])
                    mu2, rstd2 = g["stats"][1][s]
                    emit_ln_stats(p, fx, mu2, rstd2, fscr, fsm[0], fsm[1], fsm[2], g["epsb"])
                p.barrier()
            p.op("act", lambda e: load_act_set(e, SET_SILU))
            build_ffn(p, g, g["x1in"], g["x2d"], g["stats"][1], g["xqTd_1"], g["hqTd_1"], NT)
            if not os.environ.get("K_QDBG"):
                finish_debug(p, g, g["x2d"], T)
            else:
                p.barrier()
            p.finalize()
            return nc, p

        p.op("act", lambda e: load_act_set(e, SET_LNEXP))
        build_attn(p, g, "a", g["x0"], g["x1d"], g["stats"][0], g["stats"][1],
                   g["lnTd_a"], g["QTd_a"], g["Vd_a"], NQC)
        if stage == "attn_a":
            finish_debug(p, g, g["x1d"], T)
            p.finalize()
            return nc, p
        p.barrier()

        p.op("act", lambda e: load_act_set(e, SET_SILU))
        build_ffn(p, g, g["x1d"], g["x2d"], g["stats"][1], g["xqTd_1"], g["hqTd_1"], NT)
        if stage == "ffn1":
            finish_debug(p, g, g["x2d"], T)
            p.finalize()
            return nc, p
        p.barrier()

        p.op("act", lambda e: load_act_set(e, SET_LNEXP))
        build_attn(p, g, "b", g["x2d"], g["x3d"], g["stats"][2], g["stats"][3],
                   g["lnTd_b"], g["QTd_b"], g["Vd_b"], NQC_OWN)
        if stage == "attn_b":
            finish_debug(p, g, g["x3d"], TO)
            p.finalize()
            return nc, p
        p.barrier()

        p.op("act", lambda e: load_act_set(e, SET_SILU))
        build_ffn(p, g, g["x3d"], None, g["stats"][3], g["xqTd_2"], g["hqTd_2"], NTO,
                  final_out=g["out"])
        p.barrier()
        p.finalize()
    return nc, p


def dump_p0(p, g):
    nc = p.nc
    with ExitStack() as ph:
        t32 = p.sbuf("d_t32", [128, T], F32, ph)
        tb = p.sbuf("d_tb", [128, D], BF16, ph)
        dbgv = g["dbg"].ap.rearrange("(a p r) d -> a p (r d)", p=128, r=4)
        for ix, srcb in enumerate((g["Cd"], g["Sd"])):
            p.op("qs", lambda e, o=t32.ap, i=srcb.ap: e.dma_start(out=o[:, :], in_=i[:, :]),
                 reads=[srcb], writes=[t32])
            p.op("qs", lambda e, o=dbgv, i=t32.ap, ix=ix: e.dma_start(out=o[ix], in_=i[:, :]),
                 reads=[t32], writes=[g["dbg"]])
        tb3 = tb.ap.rearrange("c (dk p) -> c dk p", p=128)
        for ix, (srcb, apsel, o_ap) in enumerate((
                (g["wgTd"], g["wgTd"].ap[:, :, 0, :], tb3),
                (g["bWbf"], g["bWbf"].ap[0, 0:128, :], tb.ap[:, :]),
                (g["wdTd"], g["wdTd"].ap[:, 0:NDT, 0, :], tb3))):
            p.op("qs", lambda e, o=o_ap, i=apsel: e.dma_start(out=o, in_=i),
                 reads=[srcb], writes=[tb])
            p.op("dve", lambda e, o=t32.ap, i=tb.ap: e.tensor_copy(o[:, 0:D], i[:, :]),
                 reads=[tb], writes=[t32])
            p.op("qs", lambda e, o=g["dbg"].ap, i=t32.ap, ix=ix: e.dma_start(
                out=o[2048 + ix * 128: 2048 + (ix + 1) * 128, :], in_=i[:, 0:D]),
                reads=[t32], writes=[g["dbg"]])
        p.barrier()


def finish_debug(p, g, src, rows):
    with ExitStack() as ph:
        tmp = p.sbuf("dbg_tmp", [128, D], F32, ph)
        for i in range(rows // 128):
            p.op("qs", lambda e, o=tmp.ap, i_=src.ap, i2=i: e.dma_start(
                out=o[:, :], in_=i_[i2 * 128:(i2 + 1) * 128, :]), reads=[src], writes=[tmp])
            p.op("qs", lambda e, o=g["dbg"].ap, i_=tmp.ap, i2=i: e.dma_start(
                out=o[i2 * 128:(i2 + 1) * 128, :], in_=i_[:, :]), reads=[tmp], writes=[g["dbg"]])
        p.barrier()


def build_phase0(p, g):
    nc = p.nc
    with ExitStack() as ph:
        tmpf = p.sbuf("c_tmpf", [128, 128], F32, ph)
        p.op("qs", lambda e, o=tmpf.ap, i=g["ones_in"].ap: e.dma_start(out=o[:, 0:64], in_=i[:, :]),
             reads=[g["ones_in"]], writes=[tmpf])
        p.op("dve", lambda e, o=g["ones64"].ap, i=tmpf.ap: e.tensor_copy(o[:, :], i[:, 0:64]),
             reads=[tmpf], writes=[g["ones64"]])
        p.op("qs", lambda e, o=tmpf.ap, i=g["perm_in"].ap: e.dma_start(out=o[:, :], in_=i[:, :]),
             reads=[g["perm_in"]], writes=[tmpf])
        p.op("dve", lambda e, o=g["perm"].ap, i=tmpf.ap: e.tensor_copy(o[:, :], i[:, :]),
             reads=[tmpf], writes=[g["perm"]])
        p.op("qs", lambda e, o=g["vb_sb"].ap, i=g["vb"].ap: e.dma_start(out=o[:, :], in_=i[:, :]),
             reads=[g["vb"]], writes=[g["vb_sb"]])
        p.op("qs", lambda e, o=g["qkvb_sb"].ap, i=g["qkvb"].ap: e.dma_start(out=o[:, :], in_=i[:, :]),
             reads=[g["qkvb"]], writes=[g["qkvb_sb"]])

        onesrow = p.sbuf("c_onesrow", [1, 128], BF16, ph)
        pbrow = p.sbuf("c_pbrow", [1, D], BF16, ph)
        pbrow32 = p.sbuf("c_pbrow32", [1, D], F32, ph)
        p.op("pool", lambda e, o=onesrow.ap: e.memset(o[:, :], 1.0), writes=[onesrow])
        p.op("qs", lambda e, o=pbrow32.ap, i=g["projb_row"].ap: e.dma_start(out=o[:, :], in_=i[:, :]),
             reads=[g["projb_row"]], writes=[pbrow32])
        p.op("dve", lambda e, o=pbrow.ap, i=pbrow32.ap: e.tensor_copy(o[:, :], i[:, :]),
             reads=[pbrow32], writes=[pbrow])
        psb = p.psum("c_ps", [128, 512], ph)
        for ch in range(2):
            p.op("pe", lambda e, o=psb.ap, l=onesrow.ap, r=pbrow.ap, c=ch: e.matmul(
                o[:, :], l[:, :], r[:, c * 512:(c + 1) * 512], start=True, stop=True),
                reads=[onesrow, pbrow], writes=[psb], geom="k1")
            p.op("dve", lambda e, o=g["biasrep"].ap, i=psb.ap, c=ch: e.tensor_copy(
                o[:, c * 512:(c + 1) * 512], i[:, :]),
                reads=[psb], writes=[g["biasrep"]])
        p.barrier()

    with ExitStack() as ph:
        posb = p.sbuf("c_pos", [64, T], F32, ph)
        sgnb = p.sbuf("c_sgn", [64, 1], F32, ph)
        inv = p.sbuf("c_inv", [64, 1], F32, ph)
        freq = p.sbuf("c_freq", [64, T], F32, ph)
        freq2 = p.sbuf("c_freq2", [64, T], F32, ph)
        kk = p.sbuf("c_kk", [64, T], I32, ph)
        s1 = p.sbuf("c_s1", [64, T], F32, ph)
        s2 = p.sbuf("c_s2", [64, T], F32, ph)
        tabC = p.sbuf("c_tabC", [128, T], F32, ph)
        tabS = p.sbuf("c_tabS", [128, T], F32, ph)
        for src, dst in ((g["pos_rep"], posb), (g["inv_in"], inv), (g["sgnpat"], sgnb)):
            p.op("qs", lambda e, o=dst.ap, i=src.ap: e.dma_start(out=o[:, :], in_=i[:, :]),
                 reads=[src], writes=[dst])
        p.op("act", lambda e: load_act_set(e, SET_TRIG))
        p.op("dve", lambda e, o=freq.ap, i=posb.ap, s=inv.ap: e.tensor_scalar(
            o[:, :], i[:, :], s[:, 0:1], None, ALU.mult),
            reads=[posb, inv], writes=[freq])
        p.op("dve", lambda e, o=kk.ap, i=freq.ap: e.tensor_scalar(
            o[:, :], i[:, :], 1.0 / (2 * math.pi), None, ALU.mult),
            reads=[freq], writes=[kk])
        TWO_PI = 2 * math.pi
        c1 = float(np.float32(TWO_PI))
        c2 = float(np.float32(TWO_PI - c1))
        c3 = float(np.float32(TWO_PI - c1 - c2))
        p.op("dve", lambda e, o=s1.ap, i0=kk.ap, i1=freq.ap: e.scalar_tensor_tensor(
            o[:, :], i0[:, :], -c1, i1[:, :], ALU.mult, ALU.add),
            reads=[kk, freq], writes=[s1])
        p.op("dve", lambda e, o=s2.ap, i0=kk.ap, i1=s1.ap: e.scalar_tensor_tensor(
            o[:, :], i0[:, :], -c2, i1[:, :], ALU.mult, ALU.add),
            reads=[kk, s1], writes=[s2])
        p.op("dve", lambda e, o=s1.ap, i0=kk.ap, i1=s2.ap: e.scalar_tensor_tensor(
            o[:, :], i0[:, :], -c3, i1[:, :], ALU.mult, ALU.add),
            reads=[kk, s2], writes=[s1])
        p.op("dve", lambda e, o=freq2.ap, i=s1.ap: e.tensor_scalar(
            o[:, :], i[:, :], math.pi, 2 * math.pi, ALU.is_gt, ALU.mult),
            reads=[s1], writes=[freq2])
        p.op("dve", lambda e, o=s1.ap, i0=s1.ap, i1=freq2.ap: e.tensor_tensor(
            o[:, :], i0[:, :], i1[:, :], ALU.subtract),
            reads=[s1, freq2], writes=[s1])
        p.op("dve", lambda e, o=freq2.ap, i=s1.ap: e.tensor_scalar(
            o[:, :], i[:, :], -math.pi, 2 * math.pi, ALU.is_lt, ALU.mult),
            reads=[s1], writes=[freq2])
        p.op("dve", lambda e, o=s1.ap, i0=s1.ap, i1=freq2.ap: e.tensor_tensor(
            o[:, :], i0[:, :], i1[:, :], ALU.add),
            reads=[s1, freq2], writes=[s1])
        p.op("act", lambda e, o=tabS.ap, i=s1.ap: e.activation(out=o[0:64, :], in_=i[:, :], func=AF.Sin),
             reads=[s1], writes=[tabS])
        p.op("dve", lambda e, o=s2.ap, i=s1.ap: e.tensor_scalar(o[:, :], i[:, :], math.pi / 2, None, ALU.add),
             reads=[s1], writes=[s2])
        p.op("dve", lambda e, o=freq2.ap, i=s2.ap: e.tensor_scalar(o[:, :], i[:, :], math.pi, 2 * math.pi, ALU.is_gt, ALU.mult),
             reads=[s2], writes=[freq2])
        p.op("dve", lambda e, o=s2.ap, i0=s2.ap, i1=freq2.ap: e.tensor_tensor(o[:, :], i0[:, :], i1[:, :], ALU.subtract),
             reads=[s2, freq2], writes=[s2])
        p.op("act", lambda e, o=tabC.ap, i=s2.ap: e.activation(out=o[0:64, :], in_=i[:, :], func=AF.Sin),
             reads=[s2], writes=[tabC])
        p.op("dve", lambda e, o=tabS.ap, i=tabS.ap, s=sgnb.ap: e.tensor_scalar(
            o[0:64, :], i[0:64, :], s[:, 0:1], None, ALU.mult),
            reads=[tabS, sgnb], writes=[tabS])
        p.op("dve", lambda e, o=tabC.ap, i=tabC.ap: e.tensor_copy(o[64:128, :], i[0:64, :]),
             reads=[tabC], writes=[tabC])
        p.op("dve", lambda e, o=tabS.ap, i=tabS.ap: e.tensor_copy(o[64:128, :], i[0:64, :]),
             reads=[tabS], writes=[tabS])
        if "dbg_freq" in g:
            p.op("qs", lambda e, o=g["dbg_freq"].ap, i=freq.ap: e.dma_start(out=o[:, :], in_=i[:, :]),
                 reads=[freq], writes=[g["dbg_freq"]])
            p.op("qs", lambda e, o=g["dbg_r"].ap, i=s1.ap: e.dma_start(out=o[:, :], in_=i[:, :]),
                 reads=[s1], writes=[g["dbg_r"]])
            p.op("qs", lambda e, o=g["dbg_rc"].ap, i=s2.ap: e.dma_start(out=o[:, :], in_=i[:, :]),
                 reads=[s2], writes=[g["dbg_rc"]])
            p.op("qs", lambda e, o=g["dbg_inv"].ap, i=inv.ap: e.dma_start(out=o[:, :], in_=i[:, :]),
                 reads=[inv], writes=[g["dbg_inv"]])
        p.op("qs", lambda e, o=g["Cd"].ap, i=tabC.ap: e.dma_start(out=o[:, :], in_=i[:, :]),
             reads=[tabC], writes=[g["Cd"]])
        p.op("qs", lambda e, o=g["Sd"].ap, i=tabS.ap: e.dma_start(out=o[:, :], in_=i[:, :]),
             reads=[tabS], writes=[g["Sd"]])
        p.barrier()

    with ExitStack() as ph:
        qb = {"sets": [mk_quant_bufs(p, "wqa", ph, with_t0=False)]}
        wslab = [p.sbuf(f"c_wslab{i}", [128, DFF], F32, ph) for i in range(2)]
        wqslab = [p.sbuf(f"c_wqslab{i}", [128, DFF], BF16, ph) for i in range(2)]
        for wsrc, wdst, nslab, FW in ((g["wg"], g["wgTd"], NIT, D),
                                      (g["wu"], g["wuTd"], NIT, D),
                                      (g["wd"], g["wdTd"], NDT, DFF)):
            for s in range(nslab):
                sl, ql = wslab[s % 2], wqslab[s % 2]
                q = "qs" if s % 2 == 0 else "qg"
                p.op(q, lambda e, o=sl.ap, i=wsrc.ap, s=s, F=FW: e.dma_start(
                    out=o[:, 0:F], in_=i[s * 128:(s + 1) * 128, :]),
                    reads=[wsrc], writes=[sl])
                emit_quant(p, sl, ql, qb, FW)
                p.op(q, lambda e, o=wdst.ap, i=ql.ap, s=s, F=FW: e.dma_start(
                    out=o[:, :, s, :],
                    in_=i[:, 0:F].rearrange("c (dk p) -> c dk p", p=128)),
                    reads=[ql], writes=[wdst])
        for s in range(3 * NDT):
            sl, ql = wslab[s % 2], wqslab[s % 2]
            q = "qs" if s % 2 == 0 else "qg"
            p.op(q, lambda e, o=sl.ap, i=g["bW"].ap, s=s: e.dma_start(
                out=o[:, 0:D], in_=i[s // NDT, (s % NDT) * 128:(s % NDT + 1) * 128, :]),
                reads=[g["bW"]], writes=[sl])
            p.op("dve", lambda e, o=ql.ap, i=sl.ap: e.tensor_copy(o[:, 0:D], i[:, 0:D]),
                 reads=[sl], writes=[ql])
            p.op(q, lambda e, o=g["bWbf"].ap, i=ql.ap, s=s: e.dma_start(
                out=o[s // NDT, (s % NDT) * 128:(s % NDT + 1) * 128, :], in_=i[:, 0:D]),
                reads=[ql], writes=[g["bWbf"]])
        for s in range(NDT):
            sl, ql = wslab[s % 2], wqslab[s % 2]
            q = "qs" if s % 2 == 0 else "qg"
            p.op(q, lambda e, o=sl.ap, i=g["bWp"].ap, s=s: e.dma_start(
                out=o[:, 0:D], in_=i[s * 128:(s + 1) * 128, :]),
                reads=[g["bWp"]], writes=[sl])
            p.op("dve", lambda e, o=ql.ap, i=sl.ap: e.tensor_copy(o[:, 0:D], i[:, 0:D]),
                 reads=[sl], writes=[ql])
            p.op(q, lambda e, o=g["bWpbf"].ap, i=ql.ap, s=s: e.dma_start(
                out=o[s * 128:(s + 1) * 128, :], in_=i[:, 0:D]),
                reads=[ql], writes=[g["bWpbf"]])
        p.barrier()


def build_attn(p, g, layer, x_src, x_out, stats_ln, stats_next, lnTd, QTd, Vd, nqc):
    nc = p.nc
    rope = layer == "a"
    with ExitStack() as outer:
        KT_sb = p.sbuf(f"{layer}_KT", [128, NDT * T], BF16, outer)
        KT_t = [[p.sub(KT_sb, slice(None), slice(dk * T + tc * QC, dk * T + (tc + 1) * QC))
                 for tc in range(NQC)] for dk in range(NDT)]

        with ExitStack() as ph:
            def S(name, shape, dtype):
                return p.sbuf(f"{layer}A_{name}", shape, dtype, ph)

            xsb = [S(f"x{i}", [128, D], F32) for i in range(2)]
            scr = S("scr", [128, D], F32)
            small = [S(f"sm{i}", [128, 1], F32) for i in range(3)]
            lnsb = [S(f"ln{i}", [128, D], BF16) for i in range(2)]
            lnTc = [S(f"lnT{i}", [128, NDT * 512], BF16) for i in range(2)]
            wq_sb = [S(f"w{dk}", [128, 3 * D], BF16) for dk in range(NDT)]
            wtmp = [S(f"wtmp{i}", [128, 3 * D], F32) for i in range(2)]
            ropeC = [S(f"ropC{i}", [128, QC], F32) for i in range(2)]
            ropeS = [S(f"ropS{i}", [128, QC], F32) for i in range(2)]
            qb_e = [S(f"qb{i}", [128, QC], BF16) for i in range(2)]
            t1_e = [S(f"t1{i}", [128, QC], F32) for i in range(2)]
            t2_e = [S(f"t2{i}", [128, QC], F32) for i in range(2)]
            qro = [S(f"qro{i}", [128, QC], BF16) for i in range(2)]
            vev = [S(f"vev{i}", [128, 512], BF16) for i in range(2)]
            pp = [p.psum(f"{layer}A_pp{i}", [128, 512], ph) for i in range(2)]
            rot = [p.psum(f"{layer}A_rot{i}", [128, 512], ph) for i in range(2)]

            for dk in range(NDT):
                wt = wtmp[dk % 2]
                if layer == "a":
                    p.op("qs", lambda e, o=wt.ap, i=g["qkvwT"].ap, dk=dk: e.dma_start(
                        out=o[:, :], in_=i[dk * 128:(dk + 1) * 128, :]),
                        reads=[g["qkvwT"]], writes=[wt])
                    p.op("dve", lambda e, o=wq_sb[dk].ap, i=wt.ap: e.tensor_copy(o[:, :], i[:, :]),
                         reads=[wt], writes=[wq_sb[dk]])
                else:
                    p.op("qs", lambda e, o=wq_sb[dk].ap, i=g["bWbf"].ap, dk=dk: e.dma_start(
                        out=o.rearrange("p (w d) -> p w d", w=3),
                        in_=i[:, dk * 128:(dk + 1) * 128, :].transpose([1, 0, 2])),
                        reads=[g["bWbf"]], writes=[wq_sb[dk]])

            lp = 0
            for tc in range(K_TC or NQC):
                for tt4 in range(4):
                    ti = tc * 4 + tt4
                    xb, lb = xsb[lp], lnsb[lp]
                    lp ^= 1
                    p.op("qs", lambda e, o=xb.ap, i=x_src.ap, ti=ti: e.dma_start(
                        out=o[:, :], in_=i[ti * 128:(ti + 1) * 128, :]),
                        reads=[x_src], writes=[xb])
                    mu, rstd = stats_ln[ti]
                    emit_ln_stats(p, xb, mu, rstd, scr, small[0], small[1], small[2], g["epsb"])
                    p.op("dve", lambda e, o=lb.ap, i=xb.ap, m=mu.ap, r=rstd.ap: e.tensor_scalar(
                        o[:, :], i[:, :], m[:, 0:1], r[:, 0:1], ALU.subtract, ALU.mult),
                        reads=[xb, mu, rstd], writes=[lb])
                    p.op("qs", lambda e, o=lnTd.ap, i=lb.ap, ti=ti: e.dma_start(
                        out=o[ti, :, :], in_=i[:, :]),
                        reads=[lb], writes=[lnTd])
                lc = lnTc[tc % 2]
                src4 = lnTd.ap.rearrange("t c (dk p) -> t c dk p", p=128)
                ov = lc.ap.rearrange("p (dk r c) -> p dk r c", dk=NDT, r=4)
                for r4 in range(4):
                    p.op("qs", lambda e, o=ov, i=src4, tc=tc, r4=r4: e.dma_start(
                        out=o[:, :, r4, :], in_=i[tc * 4 + r4], transpose=True),
                        reads=[lnTd], writes=[lc])
                if rope:
                    rc, rs = ropeC[tc % 2], ropeS[tc % 2]
                    p.op("qg", lambda e, o=rc.ap, i=g["Cd"].ap, tc=tc: e.dma_start(
                        out=o[:, :], in_=i[:, tc * QC:(tc + 1) * QC]), reads=[g["Cd"]], writes=[rc])
                    p.op("qg", lambda e, o=rs.ap, i=g["Sd"].ap, tc=tc: e.dma_start(
                        out=o[:, :], in_=i[:, tc * QC:(tc + 1) * QC]), reads=[g["Sd"]], writes=[rs])

                for which in ("q", "k"):
                    if which == "q" and tc >= nqc:
                        continue
                    for dt in range(NDT):
                        ps = pp[dt % 2]
                        col = (0 if which == "q" else D) + dt * 128
                        for dk in range(NDT):
                            p.op("pe", lambda e, o=ps.ap, l=wq_sb[dk].ap, r=lc.ap, col=col, dk=dk: e.matmul(
                                o[:, :], l[:, col:col + 128], r[:, dk * 512:(dk + 1) * 512],
                                start=(dk == 0), stop=(dk == NDT - 1)),
                                reads=[wq_sb[dk], lc], writes=[ps], milestone=(dk == NDT - 1))
                        if rope:
                            qbv = qb_e[dt % 2]
                            bias_col = (0 if which == "q" else 8) + dt
                            p.op("dve", lambda e, o=qbv.ap, i=ps.ap, b=g["qkvb_sb"].ap, c=bias_col: e.tensor_scalar(
                                o[:, :], i[:, :], b[:, c:c + 1], None, ALU.add),
                                reads=[ps, g["qkvb_sb"]], writes=[qbv])
                            ro = rot[dt % 2]
                            p.op("pe", lambda e, o=ro.ap, l=g["perm"].ap, r=qbv.ap: e.matmul(
                                o[:, :], l[:, :], r[:, :], start=True, stop=True),
                                reads=[g["perm"], qbv], writes=[ro])
                            tv1, tv2 = t1_e[dt % 2], t2_e[dt % 2]
                            rc, rs = ropeC[tc % 2], ropeS[tc % 2]
                            p.op("dve", lambda e, o=tv1.ap, i0=qbv.ap, i1=rc.ap: e.tensor_tensor(
                                o[:, :], i0[:, :], i1[:, :], ALU.mult),
                                reads=[qbv, rc], writes=[tv1])
                            p.op("dve", lambda e, o=tv2.ap, i0=ro.ap, i1=rs.ap: e.tensor_tensor(
                                o[:, :], i0[:, :], i1[:, :], ALU.mult),
                                reads=[ro, rs], writes=[tv2])
                            dst = qro[dt % 2] if which == "q" else KT_t[dt][tc]
                            p.op("dve", lambda e, o=dst.ap, i0=tv1.ap, i1=tv2.ap: e.tensor_tensor(
                                o[:, :], i0[:, :], i1[:, :], ALU.add),
                                reads=[tv1, tv2], writes=[dst])
                        else:
                            dst = qro[dt % 2] if which == "q" else KT_t[dt][tc]
                            p.op("act", lambda e, o=dst.ap, i=ps.ap: e.activation(
                                out=o[:, :], in_=i[:, :], func=AF.Copy),
                                reads=[ps], writes=[dst])
                        if which == "q":
                            dst = qro[dt % 2]
                            p.op("qs", lambda e, o=QTd.ap, i=dst.ap, dt=dt, tc=tc: e.dma_start(
                                out=o[dt, :, tc * QC:(tc + 1) * QC], in_=i[:, :]),
                                reads=[dst], writes=[QTd])

                for tt4 in range(4):
                    ti = tc * 4 + tt4
                    for vc in range(2):
                        ps = pp[vc]
                        for dk in range(NDT):
                            p.op("pe", lambda e, o=ps.ap, l=lc.ap, r=wq_sb[dk].ap, tt4=tt4, vc=vc, dk=dk: e.matmul(
                                o[:, :], l[:, dk * 512 + tt4 * 128: dk * 512 + (tt4 + 1) * 128],
                                r[:, 2 * D + vc * 512: 2 * D + (vc + 1) * 512],
                                start=(dk == 0), stop=(dk == NDT - 1)),
                                reads=[lc, wq_sb[dk]], writes=[ps], milestone=(dk == NDT - 1))
                        ve = vev[vc]
                        p.op("act", lambda e, o=ve.ap, i=ps.ap: e.activation(
                            out=o[:, :], in_=i[:, :], func=AF.Copy),
                            reads=[ps], writes=[ve])
                        p.op("qs", lambda e, o=Vd.ap, i=ve.ap, ti=ti, vc=vc: e.dma_start(
                            out=o[ti, :, vc * 512:(vc + 1) * 512], in_=i[:, :]),
                            reads=[ve], writes=[Vd])
            p.barrier()

        with ExitStack() as ph:
            def S2(name, shape, dtype):
                return p.sbuf(f"{layer}B_{name}", shape, dtype, ph)

            V_sb = S2("V", [128, NT * D], BF16)
            V_t = [p.sub(V_sb, slice(None), slice(kt * D, (kt + 1) * D)) for kt in range(NT)]
            QT_c = [S2(f"QTc{i}", [128, NDT * QC], BF16) for i in range(2)]
            PT4 = S2("PT", [128, 4096], BF16)
            PTt = [p.sub(PT4, slice(None), slice(i * 1024, (i + 1) * 1024)) for i in range(4)]
            rdrep = S2("rdrep", [128, 512], F32)
            attn_sb = S2("attn", [128, NPAIR * QC], BF16)
            attn_t = [p.sub(attn_sb, slice(None), slice(pr * QC, (pr + 1) * QC)) for pr in range(NPAIR)]
            pw_sb = [S2(f"pw{i}", [128, D], BF16) for i in range(NDT)]
            pwt = [S2("pwt0", [128, D], F32)] if layer == "a" else None
            xprev = [S2(f"xp{i}", [128, D], F32) for i in range(2)]
            xnew = [S2(f"xo{i}", [128, D], F32) for i in range(2)]
            scrB = S2("scrB", [128, D], F32)
            ln3t = scrB if layer == "a" else S2("ln3t", [128, D], F32)
            smallB = [S2(f"smB{i}", [128, 1], F32) for i in range(3)]
            SS2 = p.psum(f"{layer}B_SS", [128, 2048], ph)
            SSh = [[p.sub(SS2, slice(None), slice(j * 1024 + hh * 512, j * 1024 + (hh + 1) * 512))
                    for hh in range(2)] for j in range(2)]
            PVh = [p.psum(f"{layer}B_PV{hh}", [128, 512], ph) for hh in range(2)]
            DDh = [p.psum(f"{layer}B_DD{hh}", [128, 512], ph) for hh in range(2)]
            PR = SSh[0]

            for kt in range(NT):
                p.op("qs" if kt % 2 == 0 else "qg",
                     lambda e, o=V_t[kt].ap, i=Vd.ap, kt=kt: e.dma_start(out=o[:, :], in_=i[kt, :, :]),
                     reads=[Vd], writes=[V_t[kt]])
            for dt in range(NDT):
                if layer == "a":
                    wt = pwt[0]
                    p.op("qg", lambda e, o=wt.ap, i=g["projwT"].ap, dt=dt: e.dma_start(
                        out=o[:, :], in_=i[dt * 128:(dt + 1) * 128, :]),
                        reads=[g["projwT"]], writes=[wt])
                    p.op("dve", lambda e, o=pw_sb[dt].ap, i=wt.ap: e.tensor_copy(o[:, :], i[:, :]),
                         reads=[wt], writes=[pw_sb[dt]])
                else:
                    p.op("qg", lambda e, o=pw_sb[dt].ap, i=g["bWpbf"].ap, dt=dt: e.dma_start(
                        out=o[:, :], in_=i[dt * 128:(dt + 1) * 128, :]),
                        reads=[g["bWpbf"]], writes=[pw_sb[dt]])

            for qc in range(K_QC or nqc):
                qtc = QT_c[qc % 2]
                p.op("qs", lambda e, o=qtc.ap, i=QTd.ap, qc=qc: e.dma_start(
                    out=o.rearrange("p (dt c) -> p dt c", dt=NDT),
                    in_=i[:, :, qc * QC:(qc + 1) * QC].transpose([1, 0, 2])),
                    reads=[QTd], writes=[qtc])
                for pr0 in range(K_PAIRS or NPAIR):
                    pr = pr0 + K_PRBASE
                    nkt = K_KT or NT
                    skips0 = set(K_SKIP.split(",")) if (K_SKIP and (pr >= 1 or K_PRBASE)) else set()
                    if "all" in skips0:
                        continue
                    def emit_S2(kt):
                        if "s" in skips0:
                            return
                        n2 = 2 if kt + 1 < nkt else 1
                        for j in range(n2):
                            for hh in range(2):
                                o0 = 64 * hh
                                sshb = SSh[(kt + j) % 2][hh]
                                p.op("pe", lambda e, o=sshb.ap, l=KT_sb.ap, r=qtc.ap, pr=pr, kt=kt + j, o0=o0: e.matmul(
                                    o[:, :],
                                    l[o0:o0 + 64, pr * T + kt * 128: pr * T + (kt + 1) * 128],
                                    r[o0:o0 + 64, pr * QC:(pr + 1) * QC],
                                    start=True, stop=True),
                                    reads=[KT_sb, qtc], writes=[sshb], milestone=(hh == 1), geom="r")

                    def emit_exp2(kt):
                        if "s" in skips0 or "exp" in skips0:
                            return
                        n2 = 2 if kt + 1 < nkt else 1
                        base = (kt % 4) * 1024
                        p.op("act", lambda e, o=PT4.ap, i=SS2.ap, n2=n2, base=base: e.activation(
                            out=o[:, base:base + n2 * 1024], in_=i[:, 0:n2 * 1024],
                            func=AF.Exp, scale=ATT_SCALE),
                            reads=[SSh[0][0], SSh[0][1]] + ([SSh[1][0], SSh[1][1]] if n2 == 2 else []),
                            writes=[PTt[kt % 4]] + ([PTt[(kt + 1) % 4]] if n2 == 2 else []))

                    def emit_PVDD(kt):
                        if "pvdd" in skips0:
                            return
                        pt = PTt[kt % 4]
                        for hh in range(2):
                            tp = (0, 64 * hh)
                            st_ = (kt == 0)
                            sp_ = (kt == nkt - 1)
                            vpr = 0 if "vclamp" in skips0 else pr
                            if "pv" not in skips0:
                                p.op("pe", lambda e, o=PVh[hh].ap, l=V_t[kt].ap, r=pt.ap, pr=vpr, kt=kt, hh=hh, tp=tp, st_=st_, sp_=sp_: e.matmul(
                                    o[64 * hh:64 * (hh + 1), :],
                                    l[:, (2 * pr + hh) * 64:(2 * pr + hh + 1) * 64],
                                    r[:, hh * 512:(hh + 1) * 512],
                                    start=st_, stop=sp_, tile_position=tp),
                                    reads=[V_t[kt], pt], writes=[PVh[hh]], milestone=sp_, geom="c")
                            if "dd" not in skips0:
                                p.op("pe", lambda e, o=DDh[hh].ap, l=g["ones64"].ap, r=pt.ap, kt=kt, hh=hh, tp=tp, st_=st_, sp_=sp_: e.matmul(
                                    o[64 * hh:64 * (hh + 1), :], l[:, :], r[:, hh * 512:(hh + 1) * 512],
                                    start=st_, stop=sp_, tile_position=tp),
                                    reads=[g["ones64"], pt], writes=[DDh[hh]], milestone=True, geom="c")

                    emit_S2(0)
                    emit_exp2(0)
                    for kt in range(0, nkt, 2):
                        if kt + 2 < nkt:
                            emit_S2(kt + 2)
                            emit_exp2(kt + 2)
                        emit_PVDD(kt)
                        if kt + 1 < nkt:
                            emit_PVDD(kt + 1)
                    at = attn_t[pr]
                    skips = set(K_SKIP.split(",")) if (K_SKIP and (pr >= 1 or K_PRBASE)) else set()
                    for hh in range(2):
                        if "lnd" in skips or "rd" in skips:
                            break
                        rows = slice(64 * hh, 64 * (hh + 1))
                        p.op("dve", lambda e, o=rdrep.ap, i=DDh[hh].ap, r=rows: e.reciprocal(
                            o[r, :], i[r, :]),
                            reads=[DDh[hh]], writes=[rdrep])
                    for hh in range(2):
                        if "at" in skips:
                            break
                        rows = slice(64 * hh, 64 * (hh + 1))
                        p.op("dve", lambda e, o=at.ap, i0=PVh[hh].ap, i1=rdrep.ap, r=rows: e.tensor_tensor(
                            o[r, :], i0[r, :], i1[r, :], ALU.mult),
                            reads=[PVh[hh], rdrep], writes=[at])
                    if layer == "a":
                        p.op("dve", lambda e, o=at.ap, i=at.ap, b=g["vb_sb"].ap, pr=pr: e.tensor_scalar(
                            o[:, :], i[:, :], b[:, pr:pr + 1], None, ALU.add),
                            reads=[at, g["vb_sb"]], writes=[at])
                for tt4 in range(4):
                    ti = qc * 4 + tt4
                    xp, xo = xprev[tt4 % 2], xnew[tt4 % 2]
                    p.op("qg", lambda e, o=xp.ap, i=x_src.ap, ti=ti: e.dma_start(
                        out=o[:, :], in_=i[ti * 128:(ti + 1) * 128, :]),
                        reads=[x_src], writes=[xp])
                    for ch in range(2):
                        ps = PR[ch]
                        for pr in range(NPAIR):
                            p.op("pe", lambda e, o=ps.ap, l=attn_t[pr].ap, r=pw_sb[pr].ap, tt4=tt4, ch=ch, pr=pr: e.matmul(
                                o[:, :], l[:, tt4 * 128:(tt4 + 1) * 128], r[:, ch * 512:(ch + 1) * 512],
                                start=(pr == 0), stop=(pr == NPAIR - 1)),
                                reads=[attn_t[pr], pw_sb[pr]], writes=[ps], milestone=(pr == NPAIR - 1))
                    for ch in range(2):
                        p.op("dve", lambda e, o=xo.ap, i0=PR[ch].ap, i1=xp.ap, ch=ch: e.tensor_tensor(
                            o[:, ch * 512:(ch + 1) * 512], i0[:, :], i1[:, ch * 512:(ch + 1) * 512], ALU.add),
                            reads=[PR[ch], xp], writes=[xo])
                    if layer == "a":
                        p.op("dve", lambda e, o=xo.ap, i0=xo.ap, i1=g["biasrep"].ap: e.tensor_tensor(
                            o[:, :], i0[:, :], i1[:, :], ALU.add),
                            reads=[xo, g["biasrep"]], writes=[xo])
                    else:
                        mu, rstd = stats_ln[ti]
                        p.op("dve", lambda e, o=ln3t.ap, i=xp.ap, m=mu.ap, r=rstd.ap: e.tensor_scalar(
                            o[:, :], i[:, :], m[:, 0:1], r[:, 0:1], ALU.subtract, ALU.mult),
                            reads=[xp, mu, rstd], writes=[ln3t])
                        p.op("dve", lambda e, o=xo.ap, i0=xo.ap, i1=ln3t.ap: e.tensor_tensor(
                            o[:, :], i0[:, :], i1[:, :], ALU.add),
                            reads=[xo, ln3t], writes=[xo])
                    mu2, rstd2 = stats_next[ti]
                    emit_ln_stats(p, xo, mu2, rstd2, scrB, smallB[0], smallB[1], smallB[2], g["epsb"])
                    p.op("qg", lambda e, o=x_out.ap, i=xo.ap, ti=ti: e.dma_start(
                        out=o[ti * 128:(ti + 1) * 128, :], in_=i[:, :]),
                        reads=[xo], writes=[x_out])
            p.barrier()


def build_ffn(p, g, x_src, x_out, stats_ln, xqTd, hqTd, nslab, final_out=None):
    nc = p.nc
    with ExitStack() as ph:
        def S(name, shape, dtype):
            return p.sbuf(f"f{nslab}_{name}", shape, dtype, ph)

        wgT = [S(f"wgT{dk}", [128, DFF], BF16) for dk in range(NDT)]
        wuT = [S(f"wuT{dk}", [128, DFF], BF16) for dk in range(NDT)]
        wdT = [S(f"wdT{k}", [128, D], BF16) for k in range(NIT)]
        for dk in range(NDT):
            p.op("qs", lambda e, o=wgT[dk].ap, i=g["wgTd"].ap, dk=dk: e.dma_start(
                out=o.rearrange("p (r c) -> p r c", r=NIT), in_=i[:, dk, :, :], transpose=True),
                reads=[g["wgTd"]], writes=[wgT[dk]])
            p.op("qs", lambda e, o=wuT[dk].ap, i=g["wuTd"].ap, dk=dk: e.dma_start(
                out=o.rearrange("p (r c) -> p r c", r=NIT), in_=i[:, dk, :, :], transpose=True),
                reads=[g["wuTd"]], writes=[wuT[dk]])
        for k in range(NIT):
            p.op("qs", lambda e, o=wdT[k].ap, i=g["wdTd"].ap, k=k: e.dma_start(
                out=o.rearrange("p (r c) -> p r c", r=NDT), in_=i[:, k, :, :], transpose=True),
                reads=[g["wdTd"]], writes=[wdT[k]])

        xsl = [S(f"x{i}", [128, D], F32) for i in range(2)]
        xq0 = S("xq0", [128, D], BF16)
        xq = [xq0, xq0]
        xqT = [S(f"xqT{i}", [128, NDT * 128], BF16) for i in range(2)]
        lnf0 = S("lnf0", [128, D], F32)
        hsl = [S(f"h{i}", [128, DFF], F32) for i in range(2)]
        hq0 = S("hq0", [128, DFF], BF16)
        hq = [hq0, hq0]
        hqT0 = S("hqT0", [128, NIT * 128], BF16)
        hqT = [hqT0, hqT0]
        sg0 = S("sg0", [128, 512], F32)
        sg = [sg0, sg0]
        xo0 = S("xo0", [128, D], F32)
        xo = [xo0, xo0]
        lnf = [lnf0, lnf0]
        qbx = {"sets": [mk_quant_bufs(p, f"fqa{nslab}", ph, qf=512, with_t0=False)]}
        gu = [p.psum(f"f{nslab}_gu{i}", [128, 512], ph) for i in range(4)]
        dn = [p.psum(f"f{nslab}_dn{i}", [128, 512], ph) for i in range(2)]

        CH = [(i * 512, min(512, DFF - i * 512)) for i in range((DFF + 511) // 512)]
        nsl = K_FSLAB or nslab

        def part1a(s):
            par = s % 2
            xb, lb, xqb, xqTb = xsl[par], lnf[par], xq[par], xqT[par]
            p.op("qg", lambda e, o=xb.ap, i=x_src.ap, s=s: e.dma_start(
                out=o[:, :], in_=i[s * 128:(s + 1) * 128, :]),
                reads=[x_src], writes=[xb])
            mu, rstd = stats_ln[s]
            p.op("dve", lambda e, o=lb.ap, i=xb.ap, m=mu.ap, r=rstd.ap: e.tensor_scalar(
                o[:, :], i[:, :], m[:, 0:1], r[:, 0:1], ALU.subtract, ALU.mult),
                reads=[xb, mu, rstd], writes=[lb])
            emit_quant(p, lb, xqb, qbx, D)
            p.op("qs", lambda e, o=xqTd.ap, i=xqb.ap, s=s: e.dma_start(
                out=o[s, :, :], in_=i[:, :]),
                reads=[xqb], writes=[xqTd])
            p.op("qs", lambda e, o=xqTb.ap, i=xqTd.ap, s=s: e.dma_start(
                out=o.rearrange("p (dk c) -> p dk c", dk=NDT),
                in_=i[s].rearrange("c (dk p) -> c dk p", p=128), transpose=True),
                reads=[xqTd], writes=[xqTb])

        def part1b(s):
            par = s % 2
            xqTb, hb = xqT[par], hsl[par]
            for ci, (c0, cw) in enumerate(CH):
                gp, up = gu[2 * (ci % 2)], gu[2 * (ci % 2) + 1]
                for dk in range(NDT):
                    p.op("pe", lambda e, o=gp.ap, l=xqTb.ap, r=wgT[dk].ap, dk=dk, c0=c0, cw=cw: e.matmul(
                        o[:, 0:cw], l[:, dk * 128:(dk + 1) * 128], r[:, c0:c0 + cw],
                        start=(dk == 0), stop=(dk == NDT - 1)),
                        reads=[xqTb, wgT[dk]], writes=[gp], milestone=(dk == NDT - 1))
                for dk in range(NDT):
                    p.op("pe", lambda e, o=up.ap, l=xqTb.ap, r=wuT[dk].ap, dk=dk, c0=c0, cw=cw: e.matmul(
                        o[:, 0:cw], l[:, dk * 128:(dk + 1) * 128], r[:, c0:c0 + cw],
                        start=(dk == 0), stop=(dk == NDT - 1)),
                        reads=[xqTb, wuT[dk]], writes=[up], milestone=(dk == NDT - 1))
                sgb = sg[ci % 2]
                p.op("act", lambda e, o=sgb.ap, i=gp.ap, cw=cw: e.activation(
                    out=o[:, 0:cw], in_=i[:, 0:cw], func=AF.Silu),
                    reads=[gp], writes=[sgb])
                p.op("dve", lambda e, o=hb.ap, i0=sgb.ap, i1=up.ap, c0=c0, cw=cw: e.tensor_tensor(
                    o[:, c0:c0 + cw], i0[:, 0:cw], i1[:, 0:cw], ALU.mult),
                    reads=[sgb, up], writes=[hb])

        def part2a(s):
            par = s % 2
            hb, hqb, hqTb = hsl[par], hq[par], hqT[par]
            emit_quant(p, hb, hqb, qbx, DFF)
            p.op("qs", lambda e, o=hqTd.ap, i=hqb.ap, s=s: e.dma_start(
                out=o[s, :, :], in_=i[:, :]),
                reads=[hqb], writes=[hqTd])
            p.op("qs", lambda e, o=hqTb.ap, i=hqTd.ap, s=s: e.dma_start(
                out=o.rearrange("p (k c) -> p k c", k=NIT),
                in_=i[s].rearrange("c (k p) -> c k p", p=128), transpose=True),
                reads=[hqTd], writes=[hqTb])

        def part2b(s):
            par = s % 2
            xb, hqTb, xob = xsl[par], hqT[par], xo[par]
            for ch in range(2):
                dp = dn[ch]
                for k in range(NIT):
                    p.op("pe", lambda e, o=dp.ap, l=hqTb.ap, r=wdT[k].ap, k=k, ch=ch: e.matmul(
                        o[:, :], l[:, k * 128:(k + 1) * 128], r[:, ch * 512:(ch + 1) * 512],
                        start=(k == 0), stop=(k == NIT - 1)),
                        reads=[hqTb, wdT[k]], writes=[dp], milestone=(k == NIT - 1))
                p.op("dve", lambda e, o=xob.ap, i0=dn[ch].ap, i1=xb.ap, ch=ch: e.tensor_tensor(
                    o[:, ch * 512:(ch + 1) * 512], i0[:, :], i1[:, ch * 512:(ch + 1) * 512], ALU.add),
                    reads=[dn[ch], xb], writes=[xob])
            tgt = final_out if final_out is not None else x_out
            p.op("qg", lambda e, o=tgt.ap, i=xob.ap, s=s: e.dma_start(
                out=o[s * 128:(s + 1) * 128, :], in_=i[:, :]),
                reads=[xob], writes=[tgt])

        part1a(0)
        part1b(0)
        for s in range(nsl):
            if s + 1 < nsl:
                part1a(s + 1)
            part2a(s)
            if s + 1 < nsl:
                part1b(s + 1)
            part2b(s)
        p.barrier()


_PROG_CACHE = {}


def _get_prog(stage="full"):
    if stage not in _PROG_CACHE:
        _PROG_CACHE[stage] = build_program(stage)
    return _PROG_CACHE[stage]


def host_inputs(inputs):
    x = np.asarray(inputs["L0_attn_a_x"])
    qkv_w = np.asarray(inputs["L0_attn_a_qkv_w"])
    qkv_b = np.asarray(inputs["L0_attn_a_qkv_b"])
    proj_w = np.asarray(inputs["L0_attn_a_proj_w"])
    proj_b = np.asarray(inputs["L0_attn_a_proj_b"])
    Wq = np.asarray(inputs["L0_attn_b_Wq"])
    Wk = np.asarray(inputs["L0_attn_b_Wk"])
    Wv = np.asarray(inputs["L0_attn_b_Wv"])
    Wp = np.asarray(inputs["L0_attn_b_Wp"])
    wg = np.asarray(inputs["L0_ffn_w_gate"])
    wu = np.asarray(inputs["L0_ffn_w_up"])
    wd = np.asarray(inputs["L0_ffn_w_down"])

    common = {
        "qkvwT": np.ascontiguousarray(qkv_w.T),
        "qkvb": np.ascontiguousarray(qkv_b[:2048].reshape(16, 128).T),
        "vb": np.ascontiguousarray(qkv_b[2048:].reshape(8, 128).T),
        "projwT": np.ascontiguousarray(proj_w.T),
        "projb_row": np.ascontiguousarray(proj_b.reshape(1, D)),
        "bW": np.ascontiguousarray(np.stack([Wq, Wk, Wv], 0)),
        "bWp": np.ascontiguousarray(Wp),
        "wg": np.ascontiguousarray(wg), "wu": np.ascontiguousarray(wu),
        "wd": np.ascontiguousarray(wd),
        "ones_in": np.ones((128, 64), np.float32),
    }
    permv = np.zeros((128, 128), np.float32)
    for j in range(64):
        permv[2 * j, 2 * j + 1] = 1.0
        permv[2 * j + 1, 2 * j] = 1.0
    common["perm_in"] = permv
    invv = (1.0 / 10000.0 ** (np.arange(0, 64, 2, dtype=np.float32) / 64)).astype(np.float32)
    common["inv_in"] = np.repeat(invv, 2).reshape(64, 1)
    common["sgnpat"] = np.where(np.arange(64) % 2 == 0, -1.0, 1.0).astype(np.float32).reshape(64, 1)

    in_maps = []
    for c in range(8):
        b, h = c // 2, c % 2
        xb = x[b]
        posv = np.arange(T, dtype=np.float32)
        if h == 1:
            xb = np.roll(xb, -TO, axis=0)
            posv = np.roll(posv, -TO)
        m = dict(common)
        m["x0"] = np.ascontiguousarray(xb)
        m["pos_rep"] = np.ascontiguousarray(np.tile(posv.reshape(1, T), (64, 1)))
        in_maps.append(m)
    return in_maps


def kernel(**inputs):
    nc, _ = _get_prog("full")
    in_maps = host_inputs(inputs)
    res = run_bass_kernel_spmd(nc, in_maps, core_ids=list(range(8)))
    outp = np.zeros((4, T, D), np.float32)
    for c in range(8):
        b, h = c // 2, c % 2
        outp[b, h * TO:(h + 1) * TO, :] = np.asarray(res.results[c]["out"])
    return outp
